# version 02
# baseline (speedup 1.0000x reference)
import contextlib
import numpy as np
import concourse.bass as bass
import concourse.mybir as mybir
from concourse.bass_utils import run_bass_kernel_spmd

F32 = mybir.dt.float32
BF16 = mybir.dt.bfloat16
I32 = mybir.dt.int32
OP = mybir.AluOpType
ACT = mybir.ActivationFunctionType
AX = mybir.AxisListType

D = 1024
HID = 4096
NH = 16
KVR = 256
QR = 512
EPS_LN = 1e-5
EPS_RMS = 1e-6
EPS_CLIP = 1e-5
NEG_BIG = -1.0e30
NCORES = 8
ENGS = ("sync", "gpsimd", "scalar", "vector", "tensor")


class SemC:
    __slots__ = ("h", "v")

    def __init__(self, h):
        self.h = h
        self.v = 0

    def inc(self, n=1):
        self.v += n
        return (self.h, self.v)

    def tok(self):
        return (self.h, self.v)


class DmaPool:

    def __init__(self, nc, es, name, n):
        self.sems = [SemC(es.enter_context(nc.semaphore(f"{name}{i}"))) for i in range(n)]
        self.rr = 0

    def issue(self, em, queue, dma_fn, pre_toks=()):
        s = self.sems[self.rr]
        self.rr = (self.rr + 1) % len(self.sems)
        prev = s.tok()

        def f(E, s=s, prev=prev, pre=tuple(pre_toks), dma_fn=dma_fn):
            if prev[1] > 0:
                E.wait_ge(prev[0], prev[1])
            for t in pre:
                if t is not None:
                    E.wait_ge(t[0], t[1])
            dma_fn(E).then_inc(s.h, 16)
        em(queue, f)
        return s.inc(16)

    def all_tok(self):
        return [s.tok() for s in self.sems if s.v > 0]


class Emit:
    def __init__(self, block):
        self.block = block
        self.q = {e: [] for e in ENGS}

    def __call__(self, eng, fn):
        self.q[eng].append(fn)

    def flush(self):
        for name in ENGS:
            fns = self.q[name]
            if not fns:
                continue

            def body(E, fns=fns):
                for f in fns:
                    f(E)

            getattr(self.block, name)(body)
        self.q = {e: [] for e in ENGS}


def W(E, *toks):
    for t in toks:
        if t is not None:
            E.wait_ge(t[0], t[1])


def build_program(TS=2048, NSEQ=2, do_attn=True, do_ffn2=True, debug=False, attn_upto=99, attn_heads=NH):
    assert TS % 512 == 0
    NT = TS // 128
    NSLAB = TS // 512
    NCH = TS // 512
    nc = bass.Bass("TRN2", target_bir_lowering=False)

    dx = nc.dram_tensor("x", [NSEQ, TS, D], F32, kind="ExternalInput").ap()
    dwkv = nc.dram_tensor("wkv", [D, KVR], F32, kind="ExternalInput").ap()
    dgkv = nc.dram_tensor("gkv", [128, 2], F32, kind="ExternalInput").ap()
    dwkdec = nc.dram_tensor("wkdec", [KVR, D], F32, kind="ExternalInput").ap()
    dwvdec = nc.dram_tensor("wvdec", [KVR, D], F32, kind="ExternalInput").ap()
    dwq = nc.dram_tensor("wq", [D, QR], F32, kind="ExternalInput").ap()
    dwqdec = nc.dram_tensor("wqdec", [QR, D], F32, kind="ExternalInput").ap()
    dwkrope = nc.dram_tensor("wkrope", [D, 512], F32, kind="ExternalInput").ap()
    dwqrope = nc.dram_tensor("wqrope", [QR, 512], F32, kind="ExternalInput").ap()
    dwo = nc.dram_tensor("wo", [D, D], F32, kind="ExternalInput").ap()
    dw1t = nc.dram_tensor("w1t", [D, HID], F32, kind="ExternalInput").ap()
    dw2t = nc.dram_tensor("w2t", [HID, D], F32, kind="ExternalInput").ap()
    dconvw = nc.dram_tensor("convw", [128, 96], F32, kind="ExternalInput").ap()
    dconvb = nc.dram_tensor("convb", [128, 32], F32, kind="ExternalInput").ap()
    didb = nc.dram_tensor("identb", [128, 128], BF16, kind="ExternalInput").ap()
    didf = nc.dram_tensor("identf", [128, 128], F32, kind="ExternalInput").ap()
    donesf = nc.dram_tensor("onesf", [128, 128], F32, kind="ExternalInput").ap()
    dseed = nc.dram_tensor("seed", [128, 16], F32, kind="ExternalInput").ap()
    dcos = nc.dram_tensor("cosfm", [128, TS], F32, kind="ExternalInput").ap()
    dssin = nc.dram_tensor("ssinfm", [128, TS], F32, kind="ExternalInput").ap()
    dmask = nc.dram_tensor("cmask", [128, 128], F32, kind="ExternalInput").ap()
    dx1 = nc.dram_tensor("x1s", [NSEQ, TS, D], F32, kind="Internal").ap()
    dx2 = nc.dram_tensor("x2s", [NSEQ, TS, D], F32, kind="Internal").ap()
    dw1q = nc.dram_tensor("w1qs", [4, 128, 8192], BF16, kind="Internal").ap()
    dw2q = nc.dram_tensor("w2qs", [HID, D], BF16, kind="Internal").ap()
    dout = nc.dram_tensor("out", [NSEQ, TS, D], F32, kind="ExternalOutput").ap()
    ddbg = {}
    if debug:
        for nm, shp, dt in [("kons", [128, 1], F32), ("alf", [128, 64], F32),
                            ("bet", [128, 64], F32), ("bia", [128, 64], F32),
                            ("nT0", [128, 514], F32), ("apre", [128, 16384], F32),
                            ("apost", [128, 16384], F32), ("s2t", [128, 4], F32),
                            ("g2t", [128, 4], F32), ("bl2e0", [128, 512], F32),
                            ("amx", [128, 512], F32)]:
            ddbg[nm] = nc.dram_tensor("dbg_" + nm, shp, dt, kind="ExternalOutput").ap()
        ddbg["stw"] = nc.dram_tensor("dbg_stw", [128, 8 * (TS // 128)], F32, kind="ExternalOutput").ap()
        ddbg["bns"] = nc.dram_tensor("dbg_bns", [128, 12 * (TS // 128)], F32, kind="ExternalOutput").ap()
        ddbg["smax"] = nc.dram_tensor("dbg_smax", [128, 64], F32, kind="ExternalOutput").ap()
        ddbg["hp0"] = nc.dram_tensor("dbg_hp0", [128, 514], F32, kind="ExternalOutput").ap()
        ddbg["w1s0"] = nc.dram_tensor("dbg_w1s0", [128, 8192], F32, kind="ExternalOutput").ap()
        ddbg["cv0"] = nc.dram_tensor("dbg_cv0", [128, 512], F32, kind="ExternalOutput").ap()
        ddbg["smin"] = nc.dram_tensor("dbg_smin", [128, 64], F32, kind="ExternalOutput").ap()

    es = contextlib.ExitStack()
    TT = SemC(es.enter_context(nc.semaphore("sTT")))
    VS = SemC(es.enter_context(nc.semaphore("sVS")))
    SZ = SemC(es.enter_context(nc.semaphore("sSZ")))
    VV = SemC(es.enter_context(nc.semaphore("sVV")))
    SS = SemC(es.enter_context(nc.semaphore("sSS")))
    LP = DmaPool(nc, es, "sL", 8)
    OP_ = DmaPool(nc, es, "sO", 4)

    block = es.enter_context(nc.Block())
    em = Emit(block)

    def mkvs(n):
        base = VS.v
        VS.v += n

        def factory(E):
            cnt = [0]

            def S(ins):
                cnt[0] += 1
                assert cnt[0] <= n
                ins.then_inc(VS.h, 1)
                E.wait_ge(VS.h, base + cnt[0])
            return S
        return factory

    def mkss(n):
        base = SZ.v
        SZ.v += n

        def factory(E):
            cnt = [0]

            def S(ins):
                cnt[0] += 1
                assert cnt[0] <= n
                ins.then_inc(SZ.h, 1)
                E.wait_ge(SZ.h, base + cnt[0])
            return S
        return factory

    identb = es.enter_context(nc.sbuf_tensor("sb_identb", [128, 128], BF16))
    identf = es.enter_context(nc.sbuf_tensor("sb_identf", [128, 128], F32))
    onesf = es.enter_context(nc.sbuf_tensor("sb_onesf", [128, 128], F32))
    seed = es.enter_context(nc.sbuf_tensor("sb_seed", [128, 16], F32))
    kons = es.enter_context(nc.sbuf_tensor("kons", [128, 8], F32))
    mw1 = es.enter_context(nc.sbuf_tensor("mw1", [128, 1], F32))
    rw1 = es.enter_context(nc.sbuf_tensor("rw1", [128, 1], F32))
    mw2 = es.enter_context(nc.sbuf_tensor("mw2", [128, 1], F32))
    rw2 = es.enter_context(nc.sbuf_tensor("rw2", [128, 1], F32))
    mw2h = es.enter_context(nc.sbuf_tensor("mw2h", [128, 1], F32))
    convwt = es.enter_context(nc.sbuf_tensor("sb_convwt", [128, 96], F32))
    convbt = es.enter_context(nc.sbuf_tensor("sb_convbt", [128, 32], F32))
    smallp = es.enter_context(nc.sbuf_tensor("smallp", [128, 8], F32))


    def emit_p0():
        tok_c = []
        for dst_t, src_a in ((identb, didb), (identf, didf), (onesf, donesf),
                             (seed, dseed), (convwt, dconvw), (convbt, dconvb)):
            tok_c.append(LP.issue(em, "sync", lambda E, d=dst_t, s=src_a: E.dma_start(out=d[:], in_=s)))

        stp = mkvs(4)

        def f(E, toks=tuple(tok_c), stp=stp):
            W(E, *toks)
            S = stp(E)
            S(E.tensor_scalar(smi[:], onesf[:, 0:128], 1.5, None, OP.mult))
            S(E.tensor_scalar(smib[:], smi[:], 0.0, None, OP.add))
            S(E.tensor_scalar(smallp[:, 0:8], smib[:, 0:8], 0.0, None, OP.add))
            S(E.tensor_scalar(kons[:], smallp[:, 0:8], -0.5, 129.0, OP.mult, OP.add))
            E.engine_nop().then_inc(VV.h, 1)
        em("vector", f)
        tok_kons = VV.inc()

        ps = contextlib.ExitStack()
        smi = ps.enter_context(nc.sbuf_tensor("smi", [128, 128], I32))
        smib = ps.enter_context(nc.sbuf_tensor("smib", [128, 128], BF16))
        wst = [ps.enter_context(nc.sbuf_tensor(f"p0w{i}", [128, 4096], F32)) for i in range(2)]
        wqi = ps.enter_context(nc.sbuf_tensor("p0qi", [128, 4096], I32))
        wqf = ps.enter_context(nc.sbuf_tensor("p0qf", [128, 4096], BF16))
        wqb = ps.enter_context(nc.sbuf_tensor("p0qb", [128, 4096], BF16))
        macc = ps.enter_context(nc.sbuf_tensor("p0macc", [128, 1], F32))
        red = ps.enter_context(nc.sbuf_tensor("p0red", [128, 1], F32))
        pms = ps.enter_context(nc.psum_tensor("p0ps", [128, 512], F32))

        def quant_weight(dsrc, nchunk, chwid, nelem, mean_t, recip_t, spill_fn,
                         multi_spill=False, init_vt=(None, None), init_sp=()):
            src3 = dsrc.rearrange("(c p) w -> c p w", p=128)
            vt = [init_vt[0], init_vt[1]]
            for c in range(nchunk):
                b = wst[c % 2]
                tok_l = LP.issue(em, "sync",
                                 lambda E, b=b, c=c: E.dma_start(out=b[:, 0:chwid], in_=src3[c]),
                                 pre_toks=(vt[c % 2],))

                stp = mkvs(1 if c == 0 else 2)

                def f(E, c=c, b=b, tok=tok_l, stp=stp):
                    W(E, tok)
                    S = stp(E)
                    if c == 0:
                        S(E.reduce_sum(macc[:], b[:, 0:chwid], AX.X,
                                       apply_absolute_value=True))
                    else:
                        S(E.reduce_sum(red[:], b[:, 0:chwid], AX.X, apply_absolute_value=True))
                        S(E.tensor_tensor(macc[:], macc[:], red[:], OP.add))
                    E.engine_nop().then_inc(VV.h, 1)
                em("vector", f)
                vt[c % 2] = VV.inc()
            tok_acc = vt[(nchunk - 1) % 2]

            def f(E, tok=tok_acc):
                W(E, tok)
                E.matmul(pms[0:1, 0:1], onesf[:, 0:1], macc[:], start=True,
                         stop=True).then_inc(TT.h, 1)
            em("tensor", f)
            tok_mm = TT.inc()

            def f(E, tok=tok_mm):
                W(E, tok)
                E.tensor_scalar(smallp[0:1, 0:1], pms[0:1, 0:1], 0.0, None, OP.add).then_inc(VV.h, 1)
            em("vector", f)
            tok_ev = VV.inc()

            def f(E, tok=tok_ev):
                W(E, tok)
                E.matmul(pms[:, 0:1], onesf[0:1, :], smallp[0:1, 0:1], start=True,
                         stop=True).then_inc(TT.h, 1)
            em("tensor", f)
            tok_bc = TT.inc()

            stp = mkvs(2)

            def f(E, tok=tok_bc, mean_t=mean_t, recip_t=recip_t, nelem=nelem, stp=stp):
                W(E, tok)
                S = stp(E)
                S(E.tensor_scalar(mean_t[:], pms[:, 0:1], 1.0 / nelem, EPS_CLIP, OP.mult, OP.max))
                S(E.reciprocal(recip_t[:], mean_t[:]))
                E.engine_nop().then_inc(VV.h, 1)
            em("vector", f)
            tok_scale = VV.inc()

            sp_prev = list(init_sp)
            vt2 = [tok_acc, tok_acc]
            for c in range(nchunk):
                b = wst[c % 2]
                tok_l = LP.issue(em, "sync",
                                 lambda E, b=b, c=c: E.dma_start(out=b[:, 0:chwid], in_=src3[c]),
                                 pre_toks=(vt2[c % 2],))

                stpq = mkvs(3)

                def f(E, c=c, b=b, tok=tok_l, tok3=tuple(sp_prev), recip_t=recip_t, stpq=stpq):
                    W(E, tok, tok_scale, tok_kons, *tok3)
                    S = stpq(E)
                    S(E.tensor_scalar(wqi[:, 0:chwid], b[:, 0:chwid], recip_t[:, 0:1],
                                      kons[:, 0:1], OP.mult, OP.add))
                    S(E.tensor_scalar(wqf[:, 0:chwid], wqi[:, 0:chwid], -128.0, None, OP.add))
                    S(E.tensor_scalar(wqf[:, 0:chwid], wqf[:, 0:chwid], 1.0, None, OP.min))
                    E.tensor_scalar(wqb[:, 0:chwid], wqf[:, 0:chwid], -1.0, None,
                                    OP.max).then_inc(VV.h, 1)
                em("vector", f)
                vt2[c % 2] = VV.inc()
                sp_prev = []
                if multi_spill:
                    for g in range(4):
                        sp_prev.append(OP_.issue(
                            em, "gpsimd",
                            lambda E, c=c, g=g: E.dma_start(
                                out=dw1q[g, :, 1024 * c:1024 * (c + 1)],
                                in_=wqb[:, 1024 * g:1024 * (g + 1)]),
                            pre_toks=(vt2[c % 2],)))
                else:
                    sp_prev.append(OP_.issue(
                        em, "gpsimd",
                        lambda E, c=c: E.dma_start(out=spill_fn(c), in_=wqb[:, 0:chwid]),
                        pre_toks=(vt2[c % 2],)))
            return tok_scale, (tuple(vt2), tuple(sp_prev))

        tok_w1scale, (w1_vt, w1_sp) = quant_weight(
            dw1t, 8, 4096, D * HID, mw1, rw1, None, multi_spill=True)
        tok_w1sp = tuple(OP_.all_tok())
        tok_w2scale, _ = quant_weight(
            dw2t, 32, 1024, D * HID, mw2, rw2,
            lambda c: dw2q[c * 128:(c + 1) * 128, :],
            init_vt=w1_vt, init_sp=w1_sp)
        tok_w2sp = tuple(OP_.all_tok())

        stp = mkvs(2)

        def f(E, stp=stp, tokc=tuple(tok_c)):
            W(E, tok_w2scale, tok_w1scale, *tokc)
            S = stp(E)
            S(E.tensor_scalar(convwt[:], convwt[:], mw1[:, 0:1], None, OP.mult))
            S(E.tensor_scalar(mw2h[:], mw2[:], 0.5 / 127.0, None, OP.mult))
            E.engine_nop().then_inc(VV.h, 1)
        em("vector", f)
        tok_taps = VV.inc()
        em.flush()
        ps.close()
        return tok_taps, tok_w1sp, tok_w2sp, tok_kons

    tok_taps, tok_w1sp, tok_w2sp, tok_kons = emit_p0()

    def fence():
        toks = [TT.tok(), VV.tok(), SS.tok()] + LP.all_tok() + OP_.all_tok()
        for e in ENGS:
            def f(E, toks=tuple(toks)):
                for t in toks:
                    if t[1] > 0:
                        E.wait_ge(t[0], t[1])
            em(e, f)
        em.flush()

    def emit_ffn(src, dst, tag):
        ps = contextlib.ExitStack()
        w2q = [ps.enter_context(nc.sbuf_tensor(f"F{tag}w2q{c}", [128, D], BF16)) for c in range(32)]
        w1s = [ps.enter_context(nc.sbuf_tensor(f"F{tag}w1s{g}", [128, 8192], BF16)) for g in range(2)]
        nT = [[ps.enter_context(nc.sbuf_tensor(f"F{tag}nT{b}_{c}", [128, 514], BF16))
               for c in range(8)] for b in range(2)]
        aAr = ps.enter_context(nc.sbuf_tensor(f"F{tag}aA", [128, 32 * 512], BF16))
        xsl = [ps.enter_context(nc.sbuf_tensor(f"F{tag}xsl{i}", [128, 2048], F32)) for i in range(2)]
        xrr = [ps.enter_context(nc.sbuf_tensor(f"F{tag}xrr{i}", [128, 1024], F32)) for i in range(4)]
        nI = [ps.enter_context(nc.sbuf_tensor(f"F{tag}nI0", [128, 1024], I32))] * 2
        nB = [ps.enter_context(nc.sbuf_tensor(f"F{tag}nB{i}", [128, 1024], BF16)) for i in range(2)]
        csc = [ps.enter_context(nc.sbuf_tensor(f"F{tag}csc{i}", [128, 512], F32)) for i in range(4)]
        nis = ps.enter_context(nc.sbuf_tensor(f"F{tag}nis", [128, 512], I32))
        amx = ps.enter_context(nc.sbuf_tensor(f"F{tag}amx", [128, 512], F32))
        s2r = ps.enter_context(nc.sbuf_tensor(f"F{tag}s2r", [128, 128], F32))
        bl2e = [ps.enter_context(nc.sbuf_tensor(f"F{tag}bl2e{i}", [128, 512], F32)) for i in range(2)]
        bns = ps.enter_context(nc.sbuf_tensor(f"F{tag}bns", [128, 12 * NT], F32))
        smax = ps.enter_context(nc.sbuf_tensor(f"F{tag}smax", [128, NT], F32))
        smin = ps.enter_context(nc.sbuf_tensor(f"F{tag}smin", [128, NT], F32))
        stw = ps.enter_context(nc.sbuf_tensor(f"F{tag}stw", [128, 8 * NT], F32))
        alf = ps.enter_context(nc.sbuf_tensor(f"F{tag}alf", [128, NT], F32))
        bet = ps.enter_context(nc.sbuf_tensor(f"F{tag}bet", [128, NT], F32))
        bia = ps.enter_context(nc.sbuf_tensor(f"F{tag}bia", [128, NT], F32))
        s2t = ps.enter_context(nc.sbuf_tensor(f"F{tag}s2t", [128, 4], F32))
        g2t = ps.enter_context(nc.sbuf_tensor(f"F{tag}g2t", [128, 4], F32))

        hpsA = ps.enter_context(nc.psum_tensor(f"F{tag}hA", [128, 1024], F32))
        hpsB = ps.enter_context(nc.psum_tensor(f"F{tag}hB", [128, 1024], F32))
        ntp = ps.enter_context(nc.psum_tensor(f"F{tag}ntp", [128, 1024], BF16))
        bl2p = ps.enter_context(nc.psum_tensor(f"F{tag}bl2p", [128, 1024], F32))
        pmis = ps.enter_context(nc.psum_tensor(f"F{tag}pmis", [128, 512], F32))
        hps = [hpsA, hpsB]

        x3 = src.rearrange("s (k p) d -> s k p d", p=128)
        y3 = dst.rearrange("s (k p) d -> s k p d", p=128)

        tok_w2l = []
        for c in range(32):
            tok_w2l.append(LP.issue(
                em, "sync",
                lambda E, c=c: E.dma_start(out=w2q[c][:], in_=dw2q[c * 128:(c + 1) * 128, :]),
                pre_toks=tok_w2sp))
        toks_w2ld = tuple(tok_w2l)

        st = {
            "xsl_free": [None, None],
            "nIB_free": [None, None],
            "nT_bl1_done": [None, None],
            "halo_done": {},
            "xside_done": {},
            "conv": [],
            "silu": [],
            "quant_done": None,
            "bl2_done": None,
            "last_stt": None,
            "amx_T": None,
            "w1_used": [None, None],
            "xrr_tok": [None, None, None, None],
            "xout_done": [None, None, None, None],
        }

        def emit_stats(s):
            vt = [None, None]
            for k in range(NT):
                b = k % 2
                tok_l = LP.issue(em, "sync",
                                 lambda E, s=s, k=k, b=b: E.dma_start(out=xsl[b][:, 0:1024], in_=x3[s, k]),
                                 pre_toks=(vt[b], st["xsl_free"][b]))

                def f(E, k=k, b=b, tok=tok_l):
                    W(E, tok)
                    E.bn_stats(bns[:, 12 * k:12 * k + 6], xsl[b][:, 0:512])
                    E.bn_stats(bns[:, 12 * k + 6:12 * k + 12], xsl[b][:, 512:1024])
                    E.reduce_max(smax[:, k:k + 1], xsl[b][:, 0:1024], AX.X)
                    E.tensor_reduce(smin[:, k:k + 1], xsl[b][:, 0:1024], AX.X, OP.min).then_inc(VV.h, 1)
                em("vector", f)
                vt[b] = VV.inc()
            st["xsl_free"] = [vt[0], vt[1]]
            tok_stats = vt[(NT - 1) % 2]

            stp = mkvs(12)

            def f(E, tok=tok_stats, stp=stp):
                W(E, tok, tok_kons)
                S = stp(E)
                me = bns[:, 1::3]
                m2 = bns[:, 2::3]
                R0, R1, R2 = stw[:, 0:2 * NT], stw[:, 2 * NT:3 * NT], stw[:, 3 * NT:4 * NT]
                R3, R4, R5 = stw[:, 4 * NT:5 * NT], stw[:, 5 * NT:6 * NT], stw[:, 6 * NT:7 * NT]
                S(E.tensor_tensor(R0, me[:, 0::2], me[:, 1::2], OP.add))
                S(E.tensor_tensor(R1, R0[:, 0::2], R0[:, 1::2], OP.add))
                S(E.tensor_tensor(R0, m2[:, 0::2], m2[:, 1::2], OP.add))
                S(E.tensor_tensor(R2, R0[:, 0::2], R0[:, 1::2], OP.add))
                S(E.tensor_tensor(stw[:, 4 * NT:8 * NT], me, me, OP.mult))
                S(E.tensor_tensor(R0, stw[:, 4 * NT:8 * NT][:, 0::2],
                                  stw[:, 4 * NT:8 * NT][:, 1::2], OP.add))
                S(E.tensor_tensor(R3, R0[:, 0::2], R0[:, 1::2], OP.add))
                S(E.tensor_scalar(R4, R1, 0.25, None, OP.mult))
                S(E.tensor_scalar(R5, R2, 1.0 / 1024.0, None, OP.mult))
                S(E.scalar_tensor_tensor(R5, R3, 0.25, R5, OP.mult, OP.add))
                S(E.tensor_tensor(R0[:, 0:NT], R4, R4, OP.mult))
                S(E.scalar_tensor_tensor(R5, R0[:, 0:NT], -1.0, R5, OP.mult, OP.add))
                E.engine_nop().then_inc(VV.h, 1)
            em("vector", f)
            tok_var = VV.inc()
            sstp = mkss(1)

            def f(E, tok=tok_var, sstp=sstp):
                W(E, tok)
                S = sstp(E)
                S(E.activation(stw[:, 0:NT], stw[:, 6 * NT:7 * NT], ACT.Ln,
                               bias=seed[:, 8:9], scale=1.0))
                E.activation(stw[:, 7 * NT:8 * NT], stw[:, 0:NT], ACT.Exp,
                             bias=0.0, scale=-0.5).then_inc(SS.h, 1)
            em("scalar", f)
            tok_rln = SS.inc()
            stp2 = mkvs(10)

            def f(E, tok=tok_rln, stp=stp2):
                W(E, tok)
                S = stp(E)
                mean = stw[:, 5 * NT:6 * NT]
                rln = stw[:, 7 * NT:8 * NT]
                t0 = stw[:, 0:NT]
                t1 = stw[:, NT:2 * NT]
                t2 = stw[:, 2 * NT:3 * NT]
                S(E.tensor_tensor(t0, smax[:], mean, OP.subtract))
                S(E.tensor_tensor(t1, mean, smin[:], OP.subtract))
                S(E.tensor_tensor(t0, t0, t1, OP.max))
                S(E.tensor_tensor(t1, t0, rln, OP.mult))
                S(E.tensor_scalar(t1, t1, EPS_CLIP, None, OP.max))
                S(E.reciprocal(t2, t1))
                S(E.scalar_tensor_tensor(alf[:], rln, 127.0, t2, OP.mult, OP.mult))
                S(E.tensor_scalar(bet[:], t1, 1.0 / 127.0, None, OP.mult))
                S(E.tensor_tensor(t0, alf[:], mean, OP.mult))
                S(E.tensor_scalar(bia[:], t0, -1.0, kons[:, 0:1], OP.mult, OP.add))
                E.engine_nop().then_inc(VV.h, 1)
            em("vector", f)
            return VV.inc()

        tok_ab = {}

        def emit_xside(u):
            s, b = divmod(u, NSLAB)
            bb = u % 2
            toks_half = [None, None]
            for h2 in range(2):
                t0 = LP.issue(
                    em, "sync",
                    lambda E, s=s, b=b, h2=h2: E.dma_start(
                        out=xsl[h2][:, 0:1024], in_=x3[s, 4 * b + 2 * h2]),
                    pre_toks=(st["xsl_free"][h2],))
                t1 = LP.issue(
                    em, "sync",
                    lambda E, s=s, b=b, h2=h2: E.dma_start(
                        out=xsl[h2][:, 1024:2048], in_=x3[s, 4 * b + 2 * h2 + 1]))
                toks_half[h2] = (t0, t1)
            last_e = None
            for j in range(4):
                k = 4 * b + j
                jb = j % 2

                def f(E, k=k, j=j, jb=jb, toks=toks_half[j // 2],
                      tok2=tok_ab[s], tokf=st["nIB_free"][jb], tokf2=st["nIB_free"][1 - jb]):
                    W(E, toks[0], toks[1], tok2, tokf, tokf2)
                    E.activation(nI[jb][:], xsl[j // 2][:, 1024 * (j % 2):1024 * (j % 2 + 1)],
                                 ACT.Identity, bias=bia[:, k:k + 1],
                                 scale=alf[:, k:k + 1]).then_inc(SS.h, 1)
                em("scalar", f)
                tok_s = SS.inc()

                stpn = mkvs(1)

                def f(E, k=k, jb=jb, tok=tok_s, stpn=stpn):
                    W(E, tok)
                    S = stpn(E)
                    S(E.tensor_scalar(nB[jb][:], nI[jb][:], -128.0, None, OP.add))
                    E.tensor_scalar(nB[jb][:], nB[jb][:], bet[:, k:k + 1], None,
                                    OP.mult).then_inc(VV.h, 1)
                em("vector", f)
                tok_v = VV.inc()

                def f(E, jb=jb, tok=tok_v, tokp=last_e):
                    W(E, tok, tokp)
                    ins = None
                    for c in range(8):
                        ins = E.transpose(ntp[:, 128 * c:128 * (c + 1)],
                                          nB[jb][:, 128 * c:128 * (c + 1)], identb[:])
                    ins.then_inc(TT.h, 1)
                em("tensor", f)
                tok_t = TT.inc()

                def f(E, b=b, bb=bb, j=j, tok=tok_t, tokbl1=st["nT_bl1_done"][bb]):
                    W(E, tok, tokbl1)
                    ins = None
                    for c in range(8):
                        ins = E.tensor_scalar(nT[bb][c][:, 1 + 128 * j:1 + 128 * (j + 1)],
                                              ntp[:, 128 * c:128 * (c + 1)], 0.0, None, OP.add)
                    if j == 0:
                        if b > 0:
                            for c in range(8):
                                E.tensor_scalar(nT[1 - bb][c][:, 513:514],
                                                ntp[:, 128 * c:128 * c + 1], 0.0, None, OP.add)
                            for c in range(8):
                                ins = E.tensor_scalar(nT[bb][c][:, 0:1],
                                                      nT[1 - bb][c][:, 512:513], 0.0, None, OP.add)
                        else:
                            for c in range(8):
                                ins = E.memset(nT[bb][c][:, 0:1], 0.0)
                    if j == 3 and b == NSLAB - 1:
                        for c in range(8):
                            ins = E.memset(nT[bb][c][:, 513:514], 0.0)
                    ins.then_inc(VV.h, 1)
                em("vector", f)
                last_e = VV.inc()
                st["nIB_free"][jb] = last_e
                if j == 0 and b > 0:
                    st["halo_done"][u - 1] = last_e
                if j == 1:
                    st["xsl_free"][0] = last_e
                if j == 3:
                    st["xsl_free"][1] = last_e
            st["xside_done"][u] = last_e
            if b == NSLAB - 1:
                st["halo_done"][u] = last_e

        def emit_slab(u):
            s, b = divmod(u, NSLAB)
            bb = u % 2
            tok_w1g = [None] * 4
            for g in range(2):
                tok_w1g[g] = LP.issue(
                    em, "sync",
                    lambda E, g=g: E.dma_start(out=w1s[g % 2][:], in_=dw1q[g]),
                    pre_toks=(st["w1_used"][g % 2],) + tok_w1sp)

            if debug and u == 0 and src is dx:
                OP_.issue(em, "gpsimd", lambda E: E.dma_start(out=ddbg["w1s0"], in_=w1s[0][:]),
                          pre_toks=(tok_w1g[0],))
            nm0 = len(st["conv"])
            tok_am = None
            for m in range(32):
                g = m // 8
                gb = g % 2
                mc = m % 8
                hp = hps[m % 2]
                gi = nm0 + m
                tok_hp_free = st["conv"][gi - 2] if gi >= 2 else None
                if mc == 0 and g >= 2:
                    tok_w1g[g] = LP.issue(
                        em, "sync",
                        lambda E, g=g: E.dma_start(out=w1s[g % 2][:], in_=dw1q[g]),
                        pre_toks=(st["w1_used"][g % 2],) + tok_w1sp)

                def f(E, bb=bb, m=m, gb=gb, mc=mc, hp=hp, tokn=st["xside_done"][u],
                      tokh=st["halo_done"].get(u), tokw=tok_w1g[g], tokf=tok_hp_free):
                    W(E, tokn, tokh, tokw, tokf)
                    ins = None
                    for kf in range(8):
                        lhs = w1s[gb][:, 1024 * kf + 128 * mc:1024 * kf + 128 * (mc + 1) - 0]
                        E.matmul(hp[:, 0:512], lhs, nT[bb][kf][:, 0:512],
                                 start=(kf == 0), stop=(kf == 7))
                        ins = E.matmul(hp[:, 512:514], lhs, nT[bb][kf][:, 512:514],
                                       start=(kf == 0), stop=(kf == 7))
                    ins.then_inc(TT.h, 1)
                em("tensor", f)
                tok_bl1m = TT.inc()
                if mc == 7:
                    st["w1_used"][gb] = tok_bl1m
                if m == 31:
                    st["nT_bl1_done"][bb] = tok_bl1m

                c0 = csc[2 * (m % 2)]
                c1 = csc[2 * (m % 2) + 1]

                stpc = mkvs(2)

                def f(E, m=m, hp=hp, c0=c0, c1=c1, tok=tok_bl1m, stpc=stpc,
                      tokf=st["silu"][gi - 2] if gi >= 2 else None):
                    W(E, tok, tok_taps, tokf)
                    S = stpc(E)
                    S(E.tensor_scalar(c0[:], hp[:, 1:513], convwt[:, 3 * m + 1:3 * m + 2],
                                      convbt[:, m:m + 1], OP.mult, OP.add))
                    S(E.scalar_tensor_tensor(c1[:], hp[:, 0:512], convwt[:, 3 * m:3 * m + 1],
                                             c0[:], OP.mult, OP.add))
                    E.scalar_tensor_tensor(c0[:], hp[:, 2:514], convwt[:, 3 * m + 2:3 * m + 3],
                                           c1[:], OP.mult, OP.add).then_inc(VV.h, 1)
                em("vector", f)
                tok_conv = VV.inc()
                st["conv"].append(tok_conv)
                if debug and u == 0 and m == 0 and src is dx:
                    def fdbg(E, hp=hp, c0=c0, tok=tok_bl1m, tok2=tok_conv):
                        W(E, tok, tok2)
                        E.tensor_scalar(bl2e[0][:, 0:512], hp[:, 0:512], 0.0, None, OP.add)
                        E.tensor_scalar(bl2e[1][:, 0:2], hp[:, 512:514], 0.0, None, OP.add)
                        E.tensor_scalar(bl2e[1][:, 2:512], c0[:, 0:510], 0.0, None, OP.add).then_inc(VV.h, 1)
                    em("vector", fdbg)
                    tok_d = VV.inc()
                    OP_.issue(em, "gpsimd", lambda E: E.dma_start(out=ddbg["hp0"][:, 0:512], in_=bl2e[0][:, 0:512]),
                              pre_toks=(tok_d,))
                    OP_.issue(em, "gpsimd", lambda E: E.dma_start(out=ddbg["hp0"][:, 512:514], in_=bl2e[1][:, 0:2]),
                              pre_toks=(tok_d,))
                    tok_dd = OP_.issue(em, "gpsimd", lambda E: E.dma_start(out=ddbg["cv0"][:, 0:510], in_=bl2e[1][:, 2:512]),
                              pre_toks=(tok_d,))
                    st["bl2_done"] = tok_dd
                    st["conv"][-1] = tok_d

                def f(E, m=m, c0=c0, tok=tok_conv, tokq=st["quant_done"]):
                    W(E, tok, tokq)
                    E.activation(aAr[:, 512 * m:512 * (m + 1)], c0[:], ACT.Silu).then_inc(SS.h, 1)
                em("scalar", f)
                tok_sl = SS.inc()
                st["silu"].append(tok_sl)

            tok_amax = tok_sl

            def f(E, tok=tok_amax, toka=st["amx_T"]):
                W(E, tok, toka)
                E.tensor_reduce(amx[:], aAr[:].rearrange("p (c w) -> p w c", c=32),
                                AX.X, OP.max, apply_absolute_value=True).then_inc(VV.h, 1)
            em("vector", f)
            tok_amax = VV.inc()

            if debug and u == 0 and src is dx:
                OP_.issue(em, "gpsimd", lambda E: E.dma_start(out=ddbg["nT0"], in_=nT[0][0][:]),
                          pre_toks=(st["xside_done"][0], st["halo_done"].get(0)))
                OP_.issue(em, "gpsimd", lambda E: E.dma_start(out=ddbg["apre"], in_=aAr[:]),
                          pre_toks=(tok_amax,))

            def f(E, tok=tok_amax, tokp=st["last_stt"]):
                W(E, tok, tokp)
                ins = None
                for j in range(4):
                    ins = E.transpose(pmis[:, 128 * j:128 * (j + 1)],
                                      amx[:, 128 * j:128 * (j + 1)], identf[:])
                ins.then_inc(TT.h, 1)
            em("tensor", f)
            st["amx_T"] = TT.inc()
            tok_amT = st["amx_T"]

            stp_s2 = mkvs(5)

            def f(E, tok=tok_amT, stp=stp_s2):
                W(E, tok)
                S = stp(E)
                for j in range(4):
                    ins = E.reduce_max(s2t[:, j:j + 1], pmis[:, 128 * j:128 * (j + 1)], AX.X)
                S(ins)
                S(E.tensor_scalar(s2t[:], s2t[:], EPS_CLIP, None, OP.max))
                S(E.tensor_scalar(g2t[:], s2t[:], mw2h[:, 0:1], None, OP.mult))
                S(E.reciprocal(s2t[:], s2t[:]))
                S(E.tensor_scalar(s2t[:], s2t[:], 127.0, None, OP.mult))
                E.engine_nop().then_inc(VV.h, 1)
            em("vector", f)
            tok_s2 = VV.inc()

            tok_rank1 = None
            for j in range(4):
                def f(E, j=j, tok=tok_s2, tokp=tok_rank1):
                    W(E, tok, tokp)
                    E.transpose(pmis[0:1, 128 * j:128 * (j + 1)], s2t[:, j:j + 1],
                                identf[:]).then_inc(TT.h, 1)
                em("tensor", f)
                tok_jT = TT.inc()

                def f(E, j=j, tok=tok_jT):
                    W(E, tok)
                    E.tensor_scalar(s2r[0:1, 0:128], pmis[0:1, 128 * j:128 * (j + 1)],
                                    0.0, None, OP.add).then_inc(VV.h, 1)
                em("vector", f)
                tok_row = VV.inc()

                def f(E, j=j, tok=tok_row):
                    W(E, tok)
                    E.matmul(pmis[:, 128 * j:128 * (j + 1)], onesf[0:1, :],
                             s2r[0:1, 0:128], start=True, stop=True).then_inc(TT.h, 1)
                em("tensor", f)
                tok_rank1 = TT.inc()
            tok_s2b = tok_rank1

            tok_q = None
            for m in range(32):
                stpz = mkvs(2)

                def f(E, m=m, tok=tok_s2b, stpz=stpz):
                    W(E, tok)
                    S = stpz(E)
                    acol = aAr[:, 512 * m:512 * (m + 1)]
                    S(E.tensor_tensor(csc[0][:], acol, pmis[:, 0:512], OP.mult))
                    S(E.tensor_scalar(nis[:], csc[0][:], kons[:, 0:1], None, OP.add))
                    E.tensor_scalar(acol, nis[:], -128.0, None, OP.add).then_inc(VV.h, 1)
                em("vector", f)
                tok_q = VV.inc()
            st["quant_done"] = tok_q
            if debug and u == 0 and src is dx:
                OP_.issue(em, "gpsimd", lambda E: E.dma_start(out=ddbg["amx"], in_=amx[:]),
                          pre_toks=(tok_amax,))
                OP_.issue(em, "gpsimd", lambda E: E.dma_start(out=ddbg["s2t"], in_=s2t[:]),
                          pre_toks=(tok_q,))
                OP_.issue(em, "gpsimd", lambda E: E.dma_start(out=ddbg["g2t"], in_=g2t[:]),
                          pre_toks=(tok_q,))
                OP_.issue(em, "gpsimd", lambda E: E.dma_start(out=ddbg["apost"], in_=aAr[:]),
                          pre_toks=(tok_q,))

            for j in range(4):
                st["xrr_tok"][j] = LP.issue(
                    em, "sync",
                    lambda E, s=s, b=b, j=j: E.dma_start(out=xrr[j][:], in_=x3[s, 4 * b + j]),
                    pre_toks=(st["xout_done"][j],))
            toks_xr = tuple(st["xrr_tok"])

            tok_prev_pair_ev = None
            tok_stt_last = st["last_stt"]
            for p in range(4):
                def f(E, p=p, tok=tok_q, toke=tok_prev_pair_ev,
                      tokb=st["bl2_done"] if p == 0 else None):
                    W(E, tok, toke, tokb, *toks_w2ld)
                    ins = None
                    for half in range(2):
                        d = 2 * p + half
                        for c in range(32):
                            ins = E.matmul(bl2p[:, 512 * half:512 * (half + 1)],
                                           w2q[c][:, 128 * d:128 * (d + 1)],
                                           aAr[:, 512 * c:512 * (c + 1)],
                                           start=(c == 0), stop=(c == 31))
                    ins.then_inc(TT.h, 1)
                em("tensor", f)
                tok_mm = TT.inc()

                def f(E, tok=tok_mm):
                    W(E, tok)
                    E.tensor_scalar(bl2e[0][:], bl2p[:, 0:512], 0.0, None, OP.add)
                    E.tensor_scalar(bl2e[1][:], bl2p[:, 512:1024], 0.0, None, OP.add).then_inc(VV.h, 1)
                em("vector", f)
                tok_ev = VV.inc()
                tok_prev_pair_ev = tok_ev
                if debug and u == 0 and p == 0 and src is dx:
                    OP_.issue(em, "gpsimd", lambda E: E.dma_start(out=ddbg["bl2e0"], in_=bl2e[0][:]),
                              pre_toks=(tok_ev,))

                for half in range(2):
                    d = 2 * p + half

                    def f(E, half=half, tok=tok_ev, tokq2=tok_q, tokst=tok_stt_last):
                        W(E, tok, tokq2, tokst)
                        ins = None
                        for j in range(4):
                            ins = E.transpose(pmis[:, 128 * j:128 * (j + 1)],
                                              bl2e[half][:, 128 * j:128 * (j + 1)], identf[:])
                        ins.then_inc(TT.h, 1)
                    em("tensor", f)
                    tok_tr = TT.inc()

                    def f(E, d=d, tok=tok_tr, tokx=toks_xr):
                        W(E, tok, *tokx)
                        ins = None
                        for j in range(4):
                            ins = E.scalar_tensor_tensor(xrr[j][:, 128 * d:128 * (d + 1)],
                                                         pmis[:, 128 * j:128 * (j + 1)],
                                                         g2t[:, j:j + 1],
                                                         xrr[j][:, 128 * d:128 * (d + 1)],
                                                         OP.mult, OP.add)
                        ins.then_inc(VV.h, 1)
                    em("vector", f)
                    tok_stt_last = VV.inc()
            st["bl2_done"] = tok_prev_pair_ev
            st["last_stt"] = tok_stt_last

            for j in range(4):
                st["xout_done"][j] = OP_.issue(
                    em, "gpsimd",
                    lambda E, s=s, b=b, j=j: E.dma_start(out=y3[s, 4 * b + j], in_=xrr[j][:]),
                    pre_toks=(tok_stt_last,))

        for s in range(NSEQ):
            tok_ab[s] = emit_stats(s)
            if debug and s == 0 and src is dx:
                for nm, t in [("kons", kons), ("alf", alf), ("bet", bet), ("bia", bia)]:
                    def fd(E, nm=nm, t=t, tok=tok_ab[s]):
                        W(E, tok)
                        E.dma_start(out=ddbg[nm][:, 0:t.shape[1] if False else None] if False else ddbg[nm][:, 0:NT] if nm != "kons" else ddbg[nm],
                                    in_=t[:]).then_inc(SS.h, 0) if False else None
                    pass
                OP_.issue(em, "gpsimd", lambda E: E.dma_start(out=ddbg["kons"], in_=kons[:, 0:1]), pre_toks=(tok_ab[s],))
                OP_.issue(em, "gpsimd", lambda E: E.dma_start(out=ddbg["alf"][:, 0:NT], in_=alf[:]), pre_toks=(tok_ab[s],))
                OP_.issue(em, "gpsimd", lambda E: E.dma_start(out=ddbg["bet"][:, 0:NT], in_=bet[:]), pre_toks=(tok_ab[s],))
                OP_.issue(em, "gpsimd", lambda E: E.dma_start(out=ddbg["bia"][:, 0:NT], in_=bia[:]), pre_toks=(tok_ab[s],))
                OP_.issue(em, "gpsimd", lambda E: E.dma_start(out=ddbg["stw"], in_=stw[:]), pre_toks=(tok_ab[s],))
                OP_.issue(em, "gpsimd", lambda E: E.dma_start(out=ddbg["bns"], in_=bns[:]), pre_toks=(tok_ab[s],))
                OP_.issue(em, "gpsimd", lambda E: E.dma_start(out=ddbg["smax"][:, 0:NT], in_=smax[:]), pre_toks=(tok_ab[s],))
                OP_.issue(em, "gpsimd", lambda E: E.dma_start(out=ddbg["smin"][:, 0:NT], in_=smin[:]), pre_toks=(tok_ab[s],))
            base = s * NSLAB
            emit_xside(base)
            if NSLAB > 1:
                emit_xside(base + 1)
            for b in range(NSLAB):
                emit_slab(base + b)
                if b + 2 < NSLAB:
                    emit_xside(base + b + 2)
        em.flush()
        ps.close()

    def emit_attn(src, dst, upto=99, nheads=NH):
        ps = contextlib.ExitStack()
        wkdecb = [ps.enter_context(nc.sbuf_tensor(f"Awkd{c}", [128, D], BF16)) for c in range(2)]
        wvdecb = [ps.enter_context(nc.sbuf_tensor(f"Awvd{c}", [128, D], BF16)) for c in range(2)]
        maskt = ps.enter_context(nc.sbuf_tensor("Amask", [128, 128], F32))
        gkvt = ps.enter_context(nc.sbuf_tensor("Agkv", [128, 2], F32))
        wdf = ps.enter_context(nc.sbuf_tensor("Awdf", [128, 1024], F32))
        sc3 = [ps.enter_context(nc.psum_tensor(f"Asc{i}", [128, 1024], F32)) for i in range(3)]
        ptp2 = [ps.enter_context(nc.psum_tensor(f"Aptp{i}", [128, 1024], BF16)) for i in range(2)]
        ptp = ptp2[0]

        x3 = src.rearrange("s (k p) d -> s k p d", p=128)
        y3 = dst.rearrange("s (k p) d -> s k p d", p=128)

        tok_g = LP.issue(em, "sync", lambda E: E.dma_start(out=gkvt[:], in_=dgkv))
        tok_m = LP.issue(em, "sync", lambda E: E.dma_start(out=maskt[:], in_=dmask))

        tok_wdf_free = None
        tok_dec = []
        for c in range(2):
            for dsrcw, dstt in ((dwkdec, wkdecb), (dwvdec, wvdecb)):
                tok_l = LP.issue(em, "sync",
                                 lambda E, c=c, dsrcw=dsrcw: E.dma_start(
                                     out=wdf[:], in_=dsrcw[c * 128:(c + 1) * 128, :]),
                                 pre_toks=(tok_wdf_free,))

                def f(E, c=c, dstt=dstt, tok=tok_l):
                    W(E, tok, tok_g)
                    E.tensor_scalar(dstt[c][:], wdf[:], gkvt[:, c:c + 1], None,
                                    OP.mult).then_inc(VV.h, 1)
                em("vector", f)
                tok_wdf_free = VV.inc()
                tok_dec.append(tok_wdf_free)
        toks_dec = tuple(tok_dec)
        em.flush()

        SC96 = float(1.0 / np.sqrt(96.0))
        prev_seq_store = None
        wdf_last = tok_wdf_free
        for s in range(NSEQ):
            pseq = contextlib.ExitStack()
            ofm = [pseq.enter_context(nc.sbuf_tensor(f"Aof{s}_{c}", [128, TS], BF16, side="left"))
                   for c in range(8)]
            pmid = contextlib.ExitStack()
            kvr = [pmid.enter_context(nc.sbuf_tensor(f"Akvr{s}_{c}", [128, TS], BF16, side="left"))
                   for c in range(2)]
            krot = [pmid.enter_context(nc.sbuf_tensor(f"Akr{s}_{c}", [128, TS], BF16, side="left"))
                    for c in range(4)]

            Rq = contextlib.ExitStack()
            qcb = [Rq.enter_context(nc.sbuf_tensor(f"Aqcb{s}_{c}", [128, TS], BF16, side="right"))
                   for c in range(4)]
            R2 = contextlib.ExitStack()
            uT = [R2.enter_context(nc.sbuf_tensor(f"AuT{s}_{c}", [128, TS], BF16, side="right"))
                  for c in range(8)]
            wkvb = [R2.enter_context(nc.sbuf_tensor(f"Awkv{s}_{c}", [128, KVR], BF16, side="right"))
                    for c in range(8)]
            wqb = [R2.enter_context(nc.sbuf_tensor(f"Awq{s}_{c}", [128, QR], BF16, side="right"))
                   for c in range(8)]
            wkrb = [R2.enter_context(nc.sbuf_tensor(f"Awkr{s}_{c}", [128, 512], BF16, side="right"))
                    for c in range(8)]
            cosf = R2.enter_context(nc.sbuf_tensor(f"Acos{s}", [128, TS], F32, side="right"))
            ssinf = R2.enter_context(nc.sbuf_tensor(f"Assin{s}", [128, TS], F32, side="right"))
            bnsA = R2.enter_context(nc.sbuf_tensor(f"Abns{s}", [128, 12 * NT], F32, side="right"))
            stwA = R2.enter_context(nc.sbuf_tensor(f"Astw{s}", [128, 8 * NT], F32, side="right"))
            biaA = R2.enter_context(nc.sbuf_tensor(f"AbiaA{s}", [128, NT], F32, side="right"))
            rlnA = R2.enter_context(nc.sbuf_tensor(f"ArlnA{s}", [128, NT], F32, side="right"))
            xtl = [R2.enter_context(nc.sbuf_tensor(f"Axt{s}_{i}", [128, 1024], F32, side="right"))
                   for i in range(2)]
            uB = [R2.enter_context(nc.sbuf_tensor(f"AuB{s}_{i}", [128, 1024], BF16, side="right"))
                  for i in range(2)]
            sqs = R2.enter_context(nc.sbuf_tensor(f"Asqs{s}", [128, 512], F32, side="right"))
            rtok = R2.enter_context(nc.sbuf_tensor(f"Artok{s}", [128, 8], F32, side="right"))
            kvf = [R2.enter_context(nc.sbuf_tensor(f"Akvf{s}_{i}", [128, 512], F32, side="right"))
                   for i in range(2)]
            rrow = R2.enter_context(nc.sbuf_tensor(f"Arrow{s}", [128, 512], F32, side="right"))
            krf = R2.enter_context(nc.sbuf_tensor(f"Akrf{s}", [128, 512], F32, side="right"))
            krs = R2.enter_context(nc.sbuf_tensor(f"Akrs{s}", [128, 512], F32, side="right"))
            kr1 = R2.enter_context(nc.sbuf_tensor(f"Akr1{s}", [128, 512], F32, side="right"))

            tok_wcast = wdf_last

            def ldcast(dsts, dsrc, width, scale):
                nonlocal tok_wcast
                for c in range(len(dsts)):
                    tok_l = LP.issue(em, "sync",
                                     lambda E, c=c, dsrc=dsrc, width=width: E.dma_start(
                                         out=wdf[:, 0:width], in_=dsrc[c * 128:(c + 1) * 128, :]),
                                     pre_toks=(tok_wcast,))

                    def f(E, c=c, dsts=dsts, width=width, scale=scale, tok=tok_l):
                        W(E, tok)
                        E.tensor_scalar(dsts[c][:, 0:width], wdf[:, 0:width], scale,
                                        None, OP.mult).then_inc(VV.h, 1)
                    em("vector", f)
                    tok_wcast = VV.inc()

            ldcast(wkvb, dwkv, KVR, 1.0)
            ldcast(wqb, dwq, QR, 1.0)
            ldcast(wkrb, dwkrope, 512, 1.0)
            tok_c1 = LP.issue(em, "sync", lambda E: E.dma_start(out=cosf[:], in_=dcos))
            tok_c2 = LP.issue(em, "sync", lambda E: E.dma_start(out=ssinf[:], in_=dssin))
            tok_w_ready = tok_wcast
            wdf_last = tok_wcast

            def bail(tok):
                for k in range(NT):
                    OP_.issue(em, "gpsimd",
                              lambda E, s=s, k=k: E.dma_start(out=y3[s, k], in_=x3[s, k]),
                              pre_toks=(tok,))
                em.flush()

            if upto <= 10:
                bail(tok_w_ready)
                R2.close(); Rq.close(); pmid.close(); pseq.close()
                fence()
                continue

            vt = [prev_seq_store, prev_seq_store]
            for k in range(NT):
                bq = k % 2
                tok_l = LP.issue(em, "sync",
                                 lambda E, s=s, k=k, bq=bq: E.dma_start(out=xtl[bq][:], in_=x3[s, k]),
                                 pre_toks=(vt[bq],))

                def f(E, k=k, bq=bq, tok=tok_l):
                    W(E, tok)
                    E.bn_stats(bnsA[:, 12 * k:12 * k + 6], xtl[bq][:, 0:512])
                    E.bn_stats(bnsA[:, 12 * k + 6:12 * k + 12], xtl[bq][:, 512:1024]).then_inc(VV.h, 1)
                em("vector", f)
                vt[bq] = VV.inc()
            tok_stats = vt[(NT - 1) % 2]

            stp = mkvs(12)

            def f(E, tok=tok_stats, stp=stp):
                W(E, tok)
                S = stp(E)
                me = bnsA[:, 1::3]
                m2 = bnsA[:, 2::3]
                R0, R1, R2_ = stwA[:, 0:2 * NT], stwA[:, 2 * NT:3 * NT], stwA[:, 3 * NT:4 * NT]
                R3, R4, R5 = stwA[:, 4 * NT:5 * NT], stwA[:, 5 * NT:6 * NT], stwA[:, 6 * NT:7 * NT]
                S(E.tensor_tensor(R0, me[:, 0::2], me[:, 1::2], OP.add))
                S(E.tensor_tensor(R1, R0[:, 0::2], R0[:, 1::2], OP.add))
                S(E.tensor_tensor(R0, m2[:, 0::2], m2[:, 1::2], OP.add))
                S(E.tensor_tensor(R2_, R0[:, 0::2], R0[:, 1::2], OP.add))
                S(E.tensor_tensor(stwA[:, 4 * NT:8 * NT], me, me, OP.mult))
                S(E.tensor_tensor(R0, stwA[:, 4 * NT:8 * NT][:, 0::2],
                                  stwA[:, 4 * NT:8 * NT][:, 1::2], OP.add))
                S(E.tensor_tensor(R3, R0[:, 0::2], R0[:, 1::2], OP.add))
                S(E.tensor_scalar(R4, R1, 0.25, None, OP.mult))
                S(E.tensor_scalar(R5, R2_, 1.0 / 1024.0, None, OP.mult))
                S(E.scalar_tensor_tensor(R5, R3, 0.25, R5, OP.mult, OP.add))
                S(E.tensor_tensor(R0[:, 0:NT], R4, R4, OP.mult))
                S(E.scalar_tensor_tensor(R5, R0[:, 0:NT], -1.0, R5, OP.mult, OP.add))
                E.engine_nop().then_inc(VV.h, 1)
            em("vector", f)
            tok_var = VV.inc()
            sstp = mkss(1)

            def f(E, tok=tok_var, sstp=sstp):
                W(E, tok)
                S = sstp(E)
                S(E.activation(stwA[:, 0:NT], stwA[:, 6 * NT:7 * NT], ACT.Ln,
                               bias=seed[:, 8:9], scale=1.0))
                E.activation(rlnA[:], stwA[:, 0:NT], ACT.Exp, bias=0.0,
                             scale=-0.5).then_inc(SS.h, 1)
            em("scalar", f)
            tok_rln = SS.inc()
            stp2 = mkvs(2)

            def f(E, tok=tok_rln, stp=stp2):
                W(E, tok)
                S = stp(E)
                S(E.tensor_tensor(stwA[:, 0:NT], rlnA[:], stwA[:, 5 * NT:6 * NT], OP.mult))
                S(E.tensor_scalar(biaA[:], stwA[:, 0:NT], -1.0, None, OP.mult))
                E.engine_nop().then_inc(VV.h, 1)
            em("vector", f)
            tok_biaA = VV.inc()

            last_tr = None
            for k in range(NT):
                bq = k % 2
                tok_l = LP.issue(em, "sync",
                                 lambda E, s=s, k=k, bq=bq: E.dma_start(out=xtl[bq][:], in_=x3[s, k]),
                                 pre_toks=(vt[bq],))

                def f(E, k=k, bq=bq, tok=tok_l, tok2=tok_biaA):
                    W(E, tok, tok2)
                    E.activation(uB[bq][:], xtl[bq][:], ACT.Identity,
                                 bias=biaA[:, k:k + 1], scale=rlnA[:, k:k + 1]).then_inc(SS.h, 1)
                em("scalar", f)
                tok_u = SS.inc()

                def f(E, bq=bq, tok=tok_u, tokp=last_tr):
                    W(E, tok, tokp)
                    ins = None
                    for c in range(8):
                        ins = E.transpose(ptp[:, 128 * c:128 * (c + 1)],
                                          uB[bq][:, 128 * c:128 * (c + 1)], identb[:])
                    ins.then_inc(TT.h, 1)
                em("tensor", f)
                tok_t = TT.inc()

                def f(E, k=k, tok=tok_t):
                    W(E, tok)
                    ins = None
                    for c in range(8):
                        ins = E.tensor_scalar(uT[c][:, 128 * k:128 * (k + 1)],
                                              ptp[:, 128 * c:128 * (c + 1)], 0.0, None, OP.add)
                    ins.then_inc(VV.h, 1)
                em("vector", f)
                last_tr = VV.inc()
                vt[bq] = last_tr
            tok_uT = last_tr

            if upto <= 12:
                bail(tok_uT)
                R2.close(); Rq.close(); pmid.close(); pseq.close()
                fence()
                continue

            tok_kvr = None
            tok_rms_rd = None
            for nch2 in range(NCH):
                col = slice(512 * nch2, 512 * (nch2 + 1))

                def f(E, col=col, tok=tok_uT, tokq=tok_kvr, tokw=tok_w_ready):
                    W(E, tok, tokq, tokw)
                    ins = None
                    for m2 in range(2):
                        for kf in range(8):
                            ins = E.matmul(sc3[0][:, 512 * m2:512 * (m2 + 1)],
                                           wkvb[kf][:, 128 * m2:128 * (m2 + 1)],
                                           uT[kf][:, col], start=(kf == 0), stop=(kf == 7))
                    ins.then_inc(TT.h, 1)
                em("tensor", f)
                tok_kv = TT.inc()

                stpk = mkvs(4)

                def f(E, tok=tok_kv, stpk=stpk):
                    W(E, tok)
                    S = stpk(E)
                    S(E.tensor_scalar(kvf[0][:], sc3[0][:, 0:512], 0.0, None, OP.add))
                    S(E.tensor_scalar(kvf[1][:], sc3[0][:, 512:1024], 0.0, None, OP.add))
                    S(E.tensor_tensor(sqs[:], kvf[0][:], kvf[0][:], OP.mult))
                    S(E.tensor_tensor(rrow[:], kvf[1][:], kvf[1][:], OP.mult))
                    E.tensor_tensor(sqs[:], sqs[:], rrow[:], OP.add).then_inc(VV.h, 1)
                em("vector", f)
                tok_kve = VV.inc()

                def f(E, tok=tok_kve, tokr=tok_rms_rd):
                    W(E, tok, tokr)
                    ins = None
                    for j in range(4):
                        ins = E.transpose(sc3[1][:, 128 * j:128 * (j + 1)],
                                          sqs[:, 128 * j:128 * (j + 1)], identf[:])
                    ins.then_inc(TT.h, 1)
                em("tensor", f)
                tok_sqT = TT.inc()

                stpr = mkvs(1)

                def f(E, tok=tok_sqT, stpr=stpr):
                    W(E, tok)
                    S = stpr(E)
                    ins = None
                    for j in range(4):
                        ins = E.reduce_sum(rtok[:, j:j + 1], sc3[1][:, 128 * j:128 * (j + 1)], AX.X)
                    S(ins)
                    E.engine_nop().then_inc(VV.h, 1)
                em("vector", f)
                tok_rms_rd = VV.inc()

                sstp2 = mkss(1)

                def f(E, tok=tok_rms_rd, sstp2=sstp2):
                    W(E, tok)
                    S = sstp2(E)
                    S(E.activation(rtok[:, 4:8], rtok[:, 0:4], ACT.Ln,
                                   bias=seed[:, 9:10], scale=1.0 / 256.0))
                    E.activation(rtok[:, 4:8], rtok[:, 4:8], ACT.Exp,
                                 bias=0.0, scale=-0.5).then_inc(SS.h, 1)
                em("scalar", f)
                tok_rr = SS.inc()

                tok_rk = None
                for j in range(4):
                    def f(E, j=j, tok=tok_rr, tokp=tok_rk):
                        W(E, tok, tokp)
                        E.transpose(sc3[1][0:1, 128 * j:128 * (j + 1)], rtok[:, 4 + j:5 + j],
                                    identf[:]).then_inc(TT.h, 1)
                    em("tensor", f)
                    tok_rjT = TT.inc()

                    def f(E, j=j, tok=tok_rjT):
                        W(E, tok)
                        E.tensor_scalar(rrow[0:1, 0:128], sc3[1][0:1, 128 * j:128 * (j + 1)],
                                        0.0, None, OP.add).then_inc(VV.h, 1)
                    em("vector", f)
                    tok_rw = VV.inc()

                    def f(E, j=j, tok=tok_rw):
                        W(E, tok)
                        E.matmul(sc3[1][:, 128 * j:128 * (j + 1)], onesf[0:1, :],
                                 rrow[0:1, 0:128], start=True, stop=True).then_inc(TT.h, 1)
                    em("tensor", f)
                    tok_rk = TT.inc()
                tok_rb = tok_rk

                def f(E, col=col, tok=tok_rb, tok2=tok_kve):
                    W(E, tok, tok2)
                    E.tensor_tensor(kvr[0][:, col], kvf[0][:], sc3[1][:, 0:512], OP.mult)
                    E.tensor_tensor(kvr[1][:, col], kvf[1][:], sc3[1][:, 0:512], OP.mult).then_inc(VV.h, 1)
                em("vector", f)
                tok_kvr = VV.inc()

            if upto <= 13:
                bail(tok_kvr)
                R2.close(); Rq.close(); pmid.close(); pseq.close()
                fence()
                continue

            tok_qcb = None
            for m in range(4):
                for nch2 in range(NCH):
                    pp = sc3[0][:, 0:512] if (m * NCH + nch2) % 2 == 0 else sc3[0][:, 512:1024]

                    def f(E, m=m, nch2=nch2, pp=pp, tok=tok_uT,
                          tokp=tok_qcb if tok_qcb is not None else tok_kvr):
                        W(E, tok, tokp)
                        ins = None
                        for kf in range(8):
                            ins = E.matmul(pp, wqb[kf][:, 128 * m:128 * (m + 1)],
                                           uT[kf][:, 512 * nch2:512 * (nch2 + 1)],
                                           start=(kf == 0), stop=(kf == 7))
                        ins.then_inc(TT.h, 1)
                    em("tensor", f)
                    tok_mm = TT.inc()

                    def f(E, m=m, nch2=nch2, pp=pp, tok=tok_mm):
                        W(E, tok)
                        E.tensor_scalar(qcb[m][:, 512 * nch2:512 * (nch2 + 1)], pp,
                                        0.0, None, OP.add).then_inc(VV.h, 1)
                    em("vector", f)
                    tok_qcb = VV.inc()
            tok_qcb_done = tok_qcb

            def emit_rope(srcs, n_k, wtile, dstl, tok_src, init_free,
                          cosf=cosf, ssinf=ssinf, krf=krf, krs=krs, kr1=kr1,
                          tokc=None):
                tok_last = init_free
                for m in range(4):
                    for nch2 in range(NCH):
                        pp = sc3[0][:, 0:512] if (m * NCH + nch2) % 2 == 0 else sc3[0][:, 512:1024]

                        def f(E, m=m, nch2=nch2, pp=pp, tok=tok_src, tokp=tok_last):
                            W(E, tok, tokp)
                            ins = None
                            for kf in range(n_k):
                                ins = E.matmul(pp, wtile[kf][:, 128 * m:128 * (m + 1)],
                                               srcs[kf][:, 512 * nch2:512 * (nch2 + 1)],
                                               start=(kf == 0), stop=(kf == n_k - 1))
                            ins.then_inc(TT.h, 1)
                        em("tensor", f)
                        tok_mm = TT.inc()
                        stpr2 = mkvs(4)

                        def f(E, m=m, nch2=nch2, pp=pp, dstl=dstl, tok=tok_mm, stpr2=stpr2,
                              cosf=cosf, ssinf=ssinf, krf=krf, krs=krs, kr1=kr1, tokc=tokc):
                            W(E, tok, tok_c1, tok_c2, tokc)
                            S = stpr2(E)
                            col = slice(512 * nch2, 512 * (nch2 + 1))
                            S(E.tensor_scalar(krf[:], pp, 0.0, None, OP.add))
                            S(E.stream_shuffle(krs[:], krf[:], [(i + 16) % 32 for i in range(32)]))
                            S(E.tensor_tensor(kr1[:], krf[:], cosf[:, col], OP.mult))
                            S(E.tensor_tensor(krs[:], krs[:], ssinf[:, col], OP.mult))
                            E.tensor_tensor(dstl[m][:, col], kr1[:], krs[:], OP.add).then_inc(VV.h, 1)
                        em("vector", f)
                        tok_last = VV.inc()
                return tok_last

            tok_krot = emit_rope(uT, 8, wkrb, krot, tok_uT, tok_qcb_done)

            em.flush()
            R2.close()
            fence()

            qrot = [pmid.enter_context(nc.sbuf_tensor(f"Aqr{s}_{c}", [128, TS], BF16, side="left"))
                    for c in range(4)]
            R3 = contextlib.ExitStack()
            wqdb = [R3.enter_context(nc.sbuf_tensor(f"Awqd{s}_{c}", [128, D], BF16, side="right"))
                    for c in range(4)]
            wqrb = [R3.enter_context(nc.sbuf_tensor(f"Awqr{s}_{c}", [128, 512], BF16, side="right"))
                    for c in range(4)]
            cosf2 = R3.enter_context(nc.sbuf_tensor(f"Acos2{s}", [128, TS], F32, side="right"))
            ssinf2 = R3.enter_context(nc.sbuf_tensor(f"Assin2{s}", [128, TS], F32, side="right"))
            krf2 = R3.enter_context(nc.sbuf_tensor(f"Akrf2{s}", [128, 512], F32, side="right"))
            krs2 = R3.enter_context(nc.sbuf_tensor(f"Akrs2{s}", [128, 512], F32, side="right"))
            kr12 = R3.enter_context(nc.sbuf_tensor(f"Akr12{s}", [128, 512], F32, side="right"))
            ldcast(wqdb, dwqdec, D, SC96)
            ldcast(wqrb, dwqrope, 512, SC96)
            tok_c1b = LP.issue(em, "sync", lambda E: E.dma_start(out=cosf2[:], in_=dcos))
            tok_c2b = LP.issue(em, "sync", lambda E: E.dma_start(out=ssinf2[:], in_=dssin))
            tok_w2r = tok_wcast
            wdf_last = tok_wcast
            tok_c1, tok_c2 = tok_c1b, tok_c2b

            tok_qrot = emit_rope(qcb, 4, wqrb, qrot, tok_qcb_done, tok_w2r,
                                 cosf=cosf2, ssinf=ssinf2, krf=krf2, krs=krs2, kr1=kr12,
                                 tokc=tok_w2r)

            qcont = [pmid.enter_context(nc.sbuf_tensor(f"Aqc{s}_{c}", [128, TS], BF16, side="left"))
                     for c in range(8)]
            tok_qc = None
            for m in range(8):
                for nch2 in range(NCH):
                    pp = sc3[0][:, 0:512] if (m * NCH + nch2) % 2 == 0 else sc3[0][:, 512:1024]

                    def f(E, m=m, nch2=nch2, pp=pp, tok=tok_qcb_done,
                          tokp=tok_qc if tok_qc is not None else tok_qrot):
                        W(E, tok, tokp)
                        ins = None
                        for kf in range(4):
                            ins = E.matmul(pp, wqdb[kf][:, 128 * m:128 * (m + 1)],
                                           qcb[kf][:, 512 * nch2:512 * (nch2 + 1)],
                                           start=(kf == 0), stop=(kf == 3))
                        ins.then_inc(TT.h, 1)
                    em("tensor", f)
                    tok_mm = TT.inc()

                    def f(E, m=m, nch2=nch2, pp=pp, tok=tok_mm):
                        W(E, tok)
                        E.tensor_scalar(qcont[m][:, 512 * nch2:512 * (nch2 + 1)], pp,
                                        0.0, None, OP.add).then_inc(VV.h, 1)
                    em("vector", f)
                    tok_qc = VV.inc()
            tok_qc_done = tok_qc

            if upto <= 16:
                bail(tok_qc_done)
                em.flush()
                R3.close(); Rq.close(); pmid.close(); pseq.close()
                fence()
                continue

            em.flush()
            R3.close()
            Rq.close()
            fence()

            kca = [pmid.enter_context(nc.sbuf_tensor(f"Akca{s}_{c}", [128, TS], BF16, side="left"))
                   for c in range(8)]
            vkm = [pmid.enter_context(nc.sbuf_tensor(f"Avk{s}_{t}", [128, 1024], BF16, side="left"))
                   for t in range(NT)]

            tok_kca = None
            for hc in range(8):
                for nch2 in range(NCH):
                    pp = sc3[2][:, 0:512] if nch2 % 2 == 0 else sc3[2][:, 512:1024]

                    def f(E, hc=hc, nch2=nch2, pp=pp, tok=tok_kvr, tokp=tok_kca,
                          tokq=tok_qc_done):
                        W(E, tok, tokp, tokq, *toks_dec)
                        ins = None
                        for c in range(2):
                            ins = E.matmul(pp, wkdecb[c][:, 128 * hc:128 * (hc + 1)],
                                           kvr[c][:, 512 * nch2:512 * (nch2 + 1)],
                                           start=(c == 0), stop=(c == 1))
                        ins.then_inc(TT.h, 1)
                    em("tensor", f)
                    tok_mm = TT.inc()

                    def f(E, hc=hc, nch2=nch2, pp=pp, tok=tok_mm):
                        W(E, tok)
                        E.tensor_scalar(kca[hc][:, 512 * nch2:512 * (nch2 + 1)], pp,
                                        0.0, None, OP.add).then_inc(VV.h, 1)
                    em("vector", f)
                    tok_kca = VV.inc()
            tok_kca_done = tok_kca

            tok_vkm_l = tok_kca_done
            for tc in range(NT):
                def f(E, tc=tc, tok=tok_kvr, tokp=tok_vkm_l):
                    W(E, tok, tokp, *toks_dec)
                    ins = None
                    for half in range(2):
                        for c in range(2):
                            ins = E.matmul(sc3[2][:, 512 * half:512 * (half + 1)],
                                           kvr[c][:, 128 * tc:128 * (tc + 1)],
                                           wvdecb[c][:, 512 * half:512 * (half + 1)],
                                           start=(c == 0), stop=(c == 1))
                    ins.then_inc(TT.h, 1)
                em("tensor", f)
                tok_mmv = TT.inc()

                def f(E, tc=tc, tok=tok_mmv):
                    W(E, tok)
                    E.tensor_scalar(vkm[tc][:, 0:512], sc3[2][:, 0:512], 0.0, None, OP.add)
                    E.tensor_scalar(vkm[tc][:, 512:1024], sc3[2][:, 512:1024], 0.0,
                                    None, OP.add).then_inc(VV.h, 1)
                em("vector", f)
                tok_vkm_l = VV.inc()
            tok_vkm_done = tok_vkm_l

            em.flush()
            if upto <= 2:
                bail(tok_vkm_done)
                em.flush()
                pmid.close(); pseq.close()
                fence()
                continue

            p3 = contextlib.ExitStack()
            Pb = p3.enter_context(nc.sbuf_tensor(f"APb{s}", [128, TS], BF16, side="right"))
            Pn = p3.enter_context(nc.sbuf_tensor(f"APn{s}", [128, TS], BF16, side="right"))
            pts = [p3.enter_context(nc.sbuf_tensor(f"Apt{s}_{i}", [128, 128], BF16, side="right"))
                   for i in range(2)]
            nmx = p3.enter_context(nc.sbuf_tensor(f"Anmx{s}", [128, 4], F32, side="right"))
            ssm = p3.enter_context(nc.sbuf_tensor(f"Assm{s}", [128, 4], F32, side="right"))

            sc_free = [tok_vkm_done, tok_vkm_done, tok_vkm_done]
            sc_next = 0
            tok_pv_evac_last = None
            tok_ptp_evac = [None, None]
            tok_pts_free = [None, None]
            tok_core = None
            for h in range(nheads):
                p64 = 64 * (h % 2)
                p32 = 32 * (h % 4)
                for qi in range(NT):
                    width = 128 * (qi + 1)
                    nt_need = 1 if width <= 1024 else 2
                    ids = [(sc_next + i) % 3 for i in range(nt_need)]
                    sc_next = (sc_next + nt_need) % 3
                    pieces = []
                    rem, key0 = width, 0
                    for tid in ids:
                        wid = min(rem, 1024)
                        pieces.append((tid, key0, wid))
                        key0 += wid
                        rem -= wid

                    def f(E, h=h, p64=p64, p32=p32, qi=qi, pieces=tuple(pieces),
                          tokk=tok_kca_done, tokq=tok_qc_done, tokr1=tok_qrot,
                          tokr2=tok_krot, tokf=tuple(sc_free[tid] for tid in ids)):
                        W(E, tokk, tokq, tokr1, tokr2, *tokf)
                        ins = None
                        for (tid, k0, wid) in pieces:
                            nsub = (wid + 511) // 512
                            for cc in range(nsub):
                                w2 = min(512, wid - 512 * cc)
                                dstp = sc3[tid][:, 512 * cc:512 * cc + w2]
                                E.matmul(dstp,
                                         qcont[h // 2][p64:p64 + 64, 128 * qi:128 * (qi + 1)],
                                         kca[h // 2][p64:p64 + 64, k0 + 512 * cc:k0 + 512 * cc + w2],
                                         start=True, stop=False, tile_position=(p64, 0))
                                ins = E.matmul(dstp,
                                               qrot[h // 4][p32:p32 + 32, 128 * qi:128 * (qi + 1)],
                                               krot[h // 4][p32:p32 + 32, k0 + 512 * cc:k0 + 512 * cc + w2],
                                               start=False, stop=True, tile_position=(p32, 0))
                        ins.then_inc(TT.h, 1)
                    em("tensor", f)
                    tok_qk = TT.inc()

                    dtid, dk0, dwid = pieces[-1]
                    dcol = 128 * qi - dk0
                    stp_nm = mkvs(2 + (1 if len(pieces) == 2 else 0))

                    def f(E, dtid=dtid, dcol=dcol, pieces=tuple(pieces), tok=tok_qk, stp=stp_nm):
                        W(E, tok, tok_m)
                        S = stp(E)
                        S(E.tensor_tensor(sc3[dtid][:, dcol:dcol + 128],
                                          sc3[dtid][:, dcol:dcol + 128], maskt[:], OP.add))
                        ins = None
                        for i, (tid, k0, wid) in enumerate(pieces):
                            ins = E.reduce_max(nmx[:, i:i + 1], sc3[tid][:, 0:wid], AX.X,
                                               negate=True)
                        S(ins)
                        if len(pieces) == 2:
                            S(E.tensor_tensor(nmx[:, 0:1], nmx[:, 0:1], nmx[:, 1:2], OP.min))
                        E.engine_nop().then_inc(VV.h, 1)
                    em("vector", f)
                    tok_nm = VV.inc()

                    def f(E, pieces=tuple(pieces), tok=tok_nm):
                        W(E, tok)
                        ins = None
                        for i, (tid, k0, wid) in enumerate(pieces):
                            ins = E.activation(Pb[:, k0:k0 + wid], sc3[tid][:, 0:wid], ACT.Exp,
                                               bias=nmx[:, 0:1], scale=1.0,
                                               accum_out=ssm[:, i:i + 1])
                        ins.then_inc(SS.h, 1)
                    em("scalar", f)
                    tok_exp = SS.inc()
                    for tid in ids:
                        sc_free[tid] = tok_exp

                    stp_pn = mkvs(2 + (1 if len(pieces) == 2 else 0))

                    def f(E, qi=qi, npc=len(pieces), tok=tok_exp, stp=stp_pn):
                        W(E, tok)
                        S = stp(E)
                        if npc == 2:
                            S(E.tensor_tensor(ssm[:, 0:1], ssm[:, 0:1], ssm[:, 1:2], OP.add))
                        S(E.reciprocal(ssm[:, 2:3], ssm[:, 0:1]))
                        S(E.tensor_scalar(Pn[:, 0:128 * (qi + 1)], Pb[:, 0:128 * (qi + 1)],
                                          ssm[:, 2:3], None, OP.mult))
                        E.engine_nop().then_inc(VV.h, 1)
                    em("vector", f)
                    tok_pn = VV.inc()

                    ptid = ids[-1]
                    tok_pv = None
                    for c in range(qi + 1):
                        tb = c % 2
                        slot = (c // 2) % 8

                        def f(E, c=c, tb=tb, slot=slot, tok=tok_pn,
                              toke=tok_ptp_evac[tb]):
                            W(E, tok, toke)
                            E.transpose(ptp2[tb][:, 128 * slot:128 * (slot + 1)],
                                        Pn[:, 128 * c:128 * (c + 1)], identb[:]).then_inc(TT.h, 1)
                        em("tensor", f)
                        tok_tr = TT.inc()

                        def f(E, tb=tb, slot=slot, c=c, tok=tok_tr, tokf=tok_pts_free[c % 2]):
                            W(E, tok, tokf)
                            E.tensor_scalar(pts[c % 2][:], ptp2[tb][:, 128 * slot:128 * (slot + 1)],
                                            0.0, None, OP.add).then_inc(VV.h, 1)
                        em("vector", f)
                        tok_pte = VV.inc()
                        tok_ptp_evac[tb] = tok_pte

                        def f(E, h=h, p64=p64, qi=qi, c=c, ptid=ptid, tok=tok_pte,
                              tokpv=tok_pv_evac_last if c == 0 else None,
                              tokv=tok_vkm_done):
                            W(E, tok, tokpv, tokv)
                            E.matmul(sc3[ptid][p64:p64 + 64, 512:640],
                                     vkm[c][:, 64 * h:64 * (h + 1)], pts[c % 2][:],
                                     start=(c == 0), stop=(c == qi),
                                     tile_position=(0, p64)).then_inc(TT.h, 1)
                        em("tensor", f)
                        tok_pv = TT.inc()
                        tok_pts_free[c % 2] = tok_pv

                    def f(E, h=h, p64=p64, qi=qi, ptid=ptid, tok=tok_pv):
                        W(E, tok)
                        E.tensor_scalar(ofm[h // 2][p64:p64 + 64, 128 * qi:128 * (qi + 1)],
                                        sc3[ptid][p64:p64 + 64, 512:640],
                                        0.0, None, OP.add).then_inc(VV.h, 1)
                    em("vector", f)
                    tok_pv_evac_last = VV.inc()
                    sc_free[ptid] = tok_pv_evac_last
            tok_core = tok_pv_evac_last

            em.flush()
            p3.close()
            pmid.close()
            fence()
            if upto <= 3:
                bail(tok_core)
                em.flush()
                pseq.close()
                fence()
                continue

            p4 = contextlib.ExitStack()
            wob = [p4.enter_context(nc.sbuf_tensor(f"Awo{s}_{c}", [128, D], BF16, side="right"))
                   for c in range(8)]
            xa = [p4.enter_context(nc.sbuf_tensor(f"Axa{s}_{i}", [128, 1024], F32, side="right"))
                  for i in range(4)]
            woe = [p4.enter_context(nc.sbuf_tensor(f"Awoe{s}_{i}", [128, 512], F32, side="right"))
                   for i in range(2)]
            tok_wo_c = wdf_last
            for c in range(8):
                tok_l = LP.issue(em, "sync",
                                 lambda E, c=c: E.dma_start(out=wdf[:], in_=dwo[c * 128:(c + 1) * 128, :]),
                                 pre_toks=(tok_wo_c,))

                def f(E, c=c, tok=tok_l):
                    W(E, tok)
                    E.tensor_scalar(wob[c][:], wdf[:], 0.0, None, OP.add).then_inc(VV.h, 1)
                em("vector", f)
                tok_wo_c = VV.inc()
            tok_wo_ready = tok_wo_c
            wdf_last = tok_wo_c

            tok_xa_store = [None] * 4
            tok_woe_free = [None, None]
            tok_wo_tr = None
            for nch2 in range(NCH):
                toks_xl = []
                for j in range(4):
                    toks_xl.append(LP.issue(
                        em, "sync",
                        lambda E, s=s, nch2=nch2, j=j: E.dma_start(out=xa[j][:],
                                                                   in_=x3[s, 4 * nch2 + j]),
                        pre_toks=(tok_xa_store[j],)))
                toks_xl = tuple(toks_xl)
                tok_stt_l = None
                for d in range(8):
                    db = d % 2

                    def f(E, nch2=nch2, d=d, db=db, tok=tok_core, tokf=tok_woe_free[db],
                          tokw=tok_wo_ready):
                        W(E, tok, tokf, tokw)
                        ins = None
                        for c in range(8):
                            ins = E.matmul(sc3[0][:, 512 * db:512 * (db + 1)],
                                           wob[c][:, 128 * d:128 * (d + 1)],
                                           ofm[c][:, 512 * nch2:512 * (nch2 + 1)],
                                           start=(c == 0), stop=(c == 7))
                        ins.then_inc(TT.h, 1)
                    em("tensor", f)
                    tok_mm = TT.inc()

                    def f(E, db=db, tok=tok_mm):
                        W(E, tok)
                        E.tensor_scalar(woe[db][:], sc3[0][:, 512 * db:512 * (db + 1)],
                                        0.0, None, OP.add).then_inc(VV.h, 1)
                    em("vector", f)
                    tok_ev = VV.inc()
                    tok_woe_free[db] = tok_ev

                    def f(E, db=db, tok=tok_ev, tokp=tok_wo_tr):
                        W(E, tok, tokp)
                        ins = None
                        for j in range(4):
                            ins = E.transpose(sc3[1][:, 128 * j:128 * (j + 1)],
                                              woe[db][:, 128 * j:128 * (j + 1)], identf[:])
                        ins.then_inc(TT.h, 1)
                    em("tensor", f)
                    tok_tr = TT.inc()

                    def f(E, d=d, tok=tok_tr, tokx=toks_xl):
                        W(E, tok, *tokx)
                        ins = None
                        for j in range(4):
                            ins = E.tensor_tensor(xa[j][:, 128 * d:128 * (d + 1)],
                                                  xa[j][:, 128 * d:128 * (d + 1)],
                                                  sc3[1][:, 128 * j:128 * (j + 1)], OP.add)
                        ins.then_inc(VV.h, 1)
                    em("vector", f)
                    tok_stt_l = VV.inc()
                    tok_wo_tr = tok_stt_l

                for j in range(4):
                    tok_xa_store[j] = OP_.issue(
                        em, "gpsimd",
                        lambda E, s=s, nch2=nch2, j=j: E.dma_start(out=y3[s, 4 * nch2 + j],
                                                                   in_=xa[j][:]),
                        pre_toks=(tok_stt_l,))
            prev_seq_store = tok_xa_store[3]
            em.flush()
            p4.close()
            pseq.close()
            fence()

        ps.close()

    if do_attn:
        emit_ffn(dx, dx1, "A")
        fence()
        emit_attn(dx1, dx2 if do_ffn2 else dout, upto=attn_upto, nheads=attn_heads)
        fence()
        if do_ffn2:
            emit_ffn(dx2, dout, "C")
    else:
        emit_ffn(dx, dout, "A")

    toks = OP_.all_tok()

    def f(E, toks=tuple(toks)):
        for t in toks:
            E.wait_ge(t[0], t[1])
    em("gpsimd", f)
    em.flush()
    es.close()
    return nc



def _host_constants(TS):
    import ml_dtypes
    identb = np.eye(128, dtype=ml_dtypes.bfloat16)
    identf = np.eye(128, dtype=np.float32)
    onesf = np.ones((128, 128), dtype=np.float32)
    seed = np.zeros((128, 16), dtype=np.float32)
    seed[:, 0:8] = 1.5
    seed[:, 8] = EPS_LN
    seed[:, 9] = EPS_RMS
    j = np.arange(16, dtype=np.float32)
    inv_freq = (1.0 / (10000.0 ** (2.0 * j / 32.0))).astype(np.float32)
    t = np.arange(TS, dtype=np.float32)
    ang = np.outer(t, inv_freq)
    emb = np.concatenate([ang, ang], axis=1)
    cos32 = np.cos(emb).astype(np.float32)
    sin32 = np.sin(emb).astype(np.float32)
    p = np.arange(128)
    d32 = p % 32
    cosfm = np.ascontiguousarray(cos32[:, d32].T)
    sgn = np.where(d32 < 16, -1.0, 1.0).astype(np.float32)
    ssinfm = np.ascontiguousarray((sin32[:, d32] * sgn[None, :]).T)
    cmask = np.where(np.arange(128)[:, None] >= np.arange(128)[None, :],
                     np.float32(0.0), np.float32(NEG_BIG)).astype(np.float32)
    return dict(identb=identb, identf=identf, onesf=onesf, seed=seed,
                cosfm=cosfm, ssinfm=ssinfm, cmask=cmask)


def _prep_inputs(L0_attn_W_kv, L0_attn_g_kv, L0_attn_W_kdec, L0_attn_W_vdec,
                 L0_attn_W_q, L0_attn_W_qdec, L0_attn_W_krope, L0_attn_W_qrope,
                 L0_attn_W_o, L0_ffn_w1, L0_ffn_conv_w, L0_ffn_conv_b, L0_ffn_w2, TS):
    f32 = lambda a: np.ascontiguousarray(np.asarray(a, np.float32))
    consts = _host_constants(TS)
    return dict(
        wkv=f32(L0_attn_W_kv),
        gkv=np.ascontiguousarray(np.asarray(L0_attn_g_kv, np.float32).reshape(2, 128).T),
        wkdec=f32(L0_attn_W_kdec), wvdec=f32(L0_attn_W_vdec),
        wq=f32(L0_attn_W_q), wqdec=f32(L0_attn_W_qdec),
        wkrope=f32(L0_attn_W_krope), wqrope=f32(L0_attn_W_qrope),
        wo=f32(L0_attn_W_o),
        w1t=np.ascontiguousarray(np.asarray(L0_ffn_w1, np.float32).T),
        w2t=np.ascontiguousarray(np.asarray(L0_ffn_w2, np.float32).T),
        convw=np.ascontiguousarray(np.asarray(L0_ffn_conv_w, np.float32)
                                   .reshape(32, 128, 3).transpose(1, 0, 2).reshape(128, 96)),
        convb=np.ascontiguousarray(np.asarray(L0_ffn_conv_b, np.float32).reshape(32, 128).T),
        **consts)


def kernel(L0_attn_x, L0_attn_W_kv, L0_attn_g_kv, L0_attn_W_kdec, L0_attn_W_vdec,
           L0_attn_W_q, L0_attn_W_qdec, L0_attn_W_krope, L0_attn_W_qrope,
           L0_attn_W_o, L0_ffn_w1, L0_ffn_conv_w, L0_ffn_conv_b, L0_ffn_w2):
    x = np.ascontiguousarray(np.asarray(L0_attn_x, dtype=np.float32))
    B, TS, _ = x.shape
    nseq = B // NCORES
    nc = build_program(TS=TS, NSEQ=nseq)
    shared = _prep_inputs(L0_attn_W_kv, L0_attn_g_kv, L0_attn_W_kdec, L0_attn_W_vdec,
                          L0_attn_W_q, L0_attn_W_qdec, L0_attn_W_krope, L0_attn_W_qrope,
                          L0_attn_W_o, L0_ffn_w1, L0_ffn_conv_w, L0_ffn_conv_b,
                          L0_ffn_w2, TS)
    in_maps = []
    for c in range(NCORES):
        m = dict(shared)
        m["x"] = np.ascontiguousarray(x[c * nseq:(c + 1) * nseq])
        in_maps.append(m)
    res = run_bass_kernel_spmd(nc, in_maps, core_ids=list(range(NCORES)))
    out = np.concatenate([np.asarray(r["out"]) for r in res.results], axis=0)
    return out.astype(np.float32)


# version 04
# speedup vs baseline: 1.3360x; 1.3360x over previous
import contextlib
import numpy as np
import concourse.bass as bass
import concourse.mybir as mybir
from concourse.bass_utils import run_bass_kernel_spmd

F32 = mybir.dt.float32
BF16 = mybir.dt.bfloat16
I32 = mybir.dt.int32
OP = mybir.AluOpType
ACT = mybir.ActivationFunctionType
AX = mybir.AxisListType

D = 1024
HID = 4096
NH = 16
KVR = 256
QR = 512
EPS_LN = 1e-5
EPS_RMS = 1e-6
EPS_CLIP = 1e-5
NEG_BIG = -1.0e30
NCORES = 8
ENGS = ("sync", "gpsimd", "scalar", "vector", "tensor")


class SemC:
    __slots__ = ("h", "v")

    def __init__(self, h):
        self.h = h
        self.v = 0

    def inc(self, n=1):
        self.v += n
        return (self.h, self.v)

    def tok(self):
        return (self.h, self.v)


class DmaPool:

    def __init__(self, nc, es, name, n):
        self.sems = [SemC(es.enter_context(nc.semaphore(f"{name}{i}"))) for i in range(n)]
        self.rr = 0

    def issue(self, em, queue, dma_fn, pre_toks=()):
        s = self.sems[self.rr]
        self.rr = (self.rr + 1) % len(self.sems)
        prev = s.tok()

        def f(E, s=s, prev=prev, pre=tuple(pre_toks), dma_fn=dma_fn):
            if prev[1] > 0:
                E.wait_ge(prev[0], prev[1])
            for t in pre:
                if t is not None:
                    E.wait_ge(t[0], t[1])
            dma_fn(E).then_inc(s.h, 16)
        em(queue, f)
        return s.inc(16)

    def all_tok(self):
        return [s.tok() for s in self.sems if s.v > 0]


class Emit:
    def __init__(self, block):
        self.block = block
        self.q = {e: [] for e in ENGS}

    def __call__(self, eng, fn):
        self.q[eng].append(fn)

    def flush(self):
        for name in ENGS:
            fns = self.q[name]
            if not fns:
                continue

            def body(E, fns=fns):
                for f in fns:
                    f(E)

            getattr(self.block, name)(body)
        self.q = {e: [] for e in ENGS}


def W(E, *toks):
    for t in toks:
        if t is not None:
            E.wait_ge(t[0], t[1])


def build_program(TS=2048, NSEQ=2, do_attn=True, do_ffn2=True, debug=False, attn_upto=99, attn_heads=NH):
    assert TS % 512 == 0
    NT = TS // 128
    NSLAB = TS // 512
    NCH = TS // 512
    nc = bass.Bass("TRN2", target_bir_lowering=False)

    dx = nc.dram_tensor("x", [NSEQ, TS, D], F32, kind="ExternalInput").ap()
    dwkv = nc.dram_tensor("wkv", [D, KVR], F32, kind="ExternalInput").ap()
    dgkv = nc.dram_tensor("gkv", [128, 2], F32, kind="ExternalInput").ap()
    dwkdec = nc.dram_tensor("wkdec", [KVR, D], F32, kind="ExternalInput").ap()
    dwvdec = nc.dram_tensor("wvdec", [KVR, D], F32, kind="ExternalInput").ap()
    dwq = nc.dram_tensor("wq", [D, QR], F32, kind="ExternalInput").ap()
    dwqdec = nc.dram_tensor("wqdec", [QR, D], F32, kind="ExternalInput").ap()
    dwkrope = nc.dram_tensor("wkrope", [D, 512], F32, kind="ExternalInput").ap()
    dwqrope = nc.dram_tensor("wqrope", [QR, 512], F32, kind="ExternalInput").ap()
    dwo = nc.dram_tensor("wo", [D, D], F32, kind="ExternalInput").ap()
    dw1t = nc.dram_tensor("w1t", [D, HID], F32, kind="ExternalInput").ap()
    dw2t = nc.dram_tensor("w2t", [HID, D], F32, kind="ExternalInput").ap()
    dconvw = nc.dram_tensor("convw", [128, 96], F32, kind="ExternalInput").ap()
    dconvb = nc.dram_tensor("convb", [128, 32], F32, kind="ExternalInput").ap()
    didb = nc.dram_tensor("identb", [128, 128], BF16, kind="ExternalInput").ap()
    didf = nc.dram_tensor("identf", [128, 128], F32, kind="ExternalInput").ap()
    donesf = nc.dram_tensor("onesf", [128, 128], F32, kind="ExternalInput").ap()
    dseed = nc.dram_tensor("seed", [128, 16], F32, kind="ExternalInput").ap()
    dcos = nc.dram_tensor("cosfm", [128, TS], F32, kind="ExternalInput").ap()
    dssin = nc.dram_tensor("ssinfm", [128, TS], F32, kind="ExternalInput").ap()
    donesrow = nc.dram_tensor("onesrow", [1, TS], BF16, kind="ExternalInput").ap()
    dminm = nc.dram_tensor("minmask", [4, 128, 512], BF16, kind="ExternalInput").ap()
    dx1 = nc.dram_tensor("x1s", [NSEQ, TS, D], F32, kind="Internal").ap()
    dx2 = nc.dram_tensor("x2s", [NSEQ, TS, D], F32, kind="Internal").ap()
    dw1q = nc.dram_tensor("w1qs", [4, 128, 8192], BF16, kind="Internal").ap()
    dw2q = nc.dram_tensor("w2qs", [HID, D], BF16, kind="Internal").ap()
    dout = nc.dram_tensor("out", [NSEQ, TS, D], F32, kind="ExternalOutput").ap()
    ddbg = {}
    if debug:
        for nm, shp, dt in [("kons", [128, 1], F32), ("alf", [128, 64], F32),
                            ("bet", [128, 64], F32), ("bia", [128, 64], F32),
                            ("nT0", [128, 514], F32), ("apre", [128, 16384], F32),
                            ("apost", [128, 16384], F32), ("s2t", [128, 4], F32),
                            ("g2t", [128, 4], F32), ("bl2e0", [128, 512], F32),
                            ("amx", [128, 512], F32)]:
            ddbg[nm] = nc.dram_tensor("dbg_" + nm, shp, dt, kind="ExternalOutput").ap()
        ddbg["stw"] = nc.dram_tensor("dbg_stw", [128, 8 * (TS // 128)], F32, kind="ExternalOutput").ap()
        ddbg["bns"] = nc.dram_tensor("dbg_bns", [128, 12 * (TS // 128)], F32, kind="ExternalOutput").ap()
        ddbg["smax"] = nc.dram_tensor("dbg_smax", [128, 64], F32, kind="ExternalOutput").ap()
        ddbg["hp0"] = nc.dram_tensor("dbg_hp0", [128, 514], F32, kind="ExternalOutput").ap()
        ddbg["w1s0"] = nc.dram_tensor("dbg_w1s0", [128, 8192], F32, kind="ExternalOutput").ap()
        ddbg["cv0"] = nc.dram_tensor("dbg_cv0", [128, 512], F32, kind="ExternalOutput").ap()
        ddbg["smin"] = nc.dram_tensor("dbg_smin", [128, 64], F32, kind="ExternalOutput").ap()

    es = contextlib.ExitStack()
    TT = SemC(es.enter_context(nc.semaphore("sTT")))
    VS = SemC(es.enter_context(nc.semaphore("sVS")))
    SZ = SemC(es.enter_context(nc.semaphore("sSZ")))
    VV = SemC(es.enter_context(nc.semaphore("sVV")))
    SS = SemC(es.enter_context(nc.semaphore("sSS")))
    LP = DmaPool(nc, es, "sL", 8)
    OP_ = DmaPool(nc, es, "sO", 4)

    block = es.enter_context(nc.Block())
    em = Emit(block)

    def mkvs(n):
        base = VS.v
        VS.v += n

        def factory(E):
            cnt = [0]

            def S(ins):
                cnt[0] += 1
                assert cnt[0] <= n
                ins.then_inc(VS.h, 1)
                E.wait_ge(VS.h, base + cnt[0])
            return S
        return factory

    def mkss(n):
        base = SZ.v
        SZ.v += n

        def factory(E):
            cnt = [0]

            def S(ins):
                cnt[0] += 1
                assert cnt[0] <= n
                ins.then_inc(SZ.h, 1)
                E.wait_ge(SZ.h, base + cnt[0])
            return S
        return factory

    identb = es.enter_context(nc.sbuf_tensor("sb_identb", [128, 128], BF16))
    identf = es.enter_context(nc.sbuf_tensor("sb_identf", [128, 128], F32))
    onesf = es.enter_context(nc.sbuf_tensor("sb_onesf", [128, 128], F32))
    seed = es.enter_context(nc.sbuf_tensor("sb_seed", [128, 16], F32))
    kons = es.enter_context(nc.sbuf_tensor("kons", [128, 8], F32))
    mw1 = es.enter_context(nc.sbuf_tensor("mw1", [128, 1], F32))
    rw1 = es.enter_context(nc.sbuf_tensor("rw1", [128, 1], F32))
    mw2 = es.enter_context(nc.sbuf_tensor("mw2", [128, 1], F32))
    rw2 = es.enter_context(nc.sbuf_tensor("rw2", [128, 1], F32))
    mw2h = es.enter_context(nc.sbuf_tensor("mw2h", [128, 1], F32))
    convwt = es.enter_context(nc.sbuf_tensor("sb_convwt", [128, 96], F32))
    convbt = es.enter_context(nc.sbuf_tensor("sb_convbt", [128, 32], F32))
    smallp = es.enter_context(nc.sbuf_tensor("smallp", [128, 8], F32))


    def emit_p0():
        tok_c = []
        for dst_t, src_a in ((identb, didb), (identf, didf), (onesf, donesf),
                             (seed, dseed), (convwt, dconvw), (convbt, dconvb)):
            tok_c.append(LP.issue(em, "sync", lambda E, d=dst_t, s=src_a: E.dma_start(out=d[:], in_=s)))

        stp = mkvs(4)

        def f(E, toks=tuple(tok_c), stp=stp):
            W(E, *toks)
            S = stp(E)
            S(E.tensor_scalar(smi[:], onesf[:, 0:128], 1.5, None, OP.mult))
            S(E.tensor_scalar(smib[:], smi[:], 0.0, None, OP.add))
            S(E.tensor_scalar(smallp[:, 0:8], smib[:, 0:8], 0.0, None, OP.add))
            S(E.tensor_scalar(kons[:], smallp[:, 0:8], -0.5, 129.0, OP.mult, OP.add))
            E.engine_nop().then_inc(VV.h, 1)
        em("vector", f)
        tok_kons = VV.inc()

        ps = contextlib.ExitStack()
        smi = ps.enter_context(nc.sbuf_tensor("smi", [128, 128], I32))
        smib = ps.enter_context(nc.sbuf_tensor("smib", [128, 128], BF16))
        wst = [ps.enter_context(nc.sbuf_tensor(f"p0w{i}", [128, 4096], F32)) for i in range(2)]
        wqi = ps.enter_context(nc.sbuf_tensor("p0qi", [128, 4096], I32))
        wqf = ps.enter_context(nc.sbuf_tensor("p0qf", [128, 4096], BF16))
        wqb = ps.enter_context(nc.sbuf_tensor("p0qb", [128, 4096], BF16))
        macc = ps.enter_context(nc.sbuf_tensor("p0macc", [128, 1], F32))
        red = ps.enter_context(nc.sbuf_tensor("p0red", [128, 1], F32))
        pms = ps.enter_context(nc.psum_tensor("p0ps", [128, 512], F32))

        def quant_weight(dsrc, nchunk, chwid, nelem, mean_t, recip_t, spill_fn,
                         multi_spill=False, init_vt=(None, None), init_sp=()):
            src3 = dsrc.rearrange("(c p) w -> c p w", p=128)
            vt = [init_vt[0], init_vt[1]]
            for c in range(nchunk):
                b = wst[c % 2]
                tok_l = LP.issue(em, "sync",
                                 lambda E, b=b, c=c: E.dma_start(out=b[:, 0:chwid], in_=src3[c]),
                                 pre_toks=(vt[c % 2],))

                stp = mkvs(1 if c == 0 else 2)

                def f(E, c=c, b=b, tok=tok_l, stp=stp):
                    W(E, tok)
                    S = stp(E)
                    if c == 0:
                        S(E.reduce_sum(macc[:], b[:, 0:chwid], AX.X,
                                       apply_absolute_value=True))
                    else:
                        S(E.reduce_sum(red[:], b[:, 0:chwid], AX.X, apply_absolute_value=True))
                        S(E.tensor_tensor(macc[:], macc[:], red[:], OP.add))
                    E.engine_nop().then_inc(VV.h, 1)
                em("vector", f)
                vt[c % 2] = VV.inc()
            tok_acc = vt[(nchunk - 1) % 2]

            def f(E, tok=tok_acc):
                W(E, tok)
                E.matmul(pms[0:1, 0:1], onesf[:, 0:1], macc[:], start=True,
                         stop=True).then_inc(TT.h, 1)
            em("tensor", f)
            tok_mm = TT.inc()

            def f(E, tok=tok_mm):
                W(E, tok)
                E.tensor_scalar(smallp[0:1, 0:1], pms[0:1, 0:1], 0.0, None, OP.add).then_inc(VV.h, 1)
            em("vector", f)
            tok_ev = VV.inc()

            def f(E, tok=tok_ev):
                W(E, tok)
                E.matmul(pms[:, 0:1], onesf[0:1, :], smallp[0:1, 0:1], start=True,
                         stop=True).then_inc(TT.h, 1)
            em("tensor", f)
            tok_bc = TT.inc()

            stp = mkvs(2)

            def f(E, tok=tok_bc, mean_t=mean_t, recip_t=recip_t, nelem=nelem, stp=stp):
                W(E, tok)
                S = stp(E)
                S(E.tensor_scalar(mean_t[:], pms[:, 0:1], 1.0 / nelem, EPS_CLIP, OP.mult, OP.max))
                S(E.reciprocal(recip_t[:], mean_t[:]))
                E.engine_nop().then_inc(VV.h, 1)
            em("vector", f)
            tok_scale = VV.inc()

            sp_prev = list(init_sp)
            vt2 = [tok_acc, tok_acc]
            for c in range(nchunk):
                b = wst[c % 2]
                tok_l = LP.issue(em, "sync",
                                 lambda E, b=b, c=c: E.dma_start(out=b[:, 0:chwid], in_=src3[c]),
                                 pre_toks=(vt2[c % 2],))

                stpq = mkvs(3)

                def f(E, c=c, b=b, tok=tok_l, tok3=tuple(sp_prev), recip_t=recip_t, stpq=stpq):
                    W(E, tok, tok_scale, tok_kons, *tok3)
                    S = stpq(E)
                    S(E.tensor_scalar(wqi[:, 0:chwid], b[:, 0:chwid], recip_t[:, 0:1],
                                      kons[:, 0:1], OP.mult, OP.add))
                    S(E.tensor_scalar(wqf[:, 0:chwid], wqi[:, 0:chwid], -128.0, None, OP.add))
                    S(E.tensor_scalar(wqf[:, 0:chwid], wqf[:, 0:chwid], 1.0, None, OP.min))
                    E.tensor_scalar(wqb[:, 0:chwid], wqf[:, 0:chwid], -1.0, None,
                                    OP.max).then_inc(VV.h, 1)
                em("vector", f)
                vt2[c % 2] = VV.inc()
                sp_prev = []
                if multi_spill:
                    for g in range(4):
                        sp_prev.append(OP_.issue(
                            em, "gpsimd",
                            lambda E, c=c, g=g: E.dma_start(
                                out=dw1q[g, :, 1024 * c:1024 * (c + 1)],
                                in_=wqb[:, 1024 * g:1024 * (g + 1)]),
                            pre_toks=(vt2[c % 2],)))
                else:
                    sp_prev.append(OP_.issue(
                        em, "gpsimd",
                        lambda E, c=c: E.dma_start(out=spill_fn(c), in_=wqb[:, 0:chwid]),
                        pre_toks=(vt2[c % 2],)))
            return tok_scale, (tuple(vt2), tuple(sp_prev))

        tok_w1scale, (w1_vt, w1_sp) = quant_weight(
            dw1t, 8, 4096, D * HID, mw1, rw1, None, multi_spill=True)
        tok_w1sp = tuple(OP_.all_tok())
        tok_w2scale, _ = quant_weight(
            dw2t, 32, 1024, D * HID, mw2, rw2,
            lambda c: dw2q[c * 128:(c + 1) * 128, :],
            init_vt=w1_vt, init_sp=w1_sp)
        tok_w2sp = tuple(OP_.all_tok())

        stp = mkvs(2)

        def f(E, stp=stp, tokc=tuple(tok_c)):
            W(E, tok_w2scale, tok_w1scale, *tokc)
            S = stp(E)
            S(E.tensor_scalar(convwt[:], convwt[:], mw1[:, 0:1], None, OP.mult))
            S(E.tensor_scalar(mw2h[:], mw2[:], 0.5 / 127.0, None, OP.mult))
            E.engine_nop().then_inc(VV.h, 1)
        em("vector", f)
        tok_taps = VV.inc()
        em.flush()
        ps.close()
        return tok_taps, tok_w1sp, tok_w2sp, tok_kons

    tok_taps, tok_w1sp, tok_w2sp, tok_kons = emit_p0()

    def fence():
        toks = [TT.tok(), VV.tok(), SS.tok()] + LP.all_tok() + OP_.all_tok()
        for e in ENGS:
            def f(E, toks=tuple(toks)):
                for t in toks:
                    if t[1] > 0:
                        E.wait_ge(t[0], t[1])
            em(e, f)
        em.flush()

    def emit_ffn(src, dst, tag):
        ps = contextlib.ExitStack()
        w2q = [ps.enter_context(nc.sbuf_tensor(f"F{tag}w2q{c}", [128, D], BF16)) for c in range(32)]
        w1s = [ps.enter_context(nc.sbuf_tensor(f"F{tag}w1s{g}", [128, 8192], BF16)) for g in range(2)]
        nT = [[ps.enter_context(nc.sbuf_tensor(f"F{tag}nT{b}_{c}", [128, 514], BF16))
               for c in range(8)] for b in range(2)]
        aAr = ps.enter_context(nc.sbuf_tensor(f"F{tag}aA", [128, 32 * 512], BF16))
        xsl = [ps.enter_context(nc.sbuf_tensor(f"F{tag}xsl{i}", [128, 2048], F32)) for i in range(2)]
        xrr = [ps.enter_context(nc.sbuf_tensor(f"F{tag}xrr{i}", [128, 1024], F32)) for i in range(4)]
        nI = [ps.enter_context(nc.sbuf_tensor(f"F{tag}nI0", [128, 1024], I32))] * 2
        nB = [ps.enter_context(nc.sbuf_tensor(f"F{tag}nB{i}", [128, 1024], BF16)) for i in range(2)]
        csc = [ps.enter_context(nc.sbuf_tensor(f"F{tag}csc{i}", [128, 512], F32)) for i in range(4)]
        nis = ps.enter_context(nc.sbuf_tensor(f"F{tag}nis", [128, 512], I32))
        amx = ps.enter_context(nc.sbuf_tensor(f"F{tag}amx", [128, 512], F32))
        s2r = ps.enter_context(nc.sbuf_tensor(f"F{tag}s2r", [128, 128], F32))
        bl2e = [ps.enter_context(nc.sbuf_tensor(f"F{tag}bl2e{i}", [128, 512], F32)) for i in range(2)]
        bns = ps.enter_context(nc.sbuf_tensor(f"F{tag}bns", [128, 12 * NT], F32))
        smax = ps.enter_context(nc.sbuf_tensor(f"F{tag}smax", [128, NT], F32))
        smin = ps.enter_context(nc.sbuf_tensor(f"F{tag}smin", [128, NT], F32))
        stw = ps.enter_context(nc.sbuf_tensor(f"F{tag}stw", [128, 8 * NT], F32))
        alf = ps.enter_context(nc.sbuf_tensor(f"F{tag}alf", [128, NT], F32))
        bet = ps.enter_context(nc.sbuf_tensor(f"F{tag}bet", [128, NT], F32))
        bia = ps.enter_context(nc.sbuf_tensor(f"F{tag}bia", [128, NT], F32))
        s2t = ps.enter_context(nc.sbuf_tensor(f"F{tag}s2t", [128, 4], F32))
        g2t = ps.enter_context(nc.sbuf_tensor(f"F{tag}g2t", [128, 4], F32))

        hpsA = ps.enter_context(nc.psum_tensor(f"F{tag}hA", [128, 1024], F32))
        hpsB = ps.enter_context(nc.psum_tensor(f"F{tag}hB", [128, 1024], F32))
        ntp = ps.enter_context(nc.psum_tensor(f"F{tag}ntp", [128, 1024], BF16))
        bl2p = ps.enter_context(nc.psum_tensor(f"F{tag}bl2p", [128, 1024], F32))
        pmis = ps.enter_context(nc.psum_tensor(f"F{tag}pmis", [128, 512], F32))
        hps = [hpsA, hpsB]

        x3 = src.rearrange("s (k p) d -> s k p d", p=128)
        y3 = dst.rearrange("s (k p) d -> s k p d", p=128)

        tok_w2l = []
        for c in range(32):
            tok_w2l.append(LP.issue(
                em, "sync",
                lambda E, c=c: E.dma_start(out=w2q[c][:], in_=dw2q[c * 128:(c + 1) * 128, :]),
                pre_toks=tok_w2sp))
        toks_w2ld = tuple(tok_w2l)

        st = {
            "xsl_free": [None, None],
            "nIB_free": [None, None],
            "nT_bl1_done": [None, None],
            "halo_done": {},
            "xside_done": {},
            "conv": [],
            "silu": [],
            "quant_done": None,
            "bl2_done": None,
            "last_stt": None,
            "amx_T": None,
            "w1_used": [None, None],
            "xrr_tok": [None, None, None, None],
            "xout_done": [None, None, None, None],
        }

        def emit_stats(s):
            vt = [None, None]
            for k in range(NT):
                b = k % 2
                tok_l = LP.issue(em, "sync",
                                 lambda E, s=s, k=k, b=b: E.dma_start(out=xsl[b][:, 0:1024], in_=x3[s, k]),
                                 pre_toks=(vt[b], st["xsl_free"][b]))

                def f(E, k=k, b=b, tok=tok_l):
                    W(E, tok)
                    E.bn_stats(bns[:, 12 * k:12 * k + 6], xsl[b][:, 0:512])
                    E.bn_stats(bns[:, 12 * k + 6:12 * k + 12], xsl[b][:, 512:1024])
                    E.reduce_max(smax[:, k:k + 1], xsl[b][:, 0:1024], AX.X)
                    E.tensor_reduce(smin[:, k:k + 1], xsl[b][:, 0:1024], AX.X, OP.min).then_inc(VV.h, 1)
                em("vector", f)
                vt[b] = VV.inc()
            st["xsl_free"] = [vt[0], vt[1]]
            tok_stats = vt[(NT - 1) % 2]

            stp = mkvs(12)

            def f(E, tok=tok_stats, stp=stp):
                W(E, tok, tok_kons)
                S = stp(E)
                me = bns[:, 1::3]
                m2 = bns[:, 2::3]
                R0, R1, R2 = stw[:, 0:2 * NT], stw[:, 2 * NT:3 * NT], stw[:, 3 * NT:4 * NT]
                R3, R4, R5 = stw[:, 4 * NT:5 * NT], stw[:, 5 * NT:6 * NT], stw[:, 6 * NT:7 * NT]
                S(E.tensor_tensor(R0, me[:, 0::2], me[:, 1::2], OP.add))
                S(E.tensor_tensor(R1, R0[:, 0::2], R0[:, 1::2], OP.add))
                S(E.tensor_tensor(R0, m2[:, 0::2], m2[:, 1::2], OP.add))
                S(E.tensor_tensor(R2, R0[:, 0::2], R0[:, 1::2], OP.add))
                S(E.tensor_tensor(stw[:, 4 * NT:8 * NT], me, me, OP.mult))
                S(E.tensor_tensor(R0, stw[:, 4 * NT:8 * NT][:, 0::2],
                                  stw[:, 4 * NT:8 * NT][:, 1::2], OP.add))
                S(E.tensor_tensor(R3, R0[:, 0::2], R0[:, 1::2], OP.add))
                S(E.tensor_scalar(R4, R1, 0.25, None, OP.mult))
                S(E.tensor_scalar(R5, R2, 1.0 / 1024.0, None, OP.mult))
                S(E.scalar_tensor_tensor(R5, R3, 0.25, R5, OP.mult, OP.add))
                S(E.tensor_tensor(R0[:, 0:NT], R4, R4, OP.mult))
                S(E.scalar_tensor_tensor(R5, R0[:, 0:NT], -1.0, R5, OP.mult, OP.add))
                E.engine_nop().then_inc(VV.h, 1)
            em("vector", f)
            tok_var = VV.inc()
            sstp = mkss(1)

            def f(E, tok=tok_var, sstp=sstp):
                W(E, tok)
                S = sstp(E)
                S(E.activation(stw[:, 0:NT], stw[:, 6 * NT:7 * NT], ACT.Ln,
                               bias=seed[:, 8:9], scale=1.0))
                E.activation(stw[:, 7 * NT:8 * NT], stw[:, 0:NT], ACT.Exp,
                             bias=0.0, scale=-0.5).then_inc(SS.h, 1)
            em("scalar", f)
            tok_rln = SS.inc()
            stp2 = mkvs(10)

            def f(E, tok=tok_rln, stp=stp2):
                W(E, tok)
                S = stp(E)
                mean = stw[:, 5 * NT:6 * NT]
                rln = stw[:, 7 * NT:8 * NT]
                t0 = stw[:, 0:NT]
                t1 = stw[:, NT:2 * NT]
                t2 = stw[:, 2 * NT:3 * NT]
                S(E.tensor_tensor(t0, smax[:], mean, OP.subtract))
                S(E.tensor_tensor(t1, mean, smin[:], OP.subtract))
                S(E.tensor_tensor(t0, t0, t1, OP.max))
                S(E.tensor_tensor(t1, t0, rln, OP.mult))
                S(E.tensor_scalar(t1, t1, EPS_CLIP, None, OP.max))
                S(E.reciprocal(t2, t1))
                S(E.scalar_tensor_tensor(alf[:], rln, 127.0, t2, OP.mult, OP.mult))
                S(E.tensor_scalar(bet[:], t1, 1.0 / 127.0, None, OP.mult))
                S(E.tensor_tensor(t0, alf[:], mean, OP.mult))
                S(E.tensor_scalar(bia[:], t0, -1.0, kons[:, 0:1], OP.mult, OP.add))
                E.engine_nop().then_inc(VV.h, 1)
            em("vector", f)
            return VV.inc()

        tok_ab = {}

        def emit_xside(u):
            s, b = divmod(u, NSLAB)
            bb = u % 2
            toks_half = [None, None]
            for h2 in range(2):
                t0 = LP.issue(
                    em, "sync",
                    lambda E, s=s, b=b, h2=h2: E.dma_start(
                        out=xsl[h2][:, 0:1024], in_=x3[s, 4 * b + 2 * h2]),
                    pre_toks=(st["xsl_free"][h2],))
                t1 = LP.issue(
                    em, "sync",
                    lambda E, s=s, b=b, h2=h2: E.dma_start(
                        out=xsl[h2][:, 1024:2048], in_=x3[s, 4 * b + 2 * h2 + 1]))
                toks_half[h2] = (t0, t1)
            last_e = None
            for j in range(4):
                k = 4 * b + j
                jb = j % 2

                def f(E, k=k, j=j, jb=jb, toks=toks_half[j // 2],
                      tok2=tok_ab[s], tokf=st["nIB_free"][jb], tokf2=st["nIB_free"][1 - jb]):
                    W(E, toks[0], toks[1], tok2, tokf, tokf2)
                    E.activation(nI[jb][:], xsl[j // 2][:, 1024 * (j % 2):1024 * (j % 2 + 1)],
                                 ACT.Identity, bias=bia[:, k:k + 1],
                                 scale=alf[:, k:k + 1]).then_inc(SS.h, 1)
                em("scalar", f)
                tok_s = SS.inc()

                stpn = mkvs(1)

                def f(E, k=k, jb=jb, tok=tok_s, stpn=stpn):
                    W(E, tok)
                    S = stpn(E)
                    S(E.tensor_scalar(nB[jb][:], nI[jb][:], -128.0, None, OP.add))
                    E.tensor_scalar(nB[jb][:], nB[jb][:], bet[:, k:k + 1], None,
                                    OP.mult).then_inc(VV.h, 1)
                em("vector", f)
                tok_v = VV.inc()

                def f(E, jb=jb, tok=tok_v, tokp=last_e):
                    W(E, tok, tokp)
                    ins = None
                    for c in range(8):
                        ins = E.transpose(ntp[:, 128 * c:128 * (c + 1)],
                                          nB[jb][:, 128 * c:128 * (c + 1)], identb[:])
                    ins.then_inc(TT.h, 1)
                em("tensor", f)
                tok_t = TT.inc()

                def f(E, b=b, bb=bb, j=j, tok=tok_t, tokbl1=st["nT_bl1_done"][bb]):
                    W(E, tok, tokbl1)
                    ins = None
                    for c in range(8):
                        ins = E.tensor_scalar(nT[bb][c][:, 1 + 128 * j:1 + 128 * (j + 1)],
                                              ntp[:, 128 * c:128 * (c + 1)], 0.0, None, OP.add)
                    if j == 0:
                        if b > 0:
                            for c in range(8):
                                E.tensor_scalar(nT[1 - bb][c][:, 513:514],
                                                ntp[:, 128 * c:128 * c + 1], 0.0, None, OP.add)
                            for c in range(8):
                                ins = E.tensor_scalar(nT[bb][c][:, 0:1],
                                                      nT[1 - bb][c][:, 512:513], 0.0, None, OP.add)
                        else:
                            for c in range(8):
                                ins = E.memset(nT[bb][c][:, 0:1], 0.0)
                    if j == 3 and b == NSLAB - 1:
                        for c in range(8):
                            ins = E.memset(nT[bb][c][:, 513:514], 0.0)
                    ins.then_inc(VV.h, 1)
                em("vector", f)
                last_e = VV.inc()
                st["nIB_free"][jb] = last_e
                if j == 0 and b > 0:
                    st["halo_done"][u - 1] = last_e
                if j == 1:
                    st["xsl_free"][0] = last_e
                if j == 3:
                    st["xsl_free"][1] = last_e
            st["xside_done"][u] = last_e
            if b == NSLAB - 1:
                st["halo_done"][u] = last_e

        def emit_slab(u):
            s, b = divmod(u, NSLAB)
            bb = u % 2
            tok_w1g = [None] * 4
            for g in range(2):
                tok_w1g[g] = LP.issue(
                    em, "sync",
                    lambda E, g=g: E.dma_start(out=w1s[g % 2][:], in_=dw1q[g]),
                    pre_toks=(st["w1_used"][g % 2],) + tok_w1sp)

            if debug and u == 0 and src is dx:
                OP_.issue(em, "gpsimd", lambda E: E.dma_start(out=ddbg["w1s0"], in_=w1s[0][:]),
                          pre_toks=(tok_w1g[0],))
            nm0 = len(st["conv"])
            tok_am = None
            for m in range(32):
                g = m // 8
                gb = g % 2
                mc = m % 8
                hp = hps[m % 2]
                gi = nm0 + m
                tok_hp_free = st["conv"][gi - 2] if gi >= 2 else None
                if mc == 0 and g >= 2:
                    tok_w1g[g] = LP.issue(
                        em, "sync",
                        lambda E, g=g: E.dma_start(out=w1s[g % 2][:], in_=dw1q[g]),
                        pre_toks=(st["w1_used"][g % 2],) + tok_w1sp)

                def f(E, bb=bb, m=m, gb=gb, mc=mc, hp=hp, tokn=st["xside_done"][u],
                      tokh=st["halo_done"].get(u), tokw=tok_w1g[g], tokf=tok_hp_free):
                    W(E, tokn, tokh, tokw, tokf)
                    ins = None
                    for kf in range(8):
                        lhs = w1s[gb][:, 1024 * kf + 128 * mc:1024 * kf + 128 * (mc + 1) - 0]
                        E.matmul(hp[:, 0:512], lhs, nT[bb][kf][:, 0:512],
                                 start=(kf == 0), stop=(kf == 7))
                        ins = E.matmul(hp[:, 512:514], lhs, nT[bb][kf][:, 512:514],
                                       start=(kf == 0), stop=(kf == 7))
                    ins.then_inc(TT.h, 1)
                em("tensor", f)
                tok_bl1m = TT.inc()
                if mc == 7:
                    st["w1_used"][gb] = tok_bl1m
                if m == 31:
                    st["nT_bl1_done"][bb] = tok_bl1m

                c0 = csc[2 * (m % 2)]
                c1 = csc[2 * (m % 2) + 1]

                stpc = mkvs(2)

                def f(E, m=m, hp=hp, c0=c0, c1=c1, tok=tok_bl1m, stpc=stpc,
                      tokf=st["silu"][gi - 2] if gi >= 2 else None):
                    W(E, tok, tok_taps, tokf)
                    S = stpc(E)
                    S(E.tensor_scalar(c0[:], hp[:, 1:513], convwt[:, 3 * m + 1:3 * m + 2],
                                      convbt[:, m:m + 1], OP.mult, OP.add))
                    S(E.scalar_tensor_tensor(c1[:], hp[:, 0:512], convwt[:, 3 * m:3 * m + 1],
                                             c0[:], OP.mult, OP.add))
                    E.scalar_tensor_tensor(c0[:], hp[:, 2:514], convwt[:, 3 * m + 2:3 * m + 3],
                                           c1[:], OP.mult, OP.add).then_inc(VV.h, 1)
                em("vector", f)
                tok_conv = VV.inc()
                st["conv"].append(tok_conv)
                if debug and u == 0 and m == 0 and src is dx:
                    def fdbg(E, hp=hp, c0=c0, tok=tok_bl1m, tok2=tok_conv):
                        W(E, tok, tok2)
                        E.tensor_scalar(bl2e[0][:, 0:512], hp[:, 0:512], 0.0, None, OP.add)
                        E.tensor_scalar(bl2e[1][:, 0:2], hp[:, 512:514], 0.0, None, OP.add)
                        E.tensor_scalar(bl2e[1][:, 2:512], c0[:, 0:510], 0.0, None, OP.add).then_inc(VV.h, 1)
                    em("vector", fdbg)
                    tok_d = VV.inc()
                    OP_.issue(em, "gpsimd", lambda E: E.dma_start(out=ddbg["hp0"][:, 0:512], in_=bl2e[0][:, 0:512]),
                              pre_toks=(tok_d,))
                    OP_.issue(em, "gpsimd", lambda E: E.dma_start(out=ddbg["hp0"][:, 512:514], in_=bl2e[1][:, 0:2]),
                              pre_toks=(tok_d,))
                    tok_dd = OP_.issue(em, "gpsimd", lambda E: E.dma_start(out=ddbg["cv0"][:, 0:510], in_=bl2e[1][:, 2:512]),
                              pre_toks=(tok_d,))
                    st["bl2_done"] = tok_dd
                    st["conv"][-1] = tok_d

                def f(E, m=m, c0=c0, tok=tok_conv, tokq=st["quant_done"]):
                    W(E, tok, tokq)
                    E.activation(aAr[:, 512 * m:512 * (m + 1)], c0[:], ACT.Silu).then_inc(SS.h, 1)
                em("scalar", f)
                tok_sl = SS.inc()
                st["silu"].append(tok_sl)

            tok_amax = tok_sl

            def f(E, tok=tok_amax, toka=st["amx_T"]):
                W(E, tok, toka)
                E.tensor_reduce(amx[:], aAr[:].rearrange("p (c w) -> p w c", c=32),
                                AX.X, OP.max, apply_absolute_value=True).then_inc(VV.h, 1)
            em("vector", f)
            tok_amax = VV.inc()

            if debug and u == 0 and src is dx:
                OP_.issue(em, "gpsimd", lambda E: E.dma_start(out=ddbg["nT0"], in_=nT[0][0][:]),
                          pre_toks=(st["xside_done"][0], st["halo_done"].get(0)))
                OP_.issue(em, "gpsimd", lambda E: E.dma_start(out=ddbg["apre"], in_=aAr[:]),
                          pre_toks=(tok_amax,))

            def f(E, tok=tok_amax, tokp=st["last_stt"]):
                W(E, tok, tokp)
                ins = None
                for j in range(4):
                    ins = E.transpose(pmis[:, 128 * j:128 * (j + 1)],
                                      amx[:, 128 * j:128 * (j + 1)], identf[:])
                ins.then_inc(TT.h, 1)
            em("tensor", f)
            st["amx_T"] = TT.inc()
            tok_amT = st["amx_T"]

            stp_s2 = mkvs(5)

            def f(E, tok=tok_amT, stp=stp_s2):
                W(E, tok)
                S = stp(E)
                for j in range(4):
                    ins = E.reduce_max(s2t[:, j:j + 1], pmis[:, 128 * j:128 * (j + 1)], AX.X)
                S(ins)
                S(E.tensor_scalar(s2t[:], s2t[:], EPS_CLIP, None, OP.max))
                S(E.tensor_scalar(g2t[:], s2t[:], mw2h[:, 0:1], None, OP.mult))
                S(E.reciprocal(s2t[:], s2t[:]))
                S(E.tensor_scalar(s2t[:], s2t[:], 127.0, None, OP.mult))
                E.engine_nop().then_inc(VV.h, 1)
            em("vector", f)
            tok_s2 = VV.inc()

            tok_rank1 = None
            for j in range(4):
                def f(E, j=j, tok=tok_s2, tokp=tok_rank1):
                    W(E, tok, tokp)
                    E.transpose(pmis[0:1, 128 * j:128 * (j + 1)], s2t[:, j:j + 1],
                                identf[:]).then_inc(TT.h, 1)
                em("tensor", f)
                tok_jT = TT.inc()

                def f(E, j=j, tok=tok_jT):
                    W(E, tok)
                    E.tensor_scalar(s2r[0:1, 0:128], pmis[0:1, 128 * j:128 * (j + 1)],
                                    0.0, None, OP.add).then_inc(VV.h, 1)
                em("vector", f)
                tok_row = VV.inc()

                def f(E, j=j, tok=tok_row):
                    W(E, tok)
                    E.matmul(pmis[:, 128 * j:128 * (j + 1)], onesf[0:1, :],
                             s2r[0:1, 0:128], start=True, stop=True).then_inc(TT.h, 1)
                em("tensor", f)
                tok_rank1 = TT.inc()
            tok_s2b = tok_rank1

            tok_q = None
            for m in range(32):
                stpz = mkvs(2)

                def f(E, m=m, tok=tok_s2b, stpz=stpz):
                    W(E, tok)
                    S = stpz(E)
                    acol = aAr[:, 512 * m:512 * (m + 1)]
                    S(E.tensor_tensor(csc[0][:], acol, pmis[:, 0:512], OP.mult))
                    S(E.tensor_scalar(nis[:], csc[0][:], kons[:, 0:1], None, OP.add))
                    E.tensor_scalar(acol, nis[:], -128.0, None, OP.add).then_inc(VV.h, 1)
                em("vector", f)
                tok_q = VV.inc()
            st["quant_done"] = tok_q
            if debug and u == 0 and src is dx:
                OP_.issue(em, "gpsimd", lambda E: E.dma_start(out=ddbg["amx"], in_=amx[:]),
                          pre_toks=(tok_amax,))
                OP_.issue(em, "gpsimd", lambda E: E.dma_start(out=ddbg["s2t"], in_=s2t[:]),
                          pre_toks=(tok_q,))
                OP_.issue(em, "gpsimd", lambda E: E.dma_start(out=ddbg["g2t"], in_=g2t[:]),
                          pre_toks=(tok_q,))
                OP_.issue(em, "gpsimd", lambda E: E.dma_start(out=ddbg["apost"], in_=aAr[:]),
                          pre_toks=(tok_q,))

            for j in range(4):
                st["xrr_tok"][j] = LP.issue(
                    em, "sync",
                    lambda E, s=s, b=b, j=j: E.dma_start(out=xrr[j][:], in_=x3[s, 4 * b + j]),
                    pre_toks=(st["xout_done"][j],))
            toks_xr = tuple(st["xrr_tok"])

            tok_prev_pair_ev = None
            tok_stt_last = st["last_stt"]
            for p in range(4):
                def f(E, p=p, tok=tok_q, toke=tok_prev_pair_ev,
                      tokb=st["bl2_done"] if p == 0 else None):
                    W(E, tok, toke, tokb, *toks_w2ld)
                    ins = None
                    for half in range(2):
                        d = 2 * p + half
                        for c in range(32):
                            ins = E.matmul(bl2p[:, 512 * half:512 * (half + 1)],
                                           w2q[c][:, 128 * d:128 * (d + 1)],
                                           aAr[:, 512 * c:512 * (c + 1)],
                                           start=(c == 0), stop=(c == 31))
                    ins.then_inc(TT.h, 1)
                em("tensor", f)
                tok_mm = TT.inc()

                def f(E, tok=tok_mm):
                    W(E, tok)
                    E.tensor_scalar(bl2e[0][:], bl2p[:, 0:512], 0.0, None, OP.add)
                    E.tensor_scalar(bl2e[1][:], bl2p[:, 512:1024], 0.0, None, OP.add).then_inc(VV.h, 1)
                em("vector", f)
                tok_ev = VV.inc()
                tok_prev_pair_ev = tok_ev
                if debug and u == 0 and p == 0 and src is dx:
                    OP_.issue(em, "gpsimd", lambda E: E.dma_start(out=ddbg["bl2e0"], in_=bl2e[0][:]),
                              pre_toks=(tok_ev,))

                for half in range(2):
                    d = 2 * p + half

                    def f(E, half=half, tok=tok_ev, tokq2=tok_q, tokst=tok_stt_last):
                        W(E, tok, tokq2, tokst)
                        ins = None
                        for j in range(4):
                            ins = E.transpose(pmis[:, 128 * j:128 * (j + 1)],
                                              bl2e[half][:, 128 * j:128 * (j + 1)], identf[:])
                        ins.then_inc(TT.h, 1)
                    em("tensor", f)
                    tok_tr = TT.inc()

                    def f(E, d=d, tok=tok_tr, tokx=toks_xr):
                        W(E, tok, *tokx)
                        ins = None
                        for j in range(4):
                            ins = E.scalar_tensor_tensor(xrr[j][:, 128 * d:128 * (d + 1)],
                                                         pmis[:, 128 * j:128 * (j + 1)],
                                                         g2t[:, j:j + 1],
                                                         xrr[j][:, 128 * d:128 * (d + 1)],
                                                         OP.mult, OP.add)
                        ins.then_inc(VV.h, 1)
                    em("vector", f)
                    tok_stt_last = VV.inc()
            st["bl2_done"] = tok_prev_pair_ev
            st["last_stt"] = tok_stt_last

            for j in range(4):
                st["xout_done"][j] = OP_.issue(
                    em, "gpsimd",
                    lambda E, s=s, b=b, j=j: E.dma_start(out=y3[s, 4 * b + j], in_=xrr[j][:]),
                    pre_toks=(tok_stt_last,))

        for s in range(NSEQ):
            tok_ab[s] = emit_stats(s)
            if debug and s == 0 and src is dx:
                for nm, t in [("kons", kons), ("alf", alf), ("bet", bet), ("bia", bia)]:
                    def fd(E, nm=nm, t=t, tok=tok_ab[s]):
                        W(E, tok)
                        E.dma_start(out=ddbg[nm][:, 0:t.shape[1] if False else None] if False else ddbg[nm][:, 0:NT] if nm != "kons" else ddbg[nm],
                                    in_=t[:]).then_inc(SS.h, 0) if False else None
                    pass
                OP_.issue(em, "gpsimd", lambda E: E.dma_start(out=ddbg["kons"], in_=kons[:, 0:1]), pre_toks=(tok_ab[s],))
                OP_.issue(em, "gpsimd", lambda E: E.dma_start(out=ddbg["alf"][:, 0:NT], in_=alf[:]), pre_toks=(tok_ab[s],))
                OP_.issue(em, "gpsimd", lambda E: E.dma_start(out=ddbg["bet"][:, 0:NT], in_=bet[:]), pre_toks=(tok_ab[s],))
                OP_.issue(em, "gpsimd", lambda E: E.dma_start(out=ddbg["bia"][:, 0:NT], in_=bia[:]), pre_toks=(tok_ab[s],))
                OP_.issue(em, "gpsimd", lambda E: E.dma_start(out=ddbg["stw"], in_=stw[:]), pre_toks=(tok_ab[s],))
                OP_.issue(em, "gpsimd", lambda E: E.dma_start(out=ddbg["bns"], in_=bns[:]), pre_toks=(tok_ab[s],))
                OP_.issue(em, "gpsimd", lambda E: E.dma_start(out=ddbg["smax"][:, 0:NT], in_=smax[:]), pre_toks=(tok_ab[s],))
                OP_.issue(em, "gpsimd", lambda E: E.dma_start(out=ddbg["smin"][:, 0:NT], in_=smin[:]), pre_toks=(tok_ab[s],))
            base = s * NSLAB
            emit_xside(base)
            if NSLAB > 1:
                emit_xside(base + 1)
            for b in range(NSLAB):
                emit_slab(base + b)
                if b + 2 < NSLAB:
                    emit_xside(base + b + 2)
        em.flush()
        ps.close()

    def emit_attn(src, dst, upto=99, nheads=NH):
        NQB = TS // 512
        ps = contextlib.ExitStack()
        gkvt = ps.enter_context(nc.sbuf_tensor("Agkv", [128, 2], F32))
        tok_g = LP.issue(em, "sync", lambda E: E.dma_start(out=gkvt[:], in_=dgkv))
        em.flush()

        x3 = src.rearrange("s (k p) d -> s k p d", p=128)
        y3 = dst.rearrange("s (k p) d -> s k p d", p=128)
        SC96 = float(1.0 / np.sqrt(96.0))

        def bail(tok, s):
            for k in range(NT):
                OP_.issue(em, "gpsimd",
                          lambda E, s=s, k=k: E.dma_start(out=y3[s, k], in_=x3[s, k]),
                          pre_toks=(tok,))
            em.flush()

        def mk_ldcast(wdft):
            state = {"t": None}

            def ldcast(dsts, dsrc, width, scale):
                for c in range(len(dsts)):
                    tok_l = LP.issue(em, "sync",
                                     lambda E, c=c, dsrc=dsrc, width=width: E.dma_start(
                                         out=wdft[:, 0:width], in_=dsrc[c * 128:(c + 1) * 128, :]),
                                     pre_toks=(state["t"],))

                    def f(E, c=c, dsts=dsts, width=width, scale=scale, tok=tok_l):
                        W(E, tok)
                        E.tensor_scalar(dsts[c][:, 0:width], wdft[:, 0:width], scale,
                                        None, OP.mult).then_inc(VV.h, 1)
                    em("vector", f)
                    state["t"] = VV.inc()
                return state["t"]
            return ldcast

        for s in range(NSEQ):
            R00 = contextlib.ExitStack()
            rlnA = R00.enter_context(nc.sbuf_tensor(f"ArlnA{s}", [128, NT], F32, side="right"))
            biaA = R00.enter_context(nc.sbuf_tensor(f"AbiaA{s}", [128, NT], F32, side="right"))
            R0 = contextlib.ExitStack()
            kvr = [R0.enter_context(nc.sbuf_tensor(f"Akvr{s}_{c}", [128, TS], BF16, side="right"))
                   for c in range(2)]
            pmid = contextlib.ExitStack()

            R1 = contextlib.ExitStack()
            uT = [R1.enter_context(nc.sbuf_tensor(f"AuT{s}_{c}", [128, TS], BF16, side="right"))
                  for c in range(8)]
            R1a = contextlib.ExitStack()
            xtl = [R1a.enter_context(nc.sbuf_tensor(f"Axt{s}_{i}", [128, 1024], F32, side="right"))
                   for i in range(2)]
            uB = [R1a.enter_context(nc.sbuf_tensor(f"AuB{s}_{i}", [128, 1024], BF16, side="right"))
                  for i in range(2)]
            bnsA = R1a.enter_context(nc.sbuf_tensor(f"Abns{s}", [128, 12 * NT], F32, side="right"))
            stwA = R1a.enter_context(nc.sbuf_tensor(f"Astw{s}", [128, 8 * NT], F32, side="right"))
            psA = contextlib.ExitStack()
            ptpB = psA.enter_context(nc.psum_tensor(f"AptpB{s}", [128, 1024], BF16))

            vt = [None, None]
            for k in range(NT):
                bq = k % 2
                tok_l = LP.issue(em, "sync",
                                 lambda E, s=s, k=k, bq=bq: E.dma_start(out=xtl[bq][:], in_=x3[s, k]),
                                 pre_toks=(vt[bq],))

                def f(E, k=k, bq=bq, tok=tok_l):
                    W(E, tok)
                    E.bn_stats(bnsA[:, 12 * k:12 * k + 6], xtl[bq][:, 0:512])
                    E.bn_stats(bnsA[:, 12 * k + 6:12 * k + 12], xtl[bq][:, 512:1024]).then_inc(VV.h, 1)
                em("vector", f)
                vt[bq] = VV.inc()
            tok_stats = vt[(NT - 1) % 2]

            stp = mkvs(12)

            def f(E, tok=tok_stats, stp=stp):
                W(E, tok)
                S = stp(E)
                me = bnsA[:, 1::3]
                m2 = bnsA[:, 2::3]
                R0_, R1_, R2_ = stwA[:, 0:2 * NT], stwA[:, 2 * NT:3 * NT], stwA[:, 3 * NT:4 * NT]
                R3_, R4_, R5_ = stwA[:, 4 * NT:5 * NT], stwA[:, 5 * NT:6 * NT], stwA[:, 6 * NT:7 * NT]
                S(E.tensor_tensor(R0_, me[:, 0::2], me[:, 1::2], OP.add))
                S(E.tensor_tensor(R1_, R0_[:, 0::2], R0_[:, 1::2], OP.add))
                S(E.tensor_tensor(R0_, m2[:, 0::2], m2[:, 1::2], OP.add))
                S(E.tensor_tensor(R2_, R0_[:, 0::2], R0_[:, 1::2], OP.add))
                S(E.tensor_tensor(stwA[:, 4 * NT:8 * NT], me, me, OP.mult))
                S(E.tensor_tensor(R0_, stwA[:, 4 * NT:8 * NT][:, 0::2],
                                  stwA[:, 4 * NT:8 * NT][:, 1::2], OP.add))
                S(E.tensor_tensor(R3_, R0_[:, 0::2], R0_[:, 1::2], OP.add))
                S(E.tensor_scalar(R4_, R1_, 0.25, None, OP.mult))
                S(E.tensor_scalar(R5_, R2_, 1.0 / 1024.0, None, OP.mult))
                S(E.scalar_tensor_tensor(R5_, R3_, 0.25, R5_, OP.mult, OP.add))
                S(E.tensor_tensor(R0_[:, 0:NT], R4_, R4_, OP.mult))
                S(E.scalar_tensor_tensor(R5_, R0_[:, 0:NT], -1.0, R5_, OP.mult, OP.add))
                E.engine_nop().then_inc(VV.h, 1)
            em("vector", f)
            tok_var = VV.inc()
            sstp = mkss(1)

            def f(E, tok=tok_var, sstp=sstp):
                W(E, tok)
                S = sstp(E)
                S(E.activation(stwA[:, 0:NT], stwA[:, 6 * NT:7 * NT], ACT.Ln,
                               bias=seed[:, 8:9], scale=1.0))
                E.activation(rlnA[:], stwA[:, 0:NT], ACT.Exp, bias=0.0,
                             scale=-0.5).then_inc(SS.h, 1)
            em("scalar", f)
            tok_rln = SS.inc()
            stp2 = mkvs(2)

            def f(E, tok=tok_rln, stp=stp2):
                W(E, tok)
                S = stp(E)
                S(E.tensor_tensor(stwA[:, 0:NT], rlnA[:], stwA[:, 5 * NT:6 * NT], OP.mult))
                S(E.tensor_scalar(biaA[:], stwA[:, 0:NT], -1.0, None, OP.mult))
                E.engine_nop().then_inc(VV.h, 1)
            em("vector", f)
            tok_biaA = VV.inc()

            last_tr = None
            for k in range(NT):
                bq = k % 2
                tok_l = LP.issue(em, "sync",
                                 lambda E, s=s, k=k, bq=bq: E.dma_start(out=xtl[bq][:], in_=x3[s, k]),
                                 pre_toks=(vt[bq],))

                def f(E, k=k, bq=bq, tok=tok_l, tok2=tok_biaA):
                    W(E, tok, tok2)
                    E.activation(uB[bq][:], xtl[bq][:], ACT.Identity,
                                 bias=biaA[:, k:k + 1], scale=rlnA[:, k:k + 1]).then_inc(SS.h, 1)
                em("scalar", f)
                tok_u = SS.inc()

                def f(E, bq=bq, tok=tok_u, tokp=last_tr):
                    W(E, tok, tokp)
                    ins = None
                    for c in range(8):
                        ins = E.transpose(ptpB[:, 128 * c:128 * (c + 1)],
                                          uB[bq][:, 128 * c:128 * (c + 1)], identb[:])
                    ins.then_inc(TT.h, 1)
                em("tensor", f)
                tok_t = TT.inc()

                def f(E, k=k, tok=tok_t):
                    W(E, tok)
                    ins = None
                    for c in range(8):
                        ins = E.tensor_scalar(uT[c][:, 128 * k:128 * (k + 1)],
                                              ptpB[:, 128 * c:128 * (c + 1)], 0.0, None, OP.add)
                    ins.then_inc(VV.h, 1)
                em("vector", f)
                last_tr = VV.inc()
                vt[bq] = last_tr
            tok_uT = last_tr
            em.flush()
            psA.close()
            R1a.close()
            fence()

            if upto <= 12:
                bail(tok_uT, s)
                R1.close(); pmid.close(); R0.close(); R00.close()
                fence()
                continue

            psB = contextlib.ExitStack()
            sS = [psB.enter_context(nc.psum_tensor(f"AsS{s}_{i}", [128, 512], F32))
                  for i in range(4)]
            stp_p = [psB.enter_context(nc.psum_tensor(f"Astp{s}_{i}", [128, 512], F32))
                     for i in range(2)]
            pvp = psB.enter_context(nc.psum_tensor(f"Apvp{s}", [128, 512], F32))

            R_qcb = contextlib.ExitStack()
            qcb = [R_qcb.enter_context(nc.sbuf_tensor(f"Aqcb{s}_{c}", [128, TS], BF16, side="right"))
                   for c in range(4)]
            R_qb = contextlib.ExitStack()
            wqb = [R_qb.enter_context(nc.sbuf_tensor(f"Awq{s}_{c}", [128, QR], BF16, side="right"))
                   for c in range(8)]
            wdf1 = R_qb.enter_context(nc.sbuf_tensor(f"Awdf1{s}", [128, 1024], F32, side="right"))
            tok_w = mk_ldcast(wdf1)(wqb, dwq, QR, 1.0)

            tok_qcb = None
            for m in range(4):
                for nch2 in range(NCH):
                    pp = sS[(m * NCH + nch2) % 2]

                    def f(E, m=m, nch2=nch2, pp=pp, tok=tok_uT, tokp=tok_qcb, tokw=tok_w):
                        W(E, tok, tokp, tokw)
                        ins = None
                        for kf in range(8):
                            ins = E.matmul(pp[:, 0:512], wqb[kf][:, 128 * m:128 * (m + 1)],
                                           uT[kf][:, 512 * nch2:512 * (nch2 + 1)],
                                           start=(kf == 0), stop=(kf == 7))
                        ins.then_inc(TT.h, 1)
                    em("tensor", f)
                    tok_mm = TT.inc()

                    def f(E, m=m, nch2=nch2, pp=pp, tok=tok_mm):
                        W(E, tok)
                        E.tensor_scalar(qcb[m][:, 512 * nch2:512 * (nch2 + 1)], pp[:, 0:512],
                                        0.0, None, OP.add).then_inc(VV.h, 1)
                    em("vector", f)
                    tok_qcb = VV.inc()
            tok_qcb_done = tok_qcb
            em.flush()
            R_qb.close()
            fence()

            q_aug = [pmid.enter_context(nc.sbuf_tensor(f"Aqa{s}_{h}", [97, TS], BF16, side="left"))
                     for h in range(16)]
            R_qc = contextlib.ExitStack()
            wqdb = [R_qc.enter_context(nc.sbuf_tensor(f"Awqd{s}_{c}", [128, D], BF16, side="right"))
                    for c in range(4)]
            wdf2 = R_qc.enter_context(nc.sbuf_tensor(f"Awdf2{s}", [128, 1024], F32, side="right"))
            bncA = [R_qc.enter_context(nc.sbuf_tensor(f"Abnc{s}_{i}", [128, 512], BF16, side="right"))
                    for i in range(2)]
            tok_w = mk_ldcast(wdf2)(wqdb, dwqdec, D, SC96)

            tok_qc = None
            tok_bnc = [None, None]
            for m in range(8):
                for nch2 in range(NCH):
                    pp = sS[(m * NCH + nch2) % 2]
                    bi = (m * NCH + nch2) % 2

                    def f(E, m=m, nch2=nch2, pp=pp, tok=tok_qcb_done, tokp=tok_qc, tokw=tok_w):
                        W(E, tok, tokp, tokw)
                        ins = None
                        for kf in range(4):
                            ins = E.matmul(pp[:, 0:512], wqdb[kf][:, 128 * m:128 * (m + 1)],
                                           qcb[kf][:, 512 * nch2:512 * (nch2 + 1)],
                                           start=(kf == 0), stop=(kf == 3))
                        ins.then_inc(TT.h, 1)
                    em("tensor", f)
                    tok_mm = TT.inc()

                    def f(E, m=m, nch2=nch2, pp=pp, bi=bi, tok=tok_mm, tokb=tok_bnc[bi]):
                        W(E, tok, tokb)
                        E.tensor_scalar(q_aug[2 * m][0:64, 512 * nch2:512 * (nch2 + 1)],
                                        pp[0:64, 0:512], 0.0, None, OP.add)
                        E.tensor_scalar(bncA[bi][64:128, 0:512], pp[64:128, 0:512],
                                        0.0, None, OP.add).then_inc(VV.h, 1)
                    em("vector", f)
                    tok_qc = VV.inc()
                    tok_bnc[bi] = OP_.issue(
                        em, "gpsimd",
                        lambda E, m=m, nch2=nch2, bi=bi: E.dma_start(
                            out=q_aug[2 * m + 1][0:64, 512 * nch2:512 * (nch2 + 1)],
                            in_=bncA[bi][64:128, 0:512]),
                        pre_toks=(tok_qc,))
            tok_qcont = tuple(tok_bnc) + (tok_qc,)
            em.flush()
            R_qc.close()
            fence()

            R_qr = contextlib.ExitStack()
            wqrb = [R_qr.enter_context(nc.sbuf_tensor(f"Awqr{s}_{c}", [128, 512], BF16, side="right"))
                    for c in range(4)]
            wdf3 = R_qr.enter_context(nc.sbuf_tensor(f"Awdf3{s}", [128, 512], F32, side="right"))
            cosc = [R_qr.enter_context(nc.sbuf_tensor(f"Acos{s}_{i}", [128, 512], F32, side="right"))
                    for i in range(2)]
            ssinc = [R_qr.enter_context(nc.sbuf_tensor(f"Assin{s}_{i}", [128, 512], F32, side="right"))
                     for i in range(2)]
            krf = R_qr.enter_context(nc.sbuf_tensor(f"Akrf{s}", [128, 512], F32, side="right"))
            krs = R_qr.enter_context(nc.sbuf_tensor(f"Akrs{s}", [128, 512], F32, side="right"))
            kr1 = R_qr.enter_context(nc.sbuf_tensor(f"Akr1{s}", [128, 512], F32, side="right"))
            bnrA = [R_qr.enter_context(nc.sbuf_tensor(f"Abnr{s}_{i}", [128, 512], BF16, side="right"))
                    for i in range(2)]

            def emit_rope(srcs, n_k, wtile, dst_aug, tok_src, init_free, cosc=cosc,
                          ssinc=ssinc, krf=krf, krs=krs, kr1=kr1, bnrA=bnrA):
                tok_last = init_free
                tok_cs = [None, None]
                tok_bnr = [[None] * 4, [None] * 4]
                for nch2 in range(NCH):
                    cb = nch2 % 2
                    tok_c1 = LP.issue(em, "sync",
                                      lambda E, nch2=nch2, cb=cb: E.dma_start(
                                          out=cosc[cb][:], in_=dcos[:, 512 * nch2:512 * (nch2 + 1)]),
                                      pre_toks=(tok_cs[cb],))
                    tok_c2 = LP.issue(em, "sync",
                                      lambda E, nch2=nch2, cb=cb: E.dma_start(
                                          out=ssinc[cb][:], in_=dssin[:, 512 * nch2:512 * (nch2 + 1)]),
                                      pre_toks=(tok_cs[cb],))
                    for m in range(4):
                        pp = sS[m % 2]

                        def f(E, m=m, nch2=nch2, pp=pp, tok=tok_src, tokp=tok_last):
                            W(E, tok, tokp)
                            ins = None
                            for kf in range(n_k):
                                ins = E.matmul(pp[:, 0:512], wtile[kf][:, 128 * m:128 * (m + 1)],
                                               srcs[kf][:, 512 * nch2:512 * (nch2 + 1)],
                                               start=(kf == 0), stop=(kf == n_k - 1))
                            ins.then_inc(TT.h, 1)
                        em("tensor", f)
                        tok_mm = TT.inc()
                        stpr = mkvs(4)

                        def f(E, m=m, pp=pp, cb=cb, tok=tok_mm, stpr=stpr,
                              tokc1=tok_c1, tokc2=tok_c2, tokb=tuple(tok_bnr[m % 2])):
                            W(E, tok, tokc1, tokc2, *tokb)
                            S = stpr(E)
                            S(E.tensor_scalar(krf[:], pp[:, 0:512], 0.0, None, OP.add))
                            S(E.stream_shuffle(krs[:], krf[:], [(i + 16) % 32 for i in range(32)]))
                            S(E.tensor_tensor(kr1[:], krf[:], cosc[cb][:], OP.mult))
                            S(E.tensor_tensor(krs[:], krs[:], ssinc[cb][:], OP.mult))
                            E.tensor_tensor(bnrA[m % 2][:], kr1[:], krs[:], OP.add).then_inc(VV.h, 1)
                        em("vector", f)
                        tok_last = VV.inc()
                        for g in range(4):
                            tok_bnr[m % 2][g] = OP_.issue(
                                em, "gpsimd",
                                lambda E, m=m, g=g, nch2=nch2: E.dma_start(
                                    out=dst_aug[4 * m + g][64:96, 512 * nch2:512 * (nch2 + 1)],
                                    in_=bnrA[m % 2][32 * g:32 * (g + 1), 0:512]),
                                pre_toks=(tok_last,))
                    tok_cs[cb] = tok_last
                toks = [tok_last]
                for a in tok_bnr:
                    toks.extend(a)
                return tuple(toks)

            tok_w = mk_ldcast(wdf3)(wqrb, dwqrope, 512, SC96)
            toks_qrope = emit_rope(qcb, 4, wqrb, q_aug, tok_qcb_done, tok_w)
            em.flush()
            R_qr.close()
            R_qcb.close()
            fence()

            if upto <= 16:
                bail(tok_qcont[-1], s)
                R1.close(); pmid.close(); R0.close(); R00.close(); psB.close()
                fence()
                continue

            R_kv = contextlib.ExitStack()
            wkvb = [R_kv.enter_context(nc.sbuf_tensor(f"Awkv{s}_{c}", [128, KVR], BF16, side="right"))
                    for c in range(8)]
            wdf4 = R_kv.enter_context(nc.sbuf_tensor(f"Awdf4{s}", [128, 1024], F32, side="right"))
            sqs = R_kv.enter_context(nc.sbuf_tensor(f"Asqs{s}", [128, 512], F32, side="right"))
            rtok = R_kv.enter_context(nc.sbuf_tensor(f"Artok{s}", [128, 8], F32, side="right"))
            kvf = [R_kv.enter_context(nc.sbuf_tensor(f"Akvf{s}_{i}", [128, 512], F32, side="right"))
                   for i in range(2)]
            rrow = R_kv.enter_context(nc.sbuf_tensor(f"Arrow{s}", [128, 512], F32, side="right"))
            tok_w = mk_ldcast(wdf4)(wkvb, dwkv, KVR, 1.0)

            tok_kvr = None
            tok_rms_rd = None
            for nch2 in range(NCH):
                col = slice(512 * nch2, 512 * (nch2 + 1))

                def f(E, col=col, tok=tok_uT, tokq=tok_kvr, tokw=tok_w):
                    W(E, tok, tokq, tokw)
                    ins = None
                    for m2 in range(2):
                        for kf in range(8):
                            ins = E.matmul(sS[m2][:, 0:512],
                                           wkvb[kf][:, 128 * m2:128 * (m2 + 1)],
                                           uT[kf][:, col], start=(kf == 0), stop=(kf == 7))
                    ins.then_inc(TT.h, 1)
                em("tensor", f)
                tok_kv = TT.inc()

                stpk = mkvs(4)

                def f(E, tok=tok_kv, stpk=stpk):
                    W(E, tok)
                    S = stpk(E)
                    S(E.tensor_scalar(kvf[0][:], sS[0][:, 0:512], 0.0, None, OP.add))
                    S(E.tensor_scalar(kvf[1][:], sS[1][:, 0:512], 0.0, None, OP.add))
                    S(E.tensor_tensor(sqs[:], kvf[0][:], kvf[0][:], OP.mult))
                    S(E.tensor_tensor(rrow[:], kvf[1][:], kvf[1][:], OP.mult))
                    E.tensor_tensor(sqs[:], sqs[:], rrow[:], OP.add).then_inc(VV.h, 1)
                em("vector", f)
                tok_kve = VV.inc()

                def f(E, tok=tok_kve, tokr=tok_rms_rd):
                    W(E, tok, tokr)
                    ins = None
                    for j in range(4):
                        ins = E.transpose(sS[2][:, 128 * j:128 * (j + 1)],
                                          sqs[:, 128 * j:128 * (j + 1)], identf[:])
                    ins.then_inc(TT.h, 1)
                em("tensor", f)
                tok_sqT = TT.inc()

                stpr = mkvs(1)

                def f(E, tok=tok_sqT, stpr=stpr):
                    W(E, tok)
                    S = stpr(E)
                    ins = None
                    for j in range(4):
                        ins = E.reduce_sum(rtok[:, j:j + 1], sS[2][:, 128 * j:128 * (j + 1)], AX.X)
                    S(ins)
                    E.engine_nop().then_inc(VV.h, 1)
                em("vector", f)
                tok_rms_rd = VV.inc()

                sstp2 = mkss(1)

                def f(E, tok=tok_rms_rd, sstp2=sstp2):
                    W(E, tok)
                    S = sstp2(E)
                    S(E.activation(rtok[:, 4:8], rtok[:, 0:4], ACT.Ln,
                                   bias=seed[:, 9:10], scale=1.0 / 256.0))
                    E.activation(rtok[:, 4:8], rtok[:, 4:8], ACT.Exp,
                                 bias=0.0, scale=-0.5).then_inc(SS.h, 1)
                em("scalar", f)
                tok_rr = SS.inc()

                tok_rk = None
                for j in range(4):
                    def f(E, j=j, tok=tok_rr, tokp=tok_rk):
                        W(E, tok, tokp)
                        E.transpose(sS[2][0:1, 128 * j:128 * (j + 1)], rtok[:, 4 + j:5 + j],
                                    identf[:]).then_inc(TT.h, 1)
                    em("tensor", f)
                    tok_rjT = TT.inc()

                    def f(E, j=j, tok=tok_rjT):
                        W(E, tok)
                        E.tensor_scalar(rrow[0:1, 0:128], sS[2][0:1, 128 * j:128 * (j + 1)],
                                        0.0, None, OP.add).then_inc(VV.h, 1)
                    em("vector", f)
                    tok_rw = VV.inc()

                    def f(E, j=j, tok=tok_rw):
                        W(E, tok)
                        E.matmul(sS[2][:, 128 * j:128 * (j + 1)], onesf[0:1, :],
                                 rrow[0:1, 0:128], start=True, stop=True).then_inc(TT.h, 1)
                    em("tensor", f)
                    tok_rk = TT.inc()
                tok_rb = tok_rk

                def f(E, col=col, tok=tok_rb, tok2=tok_kve):
                    W(E, tok, tok2)
                    E.tensor_tensor(kvr[0][:, col], kvf[0][:], sS[2][:, 0:512], OP.mult)
                    E.tensor_tensor(kvr[1][:, col], kvf[1][:], sS[2][:, 0:512], OP.mult).then_inc(VV.h, 1)
                em("vector", f)
                tok_kvr = VV.inc()
            em.flush()
            R_kv.close()
            R1.close()
            fence()

            if upto <= 13:
                bail(tok_kvr, s)
                pmid.close(); R0.close(); R00.close(); psB.close()
                fence()
                continue

            k_aug = [pmid.enter_context(nc.sbuf_tensor(f"Aka{s}_{h}", [97, TS], BF16, side="left"))
                     for h in range(16)]
            for h in range(16):
                OP_.issue(em, "gpsimd",
                          lambda E, h=h: E.dma_start(out=k_aug[h][96:97, 0:TS], in_=donesrow))

            R_kca = contextlib.ExitStack()
            wkdecb = [R_kca.enter_context(nc.sbuf_tensor(f"Awkd{s}_{c}", [128, D], BF16, side="right"))
                      for c in range(2)]
            wdf5 = R_kca.enter_context(nc.sbuf_tensor(f"Awdf5{s}", [128, 1024], F32, side="right"))
            bnkA = [R_kca.enter_context(nc.sbuf_tensor(f"Abnk{s}_{i}", [128, 512], BF16, side="right"))
                    for i in range(2)]
            tok_wd = None
            for c in range(2):
                tok_l = LP.issue(em, "sync",
                                 lambda E, c=c: E.dma_start(out=wdf5[:], in_=dwkdec[c * 128:(c + 1) * 128, :]),
                                 pre_toks=(tok_wd,))

                def f(E, c=c, tok=tok_l):
                    W(E, tok, tok_g)
                    E.tensor_scalar(wkdecb[c][:], wdf5[:], gkvt[:, c:c + 1], None,
                                    OP.mult).then_inc(VV.h, 1)
                em("vector", f)
                tok_wd = VV.inc()

            tok_kca = None
            tok_bnk = [None, None]
            for hc in range(8):
                for nch2 in range(NCH):
                    pp = sS[(hc * NCH + nch2) % 2]
                    bi = (hc * NCH + nch2) % 2

                    def f(E, hc=hc, nch2=nch2, pp=pp, tok=tok_kvr, tokp=tok_kca, tokw=tok_wd):
                        W(E, tok, tokp, tokw)
                        ins = None
                        for c in range(2):
                            ins = E.matmul(pp[:, 0:512], wkdecb[c][:, 128 * hc:128 * (hc + 1)],
                                           kvr[c][:, 512 * nch2:512 * (nch2 + 1)],
                                           start=(c == 0), stop=(c == 1))
                        ins.then_inc(TT.h, 1)
                    em("tensor", f)
                    tok_mm = TT.inc()

                    def f(E, hc=hc, nch2=nch2, pp=pp, bi=bi, tok=tok_mm, tokb=tok_bnk[bi]):
                        W(E, tok, tokb)
                        E.tensor_scalar(k_aug[2 * hc][0:64, 512 * nch2:512 * (nch2 + 1)],
                                        pp[0:64, 0:512], 0.0, None, OP.add)
                        E.tensor_scalar(bnkA[bi][64:128, 0:512], pp[64:128, 0:512],
                                        0.0, None, OP.add).then_inc(VV.h, 1)
                    em("vector", f)
                    tok_kca = VV.inc()
                    tok_bnk[bi] = OP_.issue(
                        em, "gpsimd",
                        lambda E, hc=hc, nch2=nch2, bi=bi: E.dma_start(
                            out=k_aug[2 * hc + 1][0:64, 512 * nch2:512 * (nch2 + 1)],
                            in_=bnkA[bi][64:128, 0:512]),
                        pre_toks=(tok_kca,))
            em.flush()
            R_kca.close()
            fence()

            vkm = [pmid.enter_context(nc.sbuf_tensor(f"Avk{s}_{t}", [128, 1040], BF16, side="left"))
                   for t in range(NT)]
            R_vkm = contextlib.ExitStack()
            wvdecb = [R_vkm.enter_context(nc.sbuf_tensor(f"Awvd{s}_{c}", [128, D], BF16, side="right"))
                      for c in range(2)]
            wdf6 = R_vkm.enter_context(nc.sbuf_tensor(f"Awdf6{s}", [128, 1024], F32, side="right"))
            tok_wd = None
            for c in range(2):
                tok_l = LP.issue(em, "sync",
                                 lambda E, c=c: E.dma_start(out=wdf6[:], in_=dwvdec[c * 128:(c + 1) * 128, :]),
                                 pre_toks=(tok_wd,))

                def f(E, c=c, tok=tok_l):
                    W(E, tok, tok_g)
                    E.tensor_scalar(wvdecb[c][:], wdf6[:], gkvt[:, c:c + 1], None,
                                    OP.mult).then_inc(VV.h, 1)
                em("vector", f)
                tok_wd = VV.inc()

            tok_vkm_l = None
            for tc in range(NT):
                def f(E, tc=tc, tok=tok_kvr, tokp=tok_vkm_l, tokw=tok_wd):
                    W(E, tok, tokp, tokw)
                    ins = None
                    for half in range(2):
                        for c in range(2):
                            ins = E.matmul(sS[half][:, 0:512],
                                           kvr[c][:, 128 * tc:128 * (tc + 1)],
                                           wvdecb[c][:, 512 * half:512 * (half + 1)],
                                           start=(c == 0), stop=(c == 1))
                    ins.then_inc(TT.h, 1)
                em("tensor", f)
                tok_mmv = TT.inc()

                def f(E, tc=tc, tok=tok_mmv):
                    W(E, tok)
                    va = vkm[tc][:, 0:1040].rearrange("p (h c) -> p h c", c=65)
                    E.tensor_scalar(va[:, 0:8, 0:64],
                                    sS[0][:, 0:512].rearrange("p (h c) -> p h c", c=64),
                                    0.0, None, OP.add)
                    E.tensor_scalar(va[:, 8:16, 0:64],
                                    sS[1][:, 0:512].rearrange("p (h c) -> p h c", c=64),
                                    0.0, None, OP.add)
                    E.tensor_scalar(va[:, 0:16, 64:65],
                                    onesf[:, 0:16].rearrange("p (h c) -> p h c", c=1),
                                    0.0, None, OP.add).then_inc(VV.h, 1)
                em("vector", f)
                tok_vkm_l = VV.inc()
            em.flush()
            R_vkm.close()
            R0.close()
            fence()

            if upto <= 14:
                bail(tok_vkm_l, s)
                pmid.close(); R00.close(); psB.close()
                fence()
                continue

            R_kr = contextlib.ExitStack()
            wkrb = [R_kr.enter_context(nc.sbuf_tensor(f"Awkr{s}_{c}", [128, 512], BF16, side="right"))
                    for c in range(8)]
            wdf7 = R_kr.enter_context(nc.sbuf_tensor(f"Awdf7{s}", [128, 512], F32, side="right"))
            xtl2 = R_kr.enter_context(nc.sbuf_tensor(f"Axt2{s}", [128, 1024], F32, side="right"))
            uB2 = R_kr.enter_context(nc.sbuf_tensor(f"AuB2{s}", [128, 1024], BF16, side="right"))
            uT2 = [R_kr.enter_context(nc.sbuf_tensor(f"AuT2{s}_{c}", [128, 512], BF16, side="right"))
                   for c in range(8)]
            cosc2 = [R_kr.enter_context(nc.sbuf_tensor(f"Acos2{s}_{i}", [128, 512], F32, side="right"))
                     for i in range(2)]
            ssinc2 = [R_kr.enter_context(nc.sbuf_tensor(f"Assin2{s}_{i}", [128, 512], F32, side="right"))
                      for i in range(2)]
            krf2 = R_kr.enter_context(nc.sbuf_tensor(f"Akrf2{s}", [128, 512], F32, side="right"))
            krs2 = R_kr.enter_context(nc.sbuf_tensor(f"Akrs2{s}", [128, 512], F32, side="right"))
            kr12 = R_kr.enter_context(nc.sbuf_tensor(f"Akr12{s}", [128, 512], F32, side="right"))
            bnr2 = [R_kr.enter_context(nc.sbuf_tensor(f"Abnr2{s}_{i}", [128, 512], BF16, side="right"))
                    for i in range(2)]
            psKr = contextlib.ExitStack()
            ptpB2 = psKr.enter_context(nc.psum_tensor(f"AptpB2{s}", [128, 1024], BF16))

            tok_w = mk_ldcast(wdf7)(wkrb, dwkrope, 512, 1.0)
            tok_cs2 = [None, None]
            tok_bnr2 = [[None] * 4, [None] * 4]
            tok_rope = None
            tok_xt = None
            tok_ut2 = None
            for nch2 in range(NCH):
                cb = nch2 % 2
                tok_c1 = LP.issue(em, "sync",
                                  lambda E, nch2=nch2, cb=cb: E.dma_start(
                                      out=cosc2[cb][:], in_=dcos[:, 512 * nch2:512 * (nch2 + 1)]),
                                  pre_toks=(tok_cs2[cb],))
                tok_c2 = LP.issue(em, "sync",
                                  lambda E, nch2=nch2, cb=cb: E.dma_start(
                                      out=ssinc2[cb][:], in_=dssin[:, 512 * nch2:512 * (nch2 + 1)]),
                                  pre_toks=(tok_cs2[cb],))
                tok_ev = None
                for j in range(4):
                    k = 4 * nch2 + j
                    tok_l = LP.issue(em, "sync",
                                     lambda E, s=s, k=k: E.dma_start(out=xtl2[:], in_=x3[s, k]),
                                     pre_toks=(tok_xt,))

                    def f(E, k=k, tok=tok_l, tokp=tok_ut2):
                        W(E, tok, tokp)
                        E.activation(uB2[:], xtl2[:], ACT.Identity,
                                     bias=biaA[:, k:k + 1], scale=rlnA[:, k:k + 1]).then_inc(SS.h, 1)
                    em("scalar", f)
                    tok_xt = SS.inc()

                    def f(E, tok=tok_xt, tokp=tok_ev, tokr=tok_rope):
                        W(E, tok, tokp, tokr)
                        ins = None
                        for c in range(8):
                            ins = E.transpose(ptpB2[:, 128 * c:128 * (c + 1)],
                                              uB2[:, 128 * c:128 * (c + 1)], identb[:])
                        ins.then_inc(TT.h, 1)
                    em("tensor", f)
                    tok_t = TT.inc()

                    def f(E, j=j, tok=tok_t):
                        W(E, tok)
                        ins = None
                        for c in range(8):
                            ins = E.tensor_scalar(uT2[c][:, 128 * j:128 * (j + 1)],
                                                  ptpB2[:, 128 * c:128 * (c + 1)], 0.0, None, OP.add)
                        ins.then_inc(VV.h, 1)
                    em("vector", f)
                    tok_ev = VV.inc()
                    tok_ut2 = tok_ev
                for m in range(4):
                    pp = sS[m % 2]

                    def f(E, m=m, pp=pp, tok=tok_ev, tokw=tok_w, tokr=tok_rope):
                        W(E, tok, tokw, tokr)
                        ins = None
                        for kf in range(8):
                            ins = E.matmul(pp[:, 0:512], wkrb[kf][:, 128 * m:128 * (m + 1)],
                                           uT2[kf][:, 0:512],
                                           start=(kf == 0), stop=(kf == 7))
                        ins.then_inc(TT.h, 1)
                    em("tensor", f)
                    tok_mm = TT.inc()
                    stpr = mkvs(4)

                    def f(E, m=m, pp=pp, cb=cb, tok=tok_mm, stpr=stpr,
                          tokc1=tok_c1, tokc2=tok_c2, tokb=tuple(tok_bnr2[m % 2])):
                        W(E, tok, tokc1, tokc2, *tokb)
                        S = stpr(E)
                        S(E.tensor_scalar(krf2[:], pp[:, 0:512], 0.0, None, OP.add))
                        S(E.stream_shuffle(krs2[:], krf2[:], [(i + 16) % 32 for i in range(32)]))
                        S(E.tensor_tensor(kr12[:], krf2[:], cosc2[cb][:], OP.mult))
                        S(E.tensor_tensor(krs2[:], krs2[:], ssinc2[cb][:], OP.mult))
                        E.tensor_tensor(bnr2[m % 2][:], kr12[:], krs2[:], OP.add).then_inc(VV.h, 1)
                    em("vector", f)
                    tok_rope = VV.inc()
                    for g in range(4):
                        tok_bnr2[m % 2][g] = OP_.issue(
                            em, "gpsimd",
                            lambda E, m=m, g=g, nch2=nch2: E.dma_start(
                                out=k_aug[4 * m + g][64:96, 512 * nch2:512 * (nch2 + 1)],
                                in_=bnr2[m % 2][32 * g:32 * (g + 1), 0:512]),
                            pre_toks=(tok_rope,))
                tok_cs2[cb] = tok_rope
            em.flush()
            psKr.close()
            R_kr.close()
            fence()

            if upto <= 15:
                bail(tok_rope, s)
                pmid.close(); R00.close(); psB.close()
                fence()
                continue

            R_ofm = contextlib.ExitStack()
            ofm = [R_ofm.enter_context(nc.sbuf_tensor(f"Aof{s}_{c}", [128, TS], BF16, side="right"))
                   for c in range(8)]
            R_core = contextlib.ExitStack()
            masks = [R_core.enter_context(nc.sbuf_tensor(f"Amsk{s}_{L}", [128, 512], BF16, side="right"))
                     for L in range(4)]
            Ptb = [R_core.enter_context(nc.sbuf_tensor(f"APt{s}_{i}", [128, 512], BF16, side="right"))
                   for i in range(2)]
            rcs = R_core.enter_context(nc.sbuf_tensor(f"Arcs{s}", [128, 512], F32, side="right"))
            obn = [R_core.enter_context(nc.sbuf_tensor(f"Aobn{s}_{i}", [128, 512], BF16, side="right"))
                   for i in range(2)]
            srow = R_core.enter_context(nc.sbuf_tensor(f"Asrow{s}", [1, 512], F32, side="right"))
            nmx = R_core.enter_context(nc.sbuf_tensor(f"Anmx{s}", [128, 8], F32, side="right"))
            mcolb = R_core.enter_context(nc.sbuf_tensor(f"Amcb{s}", [128, 4], BF16, side="right"))
            masks_tok = [OP_.issue(em, "gpsimd",
                                   lambda E, L=L: E.dma_start(out=masks[L][:], in_=dminm[L]))
                         for L in range(4)]

            iters = [(h, qb) for h in range(nheads) for qb in range(NQB)]
            tok_sS_free = [None] * 4
            tok_stp_free = [None] * 2
            tok_Ptb_free = [None] * 2
            tok_pvp_free = None
            bias_toks = {}
            obn_tok = [None, None]
            srot = 0
            tok_core = None

            def emit_P(i):
                nonlocal srot
                h, qb = iters[i]
                btoks = []
                for tt in range(4):
                    qt = 4 * qb + tt
                    Wt = 128 * (qt + 1)
                    nkc = (Wt + 511) // 512
                    tok_qk = None
                    rd_toks = []
                    for c in range(nkc):
                        w = min(512, Wt - 512 * c)
                        slot = srot
                        srot = (srot + 1) % 4

                        def f(E, h=h, qt=qt, c=c, w=w, slot=slot,
                              tokf=tok_sS_free[slot]):
                            W(E, tokf)
                            E.matmul(sS[slot][:, 0:w], q_aug[h][0:96, 128 * qt:128 * (qt + 1)],
                                     k_aug[h][0:96, 512 * c:512 * c + w],
                                     start=True, stop=True).then_inc(TT.h, 1)
                        em("tensor", f)
                        tok_qk = TT.inc()

                        def f(E, c=c, w=w, slot=slot, tok=tok_qk):
                            W(E, tok)
                            E.reduce_max(nmx[:, c:c + 1], sS[slot][:, 0:w], AX.X,
                                         negate=True).then_inc(VV.h, 1)
                        em("vector", f)
                        tok_sS_free[slot] = VV.inc()
                        rd_toks.append(tok_sS_free[slot])

                    def f(E, tt=tt, nkc=nkc, tok=rd_toks[-1],
                          tokd=bias_toks[i - 1][tt] if i >= 1 else None):
                        W(E, tok, tokd)
                        E.tensor_reduce(nmx[:, 4:5], nmx[:, 0:nkc], AX.X, OP.min)
                        E.engine_nop().then_inc(VV.h, 1)
                    em("vector", f)
                    tok_cb1 = VV.inc()

                    def f(E, tt=tt, tok=tok_cb1):
                        W(E, tok)
                        E.tensor_scalar(mcolb[:, tt:tt + 1], nmx[:, 4:5], 0.0,
                                        None, OP.add).then_inc(VV.h, 1)
                    em("vector", f)
                    tok_cb = VV.inc()

                    btoks.append(OP_.issue(
                        em, "gpsimd",
                        lambda E, h=h, qt=qt, tt=tt: E.dma_start(
                            out=q_aug[h][96:97, 128 * qt:128 * (qt + 1)],
                            in_=mcolb[:, tt:tt + 1]),
                        pre_toks=(tok_cb,)))
                bias_toks[i] = tuple(btoks)

            def emit_TF(j):
                nonlocal srot, tok_pvp_free, tok_core
                h, qb = iters[j]
                KC = 4 * (qb + 1)
                tok_pv = None
                for kc in range(KC):
                    ssl = kc % 2
                    psl = kc % 2

                    def f(E, h=h, qb=qb, kc=kc, ssl=ssl, tokf=tok_stp_free[ssl],
                          tokb=bias_toks[j]):
                        W(E, tokf, *tokb)
                        E.matmul(stp_p[ssl][:, 0:512], k_aug[h][0:97, 128 * kc:128 * (kc + 1)],
                                 q_aug[h][0:97, 512 * qb:512 * (qb + 1)],
                                 start=True, stop=True).then_inc(TT.h, 1)
                    em("tensor", f)
                    tok_st = TT.inc()

                    def f(E, ssl=ssl, psl=psl, tok=tok_st, tokf=tok_Ptb_free[psl]):
                        W(E, tok, tokf)
                        E.activation(Ptb[psl][:], stp_p[ssl][:, 0:512], ACT.Exp,
                                     bias=0.0, scale=1.0).then_inc(SS.h, 1)
                    em("scalar", f)
                    tok_exp = SS.inc()
                    tok_stp_free[ssl] = tok_exp

                    pv_wait = tok_exp
                    if kc >= 4 * qb:
                        L = kc - 4 * qb

                        def f(E, psl=psl, L=L, tok=tok_exp, tokm=masks_tok[L]):
                            W(E, tok, tokm)
                            E.tensor_tensor(Ptb[psl][:], Ptb[psl][:], masks[L][:],
                                            OP.min).then_inc(VV.h, 1)
                        em("vector", f)
                        pv_wait = VV.inc()

                    def f(E, h=h, kc=kc, psl=psl, KC=KC, tok=pv_wait,
                          tokp=tok_pvp_free if kc == 0 else None):
                        W(E, tok, tokp)
                        E.matmul(pvp[0:65, 0:512], vkm[kc][:, 65 * h:65 * h + 65],
                                 Ptb[psl][:, 0:512], start=(kc == 0),
                                 stop=(kc == KC - 1)).then_inc(TT.h, 1)
                    em("tensor", f)
                    tok_pv = TT.inc()
                    tok_Ptb_free[psl] = tok_pv

                def f(E, tok=tok_pv):
                    W(E, tok)
                    E.tensor_scalar(srow[0:1, 0:512], pvp[64:65, 0:512], 0.0,
                                    None, OP.add).then_inc(VV.h, 1)
                em("vector", f)
                tok_sr = VV.inc()

                bslot = srot
                srot = (srot + 1) % 4

                def f(E, bslot=bslot, tok=tok_sr, tokf=tok_sS_free[bslot]):
                    W(E, tok, tokf)
                    E.matmul(sS[bslot][:, 0:512], onesf[0:1, 0:128], srow[0:1, 0:512],
                             start=True, stop=True).then_inc(TT.h, 1)
                em("tensor", f)
                tok_bc = TT.inc()

                def f(E, bslot=bslot, tok=tok_bc):
                    W(E, tok)
                    E.reciprocal(rcs[0:64, 0:512], sS[bslot][0:64, 0:512]).then_inc(VV.h, 1)
                em("vector", f)
                tok_rc = VV.inc()
                tok_sS_free[bslot] = tok_rc

                if h % 2 == 0:
                    def f(E, h=h, qb=qb, tok=tok_rc):
                        W(E, tok)
                        E.tensor_tensor(ofm[h // 2][0:64, 512 * qb:512 * (qb + 1)],
                                        pvp[0:64, 0:512], rcs[0:64, 0:512],
                                        OP.mult).then_inc(VV.h, 1)
                    em("vector", f)
                    tok_ev = VV.inc()
                else:
                    ob = (j // 2) % 2

                    def f(E, ob=ob, tok=tok_rc, tokb=obn_tok[ob]):
                        W(E, tok, tokb)
                        E.tensor_tensor(obn[ob][0:64, 0:512],
                                        pvp[0:64, 0:512], rcs[0:64, 0:512],
                                        OP.mult).then_inc(VV.h, 1)
                    em("vector", f)
                    tok_ev = VV.inc()
                    obn_tok[ob] = OP_.issue(
                        em, "gpsimd",
                        lambda E, h=h, qb=qb, ob=ob: E.dma_start(
                            out=ofm[h // 2][64:128, 512 * qb:512 * (qb + 1)],
                            in_=obn[ob][0:64, 0:512]),
                        pre_toks=(tok_ev,))
                tok_pvp_free = tok_ev
                tok_core = tok_ev

            for i in range(len(iters)):
                emit_P(i)
                if i > 0:
                    emit_TF(i - 1)
            emit_TF(len(iters) - 1)

            em.flush()
            R_core.close()
            pmid.close()
            fence()

            if upto <= 3:
                bail(tok_core, s)
                R_ofm.close(); R00.close(); psB.close()
                fence()
                continue

            R_B4 = contextlib.ExitStack()
            wob = [R_B4.enter_context(nc.sbuf_tensor(f"Awo{s}_{c}", [128, D], BF16, side="right"))
                   for c in range(8)]
            xa = [R_B4.enter_context(nc.sbuf_tensor(f"Axa{s}_{i}", [128, 1024], F32, side="right"))
                  for i in range(4)]
            woe = [R_B4.enter_context(nc.sbuf_tensor(f"Awoe{s}_{i}", [128, 512], F32, side="right"))
                   for i in range(2)]
            wdf8 = R_B4.enter_context(nc.sbuf_tensor(f"Awdf8{s}", [128, 1024], F32, side="right"))
            tok_wo_ready = mk_ldcast(wdf8)(wob, dwo, D, 1.0)

            tok_xa_store = [None] * 4
            tok_woe_free = [None, None]
            tok_wo_tr = None
            tok_stt_l = None
            for nch2 in range(NCH):
                toks_xl = []
                for j in range(4):
                    toks_xl.append(LP.issue(
                        em, "sync",
                        lambda E, s=s, nch2=nch2, j=j: E.dma_start(out=xa[j][:],
                                                                   in_=x3[s, 4 * nch2 + j]),
                        pre_toks=(tok_xa_store[j],)))
                toks_xl = tuple(toks_xl)
                for d in range(8):
                    db = d % 2

                    def f(E, nch2=nch2, d=d, db=db, tok=tok_core, tokf=tok_woe_free[db],
                          tokw=tok_wo_ready):
                        W(E, tok, tokf, tokw)
                        ins = None
                        for c in range(8):
                            ins = E.matmul(sS[db][:, 0:512],
                                           wob[c][:, 128 * d:128 * (d + 1)],
                                           ofm[c][:, 512 * nch2:512 * (nch2 + 1)],
                                           start=(c == 0), stop=(c == 7))
                        ins.then_inc(TT.h, 1)
                    em("tensor", f)
                    tok_mm = TT.inc()

                    def f(E, db=db, tok=tok_mm):
                        W(E, tok)
                        E.tensor_scalar(woe[db][:], sS[db][:, 0:512],
                                        0.0, None, OP.add).then_inc(VV.h, 1)
                    em("vector", f)
                    tok_ev = VV.inc()
                    tok_woe_free[db] = tok_ev

                    def f(E, db=db, tok=tok_ev, tokp=tok_wo_tr):
                        W(E, tok, tokp)
                        ins = None
                        for j in range(4):
                            ins = E.transpose(sS[2][:, 128 * j:128 * (j + 1)],
                                              woe[db][:, 128 * j:128 * (j + 1)], identf[:])
                        ins.then_inc(TT.h, 1)
                    em("tensor", f)
                    tok_tr = TT.inc()

                    def f(E, d=d, tok=tok_tr, tokx=toks_xl):
                        W(E, tok, *tokx)
                        ins = None
                        for j in range(4):
                            ins = E.tensor_tensor(xa[j][:, 128 * d:128 * (d + 1)],
                                                  xa[j][:, 128 * d:128 * (d + 1)],
                                                  sS[2][:, 128 * j:128 * (j + 1)], OP.add)
                        ins.then_inc(VV.h, 1)
                    em("vector", f)
                    tok_stt_l = VV.inc()
                    tok_wo_tr = tok_stt_l

                for j in range(4):
                    tok_xa_store[j] = OP_.issue(
                        em, "gpsimd",
                        lambda E, s=s, nch2=nch2, j=j: E.dma_start(out=y3[s, 4 * nch2 + j],
                                                                   in_=xa[j][:]),
                        pre_toks=(tok_stt_l,))
            em.flush()
            R_B4.close()
            R_ofm.close()
            R00.close()
            psB.close()
            fence()

        ps.close()

    if do_attn:
        emit_ffn(dx, dx1, "A")
        fence()
        emit_attn(dx1, dx2 if do_ffn2 else dout, upto=attn_upto, nheads=attn_heads)
        fence()
        if do_ffn2:
            emit_ffn(dx2, dout, "C")
    else:
        emit_ffn(dx, dout, "A")

    toks = OP_.all_tok()

    def f(E, toks=tuple(toks)):
        for t in toks:
            E.wait_ge(t[0], t[1])
    em("gpsimd", f)
    em.flush()
    es.close()
    return nc



def _host_constants(TS):
    import ml_dtypes
    identb = np.eye(128, dtype=ml_dtypes.bfloat16)
    identf = np.eye(128, dtype=np.float32)
    onesf = np.ones((128, 128), dtype=np.float32)
    seed = np.zeros((128, 16), dtype=np.float32)
    seed[:, 0:8] = 1.5
    seed[:, 8] = EPS_LN
    seed[:, 9] = EPS_RMS
    j = np.arange(16, dtype=np.float32)
    inv_freq = (1.0 / (10000.0 ** (2.0 * j / 32.0))).astype(np.float32)
    t = np.arange(TS, dtype=np.float32)
    ang = np.outer(t, inv_freq)
    emb = np.concatenate([ang, ang], axis=1)
    cos32 = np.cos(emb).astype(np.float32)
    sin32 = np.sin(emb).astype(np.float32)
    p = np.arange(128)
    d32 = p % 32
    cosfm = np.ascontiguousarray(cos32[:, d32].T)
    sgn = np.where(d32 < 16, -1.0, 1.0).astype(np.float32)
    ssinfm = np.ascontiguousarray((sin32[:, d32] * sgn[None, :]).T)
    onesrow = np.ones((1, TS), dtype=ml_dtypes.bfloat16)
    BIG = np.float32(3.0e38)
    minmask = np.zeros((4, 128, 512), dtype=np.float32)
    kr = np.arange(128)[:, None]
    qc = np.arange(128)[None, :]
    for L in range(4):
        for ti in range(4):
            blk = minmask[L][:, 128 * ti:128 * (ti + 1)]
            if ti > L:
                blk[:] = BIG
            elif ti == L:
                blk[:] = np.where(kr <= qc, BIG, np.float32(0.0))
    minmask = minmask.astype(ml_dtypes.bfloat16)
    return dict(identb=identb, identf=identf, onesf=onesf, seed=seed,
                cosfm=cosfm, ssinfm=ssinfm, onesrow=onesrow, minmask=minmask)


def _prep_inputs(L0_attn_W_kv, L0_attn_g_kv, L0_attn_W_kdec, L0_attn_W_vdec,
                 L0_attn_W_q, L0_attn_W_qdec, L0_attn_W_krope, L0_attn_W_qrope,
                 L0_attn_W_o, L0_ffn_w1, L0_ffn_conv_w, L0_ffn_conv_b, L0_ffn_w2, TS):
    f32 = lambda a: np.ascontiguousarray(np.asarray(a, np.float32))
    consts = _host_constants(TS)
    return dict(
        wkv=f32(L0_attn_W_kv),
        gkv=np.ascontiguousarray(np.asarray(L0_attn_g_kv, np.float32).reshape(2, 128).T),
        wkdec=f32(L0_attn_W_kdec), wvdec=f32(L0_attn_W_vdec),
        wq=f32(L0_attn_W_q), wqdec=f32(L0_attn_W_qdec),
        wkrope=f32(L0_attn_W_krope), wqrope=f32(L0_attn_W_qrope),
        wo=f32(L0_attn_W_o),
        w1t=np.ascontiguousarray(np.asarray(L0_ffn_w1, np.float32).T),
        w2t=np.ascontiguousarray(np.asarray(L0_ffn_w2, np.float32).T),
        convw=np.ascontiguousarray(np.asarray(L0_ffn_conv_w, np.float32)
                                   .reshape(32, 128, 3).transpose(1, 0, 2).reshape(128, 96)),
        convb=np.ascontiguousarray(np.asarray(L0_ffn_conv_b, np.float32).reshape(32, 128).T),
        **consts)


def kernel(L0_attn_x, L0_attn_W_kv, L0_attn_g_kv, L0_attn_W_kdec, L0_attn_W_vdec,
           L0_attn_W_q, L0_attn_W_qdec, L0_attn_W_krope, L0_attn_W_qrope,
           L0_attn_W_o, L0_ffn_w1, L0_ffn_conv_w, L0_ffn_conv_b, L0_ffn_w2):
    x = np.ascontiguousarray(np.asarray(L0_attn_x, dtype=np.float32))
    B, TS, _ = x.shape
    nseq = B // NCORES
    nc = build_program(TS=TS, NSEQ=nseq)
    shared = _prep_inputs(L0_attn_W_kv, L0_attn_g_kv, L0_attn_W_kdec, L0_attn_W_vdec,
                          L0_attn_W_q, L0_attn_W_qdec, L0_attn_W_krope, L0_attn_W_qrope,
                          L0_attn_W_o, L0_ffn_w1, L0_ffn_conv_w, L0_ffn_conv_b,
                          L0_ffn_w2, TS)
    in_maps = []
    for c in range(NCORES):
        m = dict(shared)
        m["x"] = np.ascontiguousarray(x[c * nseq:(c + 1) * nseq])
        in_maps.append(m)
    res = run_bass_kernel_spmd(nc, in_maps, core_ids=list(range(NCORES)))
    out = np.concatenate([np.asarray(r["out"]) for r in res.results], axis=0)
    return out.astype(np.float32)


# version 06
# speedup vs baseline: 1.3809x; 1.0336x over previous
import contextlib
import numpy as np
import concourse.bass as bass
import concourse.mybir as mybir
from concourse.bass_utils import run_bass_kernel_spmd

F32 = mybir.dt.float32
BF16 = mybir.dt.bfloat16
I32 = mybir.dt.int32
OP = mybir.AluOpType
ACT = mybir.ActivationFunctionType
AX = mybir.AxisListType

D = 1024
HID = 4096
NH = 16
KVR = 256
QR = 512
EPS_LN = 1e-5
EPS_RMS = 1e-6
EPS_CLIP = 1e-5
NEG_BIG = -1.0e30
NCORES = 8
ENGS = ("sync", "gpsimd", "scalar", "vector", "tensor")


class SemC:
    __slots__ = ("h", "v")

    def __init__(self, h):
        self.h = h
        self.v = 0

    def inc(self, n=1):
        self.v += n
        return (self.h, self.v)

    def tok(self):
        return (self.h, self.v)


class DmaPool:

    def __init__(self, nc, es, name, n):
        self.sems = [SemC(es.enter_context(nc.semaphore(f"{name}{i}"))) for i in range(n)]
        self.rr = 0

    def issue(self, em, queue, dma_fn, pre_toks=()):
        s = self.sems[self.rr]
        self.rr = (self.rr + 1) % len(self.sems)
        prev = s.tok()

        def f(E, s=s, prev=prev, pre=tuple(pre_toks), dma_fn=dma_fn):
            if prev[1] > 0:
                E.wait_ge(prev[0], prev[1])
            for t in pre:
                if t is not None:
                    E.wait_ge(t[0], t[1])
            dma_fn(E).then_inc(s.h, 16)
        em(queue, f)
        return s.inc(16)

    def all_tok(self):
        return [s.tok() for s in self.sems if s.v > 0]


class Emit:
    def __init__(self, block):
        self.block = block
        self.q = {e: [] for e in ENGS}

    def __call__(self, eng, fn):
        self.q[eng].append(fn)

    def flush(self):
        for name in ENGS:
            fns = self.q[name]
            if not fns:
                continue

            def body(E, fns=fns):
                for f in fns:
                    f(E)

            getattr(self.block, name)(body)
        self.q = {e: [] for e in ENGS}


def W(E, *toks):
    for t in toks:
        if t is not None:
            E.wait_ge(t[0], t[1])


def build_program(TS=2048, NSEQ=2, do_attn=True, do_ffn2=True, debug=False, attn_upto=99, attn_heads=NH):
    assert TS % 512 == 0
    NT = TS // 128
    NSLAB = TS // 512
    NCH = TS // 512
    nc = bass.Bass("TRN2", target_bir_lowering=False)

    dx = nc.dram_tensor("x", [NSEQ, TS, D], F32, kind="ExternalInput").ap()
    dwkv = nc.dram_tensor("wkv", [D, KVR], F32, kind="ExternalInput").ap()
    dgkv = nc.dram_tensor("gkv", [128, 2], F32, kind="ExternalInput").ap()
    dwkdec = nc.dram_tensor("wkdec", [KVR, D], F32, kind="ExternalInput").ap()
    dwvdec = nc.dram_tensor("wvdec", [KVR, D], F32, kind="ExternalInput").ap()
    dwq = nc.dram_tensor("wq", [D, QR], F32, kind="ExternalInput").ap()
    dwqdec = nc.dram_tensor("wqdec", [QR, D], F32, kind="ExternalInput").ap()
    dwkrope = nc.dram_tensor("wkrope", [D, 512], F32, kind="ExternalInput").ap()
    dwqrope = nc.dram_tensor("wqrope", [QR, 512], F32, kind="ExternalInput").ap()
    dwo = nc.dram_tensor("wo", [D, D], F32, kind="ExternalInput").ap()
    dw1t = nc.dram_tensor("w1t", [D, HID], F32, kind="ExternalInput").ap()
    dw2t = nc.dram_tensor("w2t", [HID, D], F32, kind="ExternalInput").ap()
    dconvw = nc.dram_tensor("convw", [128, 96], F32, kind="ExternalInput").ap()
    dconvb = nc.dram_tensor("convb", [128, 32], F32, kind="ExternalInput").ap()
    didb = nc.dram_tensor("identb", [128, 128], BF16, kind="ExternalInput").ap()
    didf = nc.dram_tensor("identf", [128, 128], F32, kind="ExternalInput").ap()
    donesf = nc.dram_tensor("onesf", [128, 128], F32, kind="ExternalInput").ap()
    donesb = nc.dram_tensor("onesb", [128, 128], BF16, kind="ExternalInput").ap()
    dseed = nc.dram_tensor("seed", [128, 16], F32, kind="ExternalInput").ap()
    dcos = nc.dram_tensor("cosfm", [128, TS], F32, kind="ExternalInput").ap()
    dssin = nc.dram_tensor("ssinfm", [128, TS], F32, kind="ExternalInput").ap()
    donesrow = nc.dram_tensor("onesrow", [1, TS], BF16, kind="ExternalInput").ap()
    dminm = nc.dram_tensor("minmask", [4, 128, 512], BF16, kind="ExternalInput").ap()
    dx1 = nc.dram_tensor("x1s", [NSEQ, TS, D], F32, kind="Internal").ap()
    dx2 = nc.dram_tensor("x2s", [NSEQ, TS, D], F32, kind="Internal").ap()
    dw1q = nc.dram_tensor("w1qs", [4, 128, 8192], BF16, kind="Internal").ap()
    dw2q = nc.dram_tensor("w2qs", [HID, D], BF16, kind="Internal").ap()
    dout = nc.dram_tensor("out", [NSEQ, TS, D], F32, kind="ExternalOutput").ap()
    ddbg = {}
    if debug:
        for nm, shp, dt in [("kons", [128, 1], F32), ("alf", [128, 64], F32),
                            ("bet", [128, 64], F32), ("bia", [128, 64], F32),
                            ("nT0", [128, 514], F32), ("apre", [128, 16384], F32),
                            ("apost", [128, 16384], F32), ("s2t", [128, 4], F32),
                            ("g2t", [128, 4], F32), ("bl2e0", [128, 512], F32),
                            ("amx", [128, 512], F32)]:
            ddbg[nm] = nc.dram_tensor("dbg_" + nm, shp, dt, kind="ExternalOutput").ap()
        ddbg["stw"] = nc.dram_tensor("dbg_stw", [128, 8 * (TS // 128)], F32, kind="ExternalOutput").ap()
        ddbg["bns"] = nc.dram_tensor("dbg_bns", [128, 12 * (TS // 128)], F32, kind="ExternalOutput").ap()
        ddbg["smax"] = nc.dram_tensor("dbg_smax", [128, 64], F32, kind="ExternalOutput").ap()
        ddbg["hp0"] = nc.dram_tensor("dbg_hp0", [128, 514], F32, kind="ExternalOutput").ap()
        ddbg["w1s0"] = nc.dram_tensor("dbg_w1s0", [128, 8192], F32, kind="ExternalOutput").ap()
        ddbg["cv0"] = nc.dram_tensor("dbg_cv0", [128, 512], F32, kind="ExternalOutput").ap()
        ddbg["smin"] = nc.dram_tensor("dbg_smin", [128, 64], F32, kind="ExternalOutput").ap()

    es = contextlib.ExitStack()
    TT = SemC(es.enter_context(nc.semaphore("sTT")))
    VS = SemC(es.enter_context(nc.semaphore("sVS")))
    SZ = SemC(es.enter_context(nc.semaphore("sSZ")))
    VV = SemC(es.enter_context(nc.semaphore("sVV")))
    SS = SemC(es.enter_context(nc.semaphore("sSS")))
    LP = DmaPool(nc, es, "sL", 8)
    OP_ = DmaPool(nc, es, "sO", 4)

    block = es.enter_context(nc.Block())
    em = Emit(block)

    def mkvs(n):
        base = VS.v
        VS.v += n

        def factory(E):
            cnt = [0]

            def S(ins):
                cnt[0] += 1
                assert cnt[0] <= n
                ins.then_inc(VS.h, 1)
                E.wait_ge(VS.h, base + cnt[0])
            return S
        return factory

    def mkss(n):
        base = SZ.v
        SZ.v += n

        def factory(E):
            cnt = [0]

            def S(ins):
                cnt[0] += 1
                assert cnt[0] <= n
                ins.then_inc(SZ.h, 1)
                E.wait_ge(SZ.h, base + cnt[0])
            return S
        return factory

    identb = es.enter_context(nc.sbuf_tensor("sb_identb", [128, 128], BF16))
    onesb = es.enter_context(nc.sbuf_tensor("sb_onesb", [128, 128], BF16))
    identf = es.enter_context(nc.sbuf_tensor("sb_identf", [128, 128], F32))
    onesf = es.enter_context(nc.sbuf_tensor("sb_onesf", [128, 128], F32))
    seed = es.enter_context(nc.sbuf_tensor("sb_seed", [128, 16], F32))
    kons = es.enter_context(nc.sbuf_tensor("kons", [128, 8], F32))
    mw1 = es.enter_context(nc.sbuf_tensor("mw1", [128, 1], F32))
    rw1 = es.enter_context(nc.sbuf_tensor("rw1", [128, 1], F32))
    mw2 = es.enter_context(nc.sbuf_tensor("mw2", [128, 1], F32))
    rw2 = es.enter_context(nc.sbuf_tensor("rw2", [128, 1], F32))
    mw2h = es.enter_context(nc.sbuf_tensor("mw2h", [128, 1], F32))
    convwt = es.enter_context(nc.sbuf_tensor("sb_convwt", [128, 96], F32))
    convbt = es.enter_context(nc.sbuf_tensor("sb_convbt", [128, 32], F32))
    smallp = es.enter_context(nc.sbuf_tensor("smallp", [128, 8], F32))


    def emit_p0():
        tok_c = []
        for dst_t, src_a in ((identb, didb), (identf, didf), (onesf, donesf),
                             (onesb, donesb),
                             (seed, dseed), (convwt, dconvw), (convbt, dconvb)):
            tok_c.append(LP.issue(em, "sync", lambda E, d=dst_t, s=src_a: E.dma_start(out=d[:], in_=s)))

        stp = mkvs(4)

        def f(E, toks=tuple(tok_c), stp=stp):
            W(E, *toks)
            S = stp(E)
            S(E.tensor_scalar(smi[:], onesf[:, 0:128], 1.5, None, OP.mult))
            S(E.tensor_scalar(smib[:], smi[:], 0.0, None, OP.add))
            S(E.tensor_scalar(smallp[:, 0:8], smib[:, 0:8], 0.0, None, OP.add))
            S(E.tensor_scalar(kons[:], smallp[:, 0:8], -0.5, 129.0, OP.mult, OP.add))
            E.engine_nop().then_inc(VV.h, 1)
        em("vector", f)
        tok_kons = VV.inc()

        ps = contextlib.ExitStack()
        smi = ps.enter_context(nc.sbuf_tensor("smi", [128, 128], I32))
        smib = ps.enter_context(nc.sbuf_tensor("smib", [128, 128], BF16))
        wst = [ps.enter_context(nc.sbuf_tensor(f"p0w{i}", [128, 4096], F32)) for i in range(2)]
        wqi = ps.enter_context(nc.sbuf_tensor("p0qi", [128, 4096], I32))
        wqf = ps.enter_context(nc.sbuf_tensor("p0qf", [128, 4096], BF16))
        wqb = ps.enter_context(nc.sbuf_tensor("p0qb", [128, 4096], BF16))
        macc = ps.enter_context(nc.sbuf_tensor("p0macc", [128, 1], F32))
        red = ps.enter_context(nc.sbuf_tensor("p0red", [128, 1], F32))
        pms = ps.enter_context(nc.psum_tensor("p0ps", [128, 512], F32))

        def quant_weight(dsrc, nchunk, chwid, nelem, mean_t, recip_t, spill_fn,
                         multi_spill=False, init_vt=(None, None), init_sp=()):
            src3 = dsrc.rearrange("(c p) w -> c p w", p=128)
            vt = [init_vt[0], init_vt[1]]
            for c in range(nchunk):
                b = wst[c % 2]
                tok_l = LP.issue(em, "sync",
                                 lambda E, b=b, c=c: E.dma_start(out=b[:, 0:chwid], in_=src3[c]),
                                 pre_toks=(vt[c % 2],))

                stp = mkvs(1 if c == 0 else 2)

                def f(E, c=c, b=b, tok=tok_l, stp=stp):
                    W(E, tok)
                    S = stp(E)
                    if c == 0:
                        S(E.reduce_sum(macc[:], b[:, 0:chwid], AX.X,
                                       apply_absolute_value=True))
                    else:
                        S(E.reduce_sum(red[:], b[:, 0:chwid], AX.X, apply_absolute_value=True))
                        S(E.tensor_tensor(macc[:], macc[:], red[:], OP.add))
                    E.engine_nop().then_inc(VV.h, 1)
                em("vector", f)
                vt[c % 2] = VV.inc()
            tok_acc = vt[(nchunk - 1) % 2]

            def f(E, tok=tok_acc):
                W(E, tok)
                E.matmul(pms[0:1, 0:1], onesf[:, 0:1], macc[:], start=True,
                         stop=True).then_inc(TT.h, 1)
            em("tensor", f)
            tok_mm = TT.inc()

            def f(E, tok=tok_mm):
                W(E, tok)
                E.tensor_scalar(smallp[0:1, 0:1], pms[0:1, 0:1], 0.0, None, OP.add).then_inc(VV.h, 1)
            em("vector", f)
            tok_ev = VV.inc()

            def f(E, tok=tok_ev):
                W(E, tok)
                E.matmul(pms[:, 0:1], onesf[0:1, :], smallp[0:1, 0:1], start=True,
                         stop=True).then_inc(TT.h, 1)
            em("tensor", f)
            tok_bc = TT.inc()

            stp = mkvs(2)

            def f(E, tok=tok_bc, mean_t=mean_t, recip_t=recip_t, nelem=nelem, stp=stp):
                W(E, tok)
                S = stp(E)
                S(E.tensor_scalar(mean_t[:], pms[:, 0:1], 1.0 / nelem, EPS_CLIP, OP.mult, OP.max))
                S(E.reciprocal(recip_t[:], mean_t[:]))
                E.engine_nop().then_inc(VV.h, 1)
            em("vector", f)
            tok_scale = VV.inc()

            sp_prev = list(init_sp)
            vt2 = [tok_acc, tok_acc]
            for c in range(nchunk):
                b = wst[c % 2]
                tok_l = LP.issue(em, "sync",
                                 lambda E, b=b, c=c: E.dma_start(out=b[:, 0:chwid], in_=src3[c]),
                                 pre_toks=(vt2[c % 2],))

                stpq = mkvs(3)

                def f(E, c=c, b=b, tok=tok_l, tok3=tuple(sp_prev), recip_t=recip_t, stpq=stpq):
                    W(E, tok, tok_scale, tok_kons, *tok3)
                    S = stpq(E)
                    S(E.tensor_scalar(wqi[:, 0:chwid], b[:, 0:chwid], recip_t[:, 0:1],
                                      kons[:, 0:1], OP.mult, OP.add))
                    S(E.tensor_scalar(wqf[:, 0:chwid], wqi[:, 0:chwid], -128.0, None, OP.add))
                    S(E.tensor_scalar(wqf[:, 0:chwid], wqf[:, 0:chwid], 1.0, None, OP.min))
                    E.tensor_scalar(wqb[:, 0:chwid], wqf[:, 0:chwid], -1.0, None,
                                    OP.max).then_inc(VV.h, 1)
                em("vector", f)
                vt2[c % 2] = VV.inc()
                sp_prev = []
                if multi_spill:
                    for g in range(4):
                        sp_prev.append(OP_.issue(
                            em, "gpsimd",
                            lambda E, c=c, g=g: E.dma_start(
                                out=dw1q[g, :, 1024 * c:1024 * (c + 1)],
                                in_=wqb[:, 1024 * g:1024 * (g + 1)]),
                            pre_toks=(vt2[c % 2],)))
                else:
                    sp_prev.append(OP_.issue(
                        em, "gpsimd",
                        lambda E, c=c: E.dma_start(out=spill_fn(c), in_=wqb[:, 0:chwid]),
                        pre_toks=(vt2[c % 2],)))
            return tok_scale, (tuple(vt2), tuple(sp_prev))

        tok_w1scale, (w1_vt, w1_sp) = quant_weight(
            dw1t, 8, 4096, D * HID, mw1, rw1, None, multi_spill=True)
        tok_w1sp = tuple(OP_.all_tok())
        tok_w2scale, _ = quant_weight(
            dw2t, 32, 1024, D * HID, mw2, rw2,
            lambda c: dw2q[c * 128:(c + 1) * 128, :],
            init_vt=w1_vt, init_sp=w1_sp)
        tok_w2sp = tuple(OP_.all_tok())

        stp = mkvs(2)

        def f(E, stp=stp, tokc=tuple(tok_c)):
            W(E, tok_w2scale, tok_w1scale, *tokc)
            S = stp(E)
            S(E.tensor_scalar(convwt[:], convwt[:], mw1[:, 0:1], None, OP.mult))
            S(E.tensor_scalar(mw2h[:], mw2[:], 0.5 / 127.0, None, OP.mult))
            E.engine_nop().then_inc(VV.h, 1)
        em("vector", f)
        tok_taps = VV.inc()
        em.flush()
        ps.close()
        return tok_taps, tok_w1sp, tok_w2sp, tok_kons

    tok_taps, tok_w1sp, tok_w2sp, tok_kons = emit_p0()

    def fence():
        toks = [TT.tok(), VV.tok(), SS.tok()] + LP.all_tok() + OP_.all_tok()
        for e in ENGS:
            def f(E, toks=tuple(toks)):
                for t in toks:
                    if t[1] > 0:
                        E.wait_ge(t[0], t[1])
            em(e, f)
        em.flush()

    def emit_ffn(src, dst, tag):
        ps = contextlib.ExitStack()
        w2q = [ps.enter_context(nc.sbuf_tensor(f"F{tag}w2q{c}", [128, D], BF16)) for c in range(32)]
        w1s = [ps.enter_context(nc.sbuf_tensor(f"F{tag}w1s{g}", [128, 8192], BF16)) for g in range(2)]
        nT = [[ps.enter_context(nc.sbuf_tensor(f"F{tag}nT{b}_{c}", [128, 514], BF16))
               for c in range(8)] for b in range(2)]
        aAr = ps.enter_context(nc.sbuf_tensor(f"F{tag}aA", [128, 32 * 512], BF16))
        xsl = [ps.enter_context(nc.sbuf_tensor(f"F{tag}xsl{i}", [128, 2048], F32)) for i in range(2)]
        xrr = [ps.enter_context(nc.sbuf_tensor(f"F{tag}xrr{i}", [128, 1024], F32)) for i in range(4)]
        nI = [ps.enter_context(nc.sbuf_tensor(f"F{tag}nI0", [128, 1024], I32))] * 2
        nB = [ps.enter_context(nc.sbuf_tensor(f"F{tag}nB{i}", [128, 1024], BF16)) for i in range(2)]
        csc = [ps.enter_context(nc.sbuf_tensor(f"F{tag}csc{i}", [128, 512], F32)) for i in range(4)]
        nis = ps.enter_context(nc.sbuf_tensor(f"F{tag}nis", [128, 512], I32))
        amx = ps.enter_context(nc.sbuf_tensor(f"F{tag}amx", [128, 512], F32))
        s2r = ps.enter_context(nc.sbuf_tensor(f"F{tag}s2r", [128, 128], F32))
        bl2e = [ps.enter_context(nc.sbuf_tensor(f"F{tag}bl2e{i}", [128, 512], F32)) for i in range(2)]
        bns = ps.enter_context(nc.sbuf_tensor(f"F{tag}bns", [128, 12 * NT], F32))
        smax = ps.enter_context(nc.sbuf_tensor(f"F{tag}smax", [128, NT], F32))
        smin = ps.enter_context(nc.sbuf_tensor(f"F{tag}smin", [128, NT], F32))
        stw = ps.enter_context(nc.sbuf_tensor(f"F{tag}stw", [128, 8 * NT], F32))
        alf = ps.enter_context(nc.sbuf_tensor(f"F{tag}alf", [128, NT], F32))
        bet = ps.enter_context(nc.sbuf_tensor(f"F{tag}bet", [128, NT], F32))
        bia = ps.enter_context(nc.sbuf_tensor(f"F{tag}bia", [128, NT], F32))
        s2t = ps.enter_context(nc.sbuf_tensor(f"F{tag}s2t", [128, 4], F32))
        g2t = ps.enter_context(nc.sbuf_tensor(f"F{tag}g2t", [128, 4], F32))

        hpsA = ps.enter_context(nc.psum_tensor(f"F{tag}hA", [128, 1024], F32))
        hpsB = ps.enter_context(nc.psum_tensor(f"F{tag}hB", [128, 1024], F32))
        ntp = ps.enter_context(nc.psum_tensor(f"F{tag}ntp", [128, 1024], BF16))
        bl2p = ps.enter_context(nc.psum_tensor(f"F{tag}bl2p", [128, 1024], F32))
        pmis = ps.enter_context(nc.psum_tensor(f"F{tag}pmis", [128, 512], F32))
        hps = [hpsA, hpsB]

        x3 = src.rearrange("s (k p) d -> s k p d", p=128)
        y3 = dst.rearrange("s (k p) d -> s k p d", p=128)

        tok_w2l = []
        for c in range(32):
            tok_w2l.append(LP.issue(
                em, "sync",
                lambda E, c=c: E.dma_start(out=w2q[c][:], in_=dw2q[c * 128:(c + 1) * 128, :]),
                pre_toks=tok_w2sp))
        toks_w2ld = tuple(tok_w2l)

        st = {
            "xsl_free": [None, None],
            "nIB_free": [None, None],
            "nT_bl1_done": [None, None],
            "halo_done": {},
            "xside_done": {},
            "conv": [],
            "silu": [],
            "quant_done": None,
            "bl2_done": None,
            "last_stt": None,
            "amx_T": None,
            "w1_used": [None, None],
            "xrr_tok": [None, None, None, None],
            "xout_done": [None, None, None, None],
        }

        def emit_stats(s):
            vt = [None, None]
            for k in range(NT):
                b = k % 2
                tok_l = LP.issue(em, "sync",
                                 lambda E, s=s, k=k, b=b: E.dma_start(out=xsl[b][:, 0:1024], in_=x3[s, k]),
                                 pre_toks=(vt[b], st["xsl_free"][b]))

                def f(E, k=k, b=b, tok=tok_l):
                    W(E, tok)
                    E.bn_stats(bns[:, 12 * k:12 * k + 6], xsl[b][:, 0:512])
                    E.bn_stats(bns[:, 12 * k + 6:12 * k + 12], xsl[b][:, 512:1024])
                    E.reduce_max(smax[:, k:k + 1], xsl[b][:, 0:1024], AX.X)
                    E.tensor_reduce(smin[:, k:k + 1], xsl[b][:, 0:1024], AX.X, OP.min).then_inc(VV.h, 1)
                em("vector", f)
                vt[b] = VV.inc()
            st["xsl_free"] = [vt[0], vt[1]]
            tok_stats = vt[(NT - 1) % 2]

            stp = mkvs(12)

            def f(E, tok=tok_stats, stp=stp):
                W(E, tok, tok_kons)
                S = stp(E)
                me = bns[:, 1::3]
                m2 = bns[:, 2::3]
                R0, R1, R2 = stw[:, 0:2 * NT], stw[:, 2 * NT:3 * NT], stw[:, 3 * NT:4 * NT]
                R3, R4, R5 = stw[:, 4 * NT:5 * NT], stw[:, 5 * NT:6 * NT], stw[:, 6 * NT:7 * NT]
                S(E.tensor_tensor(R0, me[:, 0::2], me[:, 1::2], OP.add))
                S(E.tensor_tensor(R1, R0[:, 0::2], R0[:, 1::2], OP.add))
                S(E.tensor_tensor(R0, m2[:, 0::2], m2[:, 1::2], OP.add))
                S(E.tensor_tensor(R2, R0[:, 0::2], R0[:, 1::2], OP.add))
                S(E.tensor_tensor(stw[:, 4 * NT:8 * NT], me, me, OP.mult))
                S(E.tensor_tensor(R0, stw[:, 4 * NT:8 * NT][:, 0::2],
                                  stw[:, 4 * NT:8 * NT][:, 1::2], OP.add))
                S(E.tensor_tensor(R3, R0[:, 0::2], R0[:, 1::2], OP.add))
                S(E.tensor_scalar(R4, R1, 0.25, None, OP.mult))
                S(E.tensor_scalar(R5, R2, 1.0 / 1024.0, None, OP.mult))
                S(E.scalar_tensor_tensor(R5, R3, 0.25, R5, OP.mult, OP.add))
                S(E.tensor_tensor(R0[:, 0:NT], R4, R4, OP.mult))
                S(E.scalar_tensor_tensor(R5, R0[:, 0:NT], -1.0, R5, OP.mult, OP.add))
                E.engine_nop().then_inc(VV.h, 1)
            em("vector", f)
            tok_var = VV.inc()
            sstp = mkss(1)

            def f(E, tok=tok_var, sstp=sstp):
                W(E, tok)
                S = sstp(E)
                S(E.activation(stw[:, 0:NT], stw[:, 6 * NT:7 * NT], ACT.Ln,
                               bias=seed[:, 8:9], scale=1.0))
                E.activation(stw[:, 7 * NT:8 * NT], stw[:, 0:NT], ACT.Exp,
                             bias=0.0, scale=-0.5).then_inc(SS.h, 1)
            em("scalar", f)
            tok_rln = SS.inc()
            stp2 = mkvs(10)

            def f(E, tok=tok_rln, stp=stp2):
                W(E, tok)
                S = stp(E)
                mean = stw[:, 5 * NT:6 * NT]
                rln = stw[:, 7 * NT:8 * NT]
                t0 = stw[:, 0:NT]
                t1 = stw[:, NT:2 * NT]
                t2 = stw[:, 2 * NT:3 * NT]
                S(E.tensor_tensor(t0, smax[:], mean, OP.subtract))
                S(E.tensor_tensor(t1, mean, smin[:], OP.subtract))
                S(E.tensor_tensor(t0, t0, t1, OP.max))
                S(E.tensor_tensor(t1, t0, rln, OP.mult))
                S(E.tensor_scalar(t1, t1, EPS_CLIP, None, OP.max))
                S(E.reciprocal(t2, t1))
                S(E.scalar_tensor_tensor(alf[:], rln, 127.0, t2, OP.mult, OP.mult))
                S(E.tensor_scalar(bet[:], t1, 1.0 / 127.0, None, OP.mult))
                S(E.tensor_tensor(t0, alf[:], mean, OP.mult))
                S(E.tensor_scalar(bia[:], t0, -1.0, kons[:, 0:1], OP.mult, OP.add))
                E.engine_nop().then_inc(VV.h, 1)
            em("vector", f)
            return VV.inc()

        tok_ab = {}

        def emit_xside(u):
            s, b = divmod(u, NSLAB)
            bb = u % 2
            toks_half = [None, None]
            for h2 in range(2):
                t0 = LP.issue(
                    em, "sync",
                    lambda E, s=s, b=b, h2=h2: E.dma_start(
                        out=xsl[h2][:, 0:1024], in_=x3[s, 4 * b + 2 * h2]),
                    pre_toks=(st["xsl_free"][h2],))
                t1 = LP.issue(
                    em, "sync",
                    lambda E, s=s, b=b, h2=h2: E.dma_start(
                        out=xsl[h2][:, 1024:2048], in_=x3[s, 4 * b + 2 * h2 + 1]))
                toks_half[h2] = (t0, t1)
            last_e = None
            for j in range(4):
                k = 4 * b + j
                jb = j % 2

                def f(E, k=k, j=j, jb=jb, toks=toks_half[j // 2],
                      tok2=tok_ab[s], tokf=st["nIB_free"][jb], tokf2=st["nIB_free"][1 - jb]):
                    W(E, toks[0], toks[1], tok2, tokf, tokf2)
                    E.activation(nI[jb][:], xsl[j // 2][:, 1024 * (j % 2):1024 * (j % 2 + 1)],
                                 ACT.Identity, bias=bia[:, k:k + 1],
                                 scale=alf[:, k:k + 1]).then_inc(SS.h, 1)
                em("scalar", f)
                tok_s = SS.inc()

                stpn = mkvs(1)

                def f(E, k=k, jb=jb, tok=tok_s, stpn=stpn):
                    W(E, tok)
                    S = stpn(E)
                    S(E.tensor_scalar(nB[jb][:], nI[jb][:], -128.0, None, OP.add))
                    E.tensor_scalar(nB[jb][:], nB[jb][:], bet[:, k:k + 1], None,
                                    OP.mult).then_inc(VV.h, 1)
                em("vector", f)
                tok_v = VV.inc()

                def f(E, jb=jb, tok=tok_v, tokp=last_e):
                    W(E, tok, tokp)
                    ins = None
                    for c in range(8):
                        ins = E.transpose(ntp[:, 128 * c:128 * (c + 1)],
                                          nB[jb][:, 128 * c:128 * (c + 1)], identb[:])
                    ins.then_inc(TT.h, 1)
                em("tensor", f)
                tok_t = TT.inc()

                def f(E, b=b, bb=bb, j=j, tok=tok_t, tokbl1=st["nT_bl1_done"][bb]):
                    W(E, tok, tokbl1)
                    ins = None
                    for c in range(8):
                        ins = E.tensor_scalar(nT[bb][c][:, 1 + 128 * j:1 + 128 * (j + 1)],
                                              ntp[:, 128 * c:128 * (c + 1)], 0.0, None, OP.add)
                    if j == 0:
                        if b > 0:
                            for c in range(8):
                                E.tensor_scalar(nT[1 - bb][c][:, 513:514],
                                                ntp[:, 128 * c:128 * c + 1], 0.0, None, OP.add)
                            for c in range(8):
                                ins = E.tensor_scalar(nT[bb][c][:, 0:1],
                                                      nT[1 - bb][c][:, 512:513], 0.0, None, OP.add)
                        else:
                            for c in range(8):
                                ins = E.memset(nT[bb][c][:, 0:1], 0.0)
                    if j == 3 and b == NSLAB - 1:
                        for c in range(8):
                            ins = E.memset(nT[bb][c][:, 513:514], 0.0)
                    ins.then_inc(VV.h, 1)
                em("vector", f)
                last_e = VV.inc()
                st["nIB_free"][jb] = last_e
                if j == 0 and b > 0:
                    st["halo_done"][u - 1] = last_e
                if j == 1:
                    st["xsl_free"][0] = last_e
                if j == 3:
                    st["xsl_free"][1] = last_e
            st["xside_done"][u] = last_e
            if b == NSLAB - 1:
                st["halo_done"][u] = last_e

        def emit_slab(u):
            s, b = divmod(u, NSLAB)
            bb = u % 2
            tok_w1g = [None] * 4
            for g in range(2):
                tok_w1g[g] = LP.issue(
                    em, "sync",
                    lambda E, g=g: E.dma_start(out=w1s[g % 2][:], in_=dw1q[g]),
                    pre_toks=(st["w1_used"][g % 2],) + tok_w1sp)

            if debug and u == 0 and src is dx:
                OP_.issue(em, "gpsimd", lambda E: E.dma_start(out=ddbg["w1s0"], in_=w1s[0][:]),
                          pre_toks=(tok_w1g[0],))
            nm0 = len(st["conv"])
            tok_am = None
            for m in range(32):
                g = m // 8
                gb = g % 2
                mc = m % 8
                hp = hps[m % 2]
                gi = nm0 + m
                tok_hp_free = st["conv"][gi - 2] if gi >= 2 else None
                if mc == 0 and g >= 2:
                    tok_w1g[g] = LP.issue(
                        em, "sync",
                        lambda E, g=g: E.dma_start(out=w1s[g % 2][:], in_=dw1q[g]),
                        pre_toks=(st["w1_used"][g % 2],) + tok_w1sp)

                def f(E, bb=bb, m=m, gb=gb, mc=mc, hp=hp, tokn=st["xside_done"][u],
                      tokh=st["halo_done"].get(u), tokw=tok_w1g[g], tokf=tok_hp_free):
                    W(E, tokn, tokh, tokw, tokf)
                    ins = None
                    for kf in range(8):
                        lhs = w1s[gb][:, 1024 * kf + 128 * mc:1024 * kf + 128 * (mc + 1) - 0]
                        E.matmul(hp[:, 0:512], lhs, nT[bb][kf][:, 0:512],
                                 start=(kf == 0), stop=(kf == 7))
                        ins = E.matmul(hp[:, 512:514], lhs, nT[bb][kf][:, 512:514],
                                       start=(kf == 0), stop=(kf == 7))
                    ins.then_inc(TT.h, 1)
                em("tensor", f)
                tok_bl1m = TT.inc()
                if mc == 7:
                    st["w1_used"][gb] = tok_bl1m
                if m == 31:
                    st["nT_bl1_done"][bb] = tok_bl1m

                c0 = csc[2 * (m % 2)]
                c1 = csc[2 * (m % 2) + 1]

                stpc = mkvs(2)

                def f(E, m=m, hp=hp, c0=c0, c1=c1, tok=tok_bl1m, stpc=stpc,
                      tokf=st["silu"][gi - 2] if gi >= 2 else None):
                    W(E, tok, tok_taps, tokf)
                    S = stpc(E)
                    S(E.tensor_scalar(c0[:], hp[:, 1:513], convwt[:, 3 * m + 1:3 * m + 2],
                                      convbt[:, m:m + 1], OP.mult, OP.add))
                    S(E.scalar_tensor_tensor(c1[:], hp[:, 0:512], convwt[:, 3 * m:3 * m + 1],
                                             c0[:], OP.mult, OP.add))
                    E.scalar_tensor_tensor(c0[:], hp[:, 2:514], convwt[:, 3 * m + 2:3 * m + 3],
                                           c1[:], OP.mult, OP.add).then_inc(VV.h, 1)
                em("vector", f)
                tok_conv = VV.inc()
                st["conv"].append(tok_conv)
                if debug and u == 0 and m == 0 and src is dx:
                    def fdbg(E, hp=hp, c0=c0, tok=tok_bl1m, tok2=tok_conv):
                        W(E, tok, tok2)
                        E.tensor_scalar(bl2e[0][:, 0:512], hp[:, 0:512], 0.0, None, OP.add)
                        E.tensor_scalar(bl2e[1][:, 0:2], hp[:, 512:514], 0.0, None, OP.add)
                        E.tensor_scalar(bl2e[1][:, 2:512], c0[:, 0:510], 0.0, None, OP.add).then_inc(VV.h, 1)
                    em("vector", fdbg)
                    tok_d = VV.inc()
                    OP_.issue(em, "gpsimd", lambda E: E.dma_start(out=ddbg["hp0"][:, 0:512], in_=bl2e[0][:, 0:512]),
                              pre_toks=(tok_d,))
                    OP_.issue(em, "gpsimd", lambda E: E.dma_start(out=ddbg["hp0"][:, 512:514], in_=bl2e[1][:, 0:2]),
                              pre_toks=(tok_d,))
                    tok_dd = OP_.issue(em, "gpsimd", lambda E: E.dma_start(out=ddbg["cv0"][:, 0:510], in_=bl2e[1][:, 2:512]),
                              pre_toks=(tok_d,))
                    st["bl2_done"] = tok_dd
                    st["conv"][-1] = tok_d

                def f(E, m=m, c0=c0, tok=tok_conv, tokq=st["quant_done"]):
                    W(E, tok, tokq)
                    E.activation(aAr[:, 512 * m:512 * (m + 1)], c0[:], ACT.Silu).then_inc(SS.h, 1)
                em("scalar", f)
                tok_sl = SS.inc()
                st["silu"].append(tok_sl)

            tok_amax = tok_sl

            def f(E, tok=tok_amax, toka=st["amx_T"]):
                W(E, tok, toka)
                E.tensor_reduce(amx[:], aAr[:].rearrange("p (c w) -> p w c", c=32),
                                AX.X, OP.max, apply_absolute_value=True).then_inc(VV.h, 1)
            em("vector", f)
            tok_amax = VV.inc()

            if debug and u == 0 and src is dx:
                OP_.issue(em, "gpsimd", lambda E: E.dma_start(out=ddbg["nT0"], in_=nT[0][0][:]),
                          pre_toks=(st["xside_done"][0], st["halo_done"].get(0)))
                OP_.issue(em, "gpsimd", lambda E: E.dma_start(out=ddbg["apre"], in_=aAr[:]),
                          pre_toks=(tok_amax,))

            def f(E, tok=tok_amax, tokp=st["last_stt"]):
                W(E, tok, tokp)
                ins = None
                for j in range(4):
                    ins = E.transpose(pmis[:, 128 * j:128 * (j + 1)],
                                      amx[:, 128 * j:128 * (j + 1)], identf[:])
                ins.then_inc(TT.h, 1)
            em("tensor", f)
            st["amx_T"] = TT.inc()
            tok_amT = st["amx_T"]

            stp_s2 = mkvs(5)

            def f(E, tok=tok_amT, stp=stp_s2):
                W(E, tok)
                S = stp(E)
                for j in range(4):
                    ins = E.reduce_max(s2t[:, j:j + 1], pmis[:, 128 * j:128 * (j + 1)], AX.X)
                S(ins)
                S(E.tensor_scalar(s2t[:], s2t[:], EPS_CLIP, None, OP.max))
                S(E.tensor_scalar(g2t[:], s2t[:], mw2h[:, 0:1], None, OP.mult))
                S(E.reciprocal(s2t[:], s2t[:]))
                S(E.tensor_scalar(s2t[:], s2t[:], 127.0, None, OP.mult))
                E.engine_nop().then_inc(VV.h, 1)
            em("vector", f)
            tok_s2 = VV.inc()

            tok_rank1 = None
            for j in range(4):
                def f(E, j=j, tok=tok_s2, tokp=tok_rank1):
                    W(E, tok, tokp)
                    E.transpose(pmis[0:1, 128 * j:128 * (j + 1)], s2t[:, j:j + 1],
                                identf[:]).then_inc(TT.h, 1)
                em("tensor", f)
                tok_jT = TT.inc()

                def f(E, j=j, tok=tok_jT):
                    W(E, tok)
                    E.tensor_scalar(s2r[0:1, 0:128], pmis[0:1, 128 * j:128 * (j + 1)],
                                    0.0, None, OP.add).then_inc(VV.h, 1)
                em("vector", f)
                tok_row = VV.inc()

                def f(E, j=j, tok=tok_row):
                    W(E, tok)
                    E.matmul(pmis[:, 128 * j:128 * (j + 1)], onesf[0:1, :],
                             s2r[0:1, 0:128], start=True, stop=True).then_inc(TT.h, 1)
                em("tensor", f)
                tok_rank1 = TT.inc()
            tok_s2b = tok_rank1

            tok_q = None
            for m in range(32):
                stpz = mkvs(2)

                def f(E, m=m, tok=tok_s2b, stpz=stpz):
                    W(E, tok)
                    S = stpz(E)
                    acol = aAr[:, 512 * m:512 * (m + 1)]
                    S(E.tensor_tensor(csc[0][:], acol, pmis[:, 0:512], OP.mult))
                    S(E.tensor_scalar(nis[:], csc[0][:], kons[:, 0:1], None, OP.add))
                    E.tensor_scalar(acol, nis[:], -128.0, None, OP.add).then_inc(VV.h, 1)
                em("vector", f)
                tok_q = VV.inc()
            st["quant_done"] = tok_q
            if debug and u == 0 and src is dx:
                OP_.issue(em, "gpsimd", lambda E: E.dma_start(out=ddbg["amx"], in_=amx[:]),
                          pre_toks=(tok_amax,))
                OP_.issue(em, "gpsimd", lambda E: E.dma_start(out=ddbg["s2t"], in_=s2t[:]),
                          pre_toks=(tok_q,))
                OP_.issue(em, "gpsimd", lambda E: E.dma_start(out=ddbg["g2t"], in_=g2t[:]),
                          pre_toks=(tok_q,))
                OP_.issue(em, "gpsimd", lambda E: E.dma_start(out=ddbg["apost"], in_=aAr[:]),
                          pre_toks=(tok_q,))

            for j in range(4):
                st["xrr_tok"][j] = LP.issue(
                    em, "sync",
                    lambda E, s=s, b=b, j=j: E.dma_start(out=xrr[j][:], in_=x3[s, 4 * b + j]),
                    pre_toks=(st["xout_done"][j],))
            toks_xr = tuple(st["xrr_tok"])

            tok_prev_pair_ev = None
            tok_stt_last = st["last_stt"]
            for p in range(4):
                def f(E, p=p, tok=tok_q, toke=tok_prev_pair_ev,
                      tokb=st["bl2_done"] if p == 0 else None):
                    W(E, tok, toke, tokb, *toks_w2ld)
                    ins = None
                    for half in range(2):
                        d = 2 * p + half
                        for c in range(32):
                            ins = E.matmul(bl2p[:, 512 * half:512 * (half + 1)],
                                           w2q[c][:, 128 * d:128 * (d + 1)],
                                           aAr[:, 512 * c:512 * (c + 1)],
                                           start=(c == 0), stop=(c == 31))
                    ins.then_inc(TT.h, 1)
                em("tensor", f)
                tok_mm = TT.inc()

                def f(E, tok=tok_mm):
                    W(E, tok)
                    E.tensor_scalar(bl2e[0][:], bl2p[:, 0:512], 0.0, None, OP.add)
                    E.tensor_scalar(bl2e[1][:], bl2p[:, 512:1024], 0.0, None, OP.add).then_inc(VV.h, 1)
                em("vector", f)
                tok_ev = VV.inc()
                tok_prev_pair_ev = tok_ev
                if debug and u == 0 and p == 0 and src is dx:
                    OP_.issue(em, "gpsimd", lambda E: E.dma_start(out=ddbg["bl2e0"], in_=bl2e[0][:]),
                              pre_toks=(tok_ev,))

                for half in range(2):
                    d = 2 * p + half

                    def f(E, half=half, tok=tok_ev, tokq2=tok_q, tokst=tok_stt_last):
                        W(E, tok, tokq2, tokst)
                        ins = None
                        for j in range(4):
                            ins = E.transpose(pmis[:, 128 * j:128 * (j + 1)],
                                              bl2e[half][:, 128 * j:128 * (j + 1)], identf[:])
                        ins.then_inc(TT.h, 1)
                    em("tensor", f)
                    tok_tr = TT.inc()

                    def f(E, d=d, tok=tok_tr, tokx=toks_xr):
                        W(E, tok, *tokx)
                        ins = None
                        for j in range(4):
                            ins = E.scalar_tensor_tensor(xrr[j][:, 128 * d:128 * (d + 1)],
                                                         pmis[:, 128 * j:128 * (j + 1)],
                                                         g2t[:, j:j + 1],
                                                         xrr[j][:, 128 * d:128 * (d + 1)],
                                                         OP.mult, OP.add)
                        ins.then_inc(VV.h, 1)
                    em("vector", f)
                    tok_stt_last = VV.inc()
            st["bl2_done"] = tok_prev_pair_ev
            st["last_stt"] = tok_stt_last

            for j in range(4):
                st["xout_done"][j] = OP_.issue(
                    em, "gpsimd",
                    lambda E, s=s, b=b, j=j: E.dma_start(out=y3[s, 4 * b + j], in_=xrr[j][:]),
                    pre_toks=(tok_stt_last,))

        for s in range(NSEQ):
            tok_ab[s] = emit_stats(s)
            if debug and s == 0 and src is dx:
                for nm, t in [("kons", kons), ("alf", alf), ("bet", bet), ("bia", bia)]:
                    def fd(E, nm=nm, t=t, tok=tok_ab[s]):
                        W(E, tok)
                        E.dma_start(out=ddbg[nm][:, 0:t.shape[1] if False else None] if False else ddbg[nm][:, 0:NT] if nm != "kons" else ddbg[nm],
                                    in_=t[:]).then_inc(SS.h, 0) if False else None
                    pass
                OP_.issue(em, "gpsimd", lambda E: E.dma_start(out=ddbg["kons"], in_=kons[:, 0:1]), pre_toks=(tok_ab[s],))
                OP_.issue(em, "gpsimd", lambda E: E.dma_start(out=ddbg["alf"][:, 0:NT], in_=alf[:]), pre_toks=(tok_ab[s],))
                OP_.issue(em, "gpsimd", lambda E: E.dma_start(out=ddbg["bet"][:, 0:NT], in_=bet[:]), pre_toks=(tok_ab[s],))
                OP_.issue(em, "gpsimd", lambda E: E.dma_start(out=ddbg["bia"][:, 0:NT], in_=bia[:]), pre_toks=(tok_ab[s],))
                OP_.issue(em, "gpsimd", lambda E: E.dma_start(out=ddbg["stw"], in_=stw[:]), pre_toks=(tok_ab[s],))
                OP_.issue(em, "gpsimd", lambda E: E.dma_start(out=ddbg["bns"], in_=bns[:]), pre_toks=(tok_ab[s],))
                OP_.issue(em, "gpsimd", lambda E: E.dma_start(out=ddbg["smax"][:, 0:NT], in_=smax[:]), pre_toks=(tok_ab[s],))
                OP_.issue(em, "gpsimd", lambda E: E.dma_start(out=ddbg["smin"][:, 0:NT], in_=smin[:]), pre_toks=(tok_ab[s],))
            base = s * NSLAB
            emit_xside(base)
            if NSLAB > 1:
                emit_xside(base + 1)
            for b in range(NSLAB):
                emit_slab(base + b)
                if b + 2 < NSLAB:
                    emit_xside(base + b + 2)
        em.flush()
        ps.close()

    def emit_attn(src, dst, upto=99, nheads=NH):
        NQB = TS // 512
        ps = contextlib.ExitStack()
        gkvt = ps.enter_context(nc.sbuf_tensor("Agkv", [128, 2], F32))
        tok_g = LP.issue(em, "sync", lambda E: E.dma_start(out=gkvt[:], in_=dgkv))
        em.flush()

        x3 = src.rearrange("s (k p) d -> s k p d", p=128)
        y3 = dst.rearrange("s (k p) d -> s k p d", p=128)
        SC96 = float(1.0 / np.sqrt(96.0))

        def bail(tok, s):
            for k in range(NT):
                OP_.issue(em, "gpsimd",
                          lambda E, s=s, k=k: E.dma_start(out=y3[s, k], in_=x3[s, k]),
                          pre_toks=(tok,))
            em.flush()

        def mk_ldcast(wdft):
            state = {"t": None}

            def ldcast(dsts, dsrc, width, scale):
                for c in range(len(dsts)):
                    tok_l = LP.issue(em, "sync",
                                     lambda E, c=c, dsrc=dsrc, width=width: E.dma_start(
                                         out=wdft[:, 0:width], in_=dsrc[c * 128:(c + 1) * 128, :]),
                                     pre_toks=(state["t"],))

                    def f(E, c=c, dsts=dsts, width=width, scale=scale, tok=tok_l):
                        W(E, tok)
                        E.tensor_scalar(dsts[c][:, 0:width], wdft[:, 0:width], scale,
                                        None, OP.mult).then_inc(VV.h, 1)
                    em("vector", f)
                    state["t"] = VV.inc()
                return state["t"]
            return ldcast

        for s in range(NSEQ):
            R00 = contextlib.ExitStack()
            rlnA = R00.enter_context(nc.sbuf_tensor(f"ArlnA{s}", [128, NT], F32, side="right"))
            biaA = R00.enter_context(nc.sbuf_tensor(f"AbiaA{s}", [128, NT], F32, side="right"))
            R0 = contextlib.ExitStack()
            kvr = [R0.enter_context(nc.sbuf_tensor(f"Akvr{s}_{c}", [128, TS], BF16, side="right"))
                   for c in range(2)]
            pmid = contextlib.ExitStack()

            R1 = contextlib.ExitStack()
            uT = [R1.enter_context(nc.sbuf_tensor(f"AuT{s}_{c}", [128, TS], BF16, side="right"))
                  for c in range(8)]
            R1a = contextlib.ExitStack()
            xtl = [R1a.enter_context(nc.sbuf_tensor(f"Axt{s}_{i}", [128, 1024], F32, side="right"))
                   for i in range(2)]
            uB = [R1a.enter_context(nc.sbuf_tensor(f"AuB{s}_{i}", [128, 1024], BF16, side="right"))
                  for i in range(2)]
            bnsA = R1a.enter_context(nc.sbuf_tensor(f"Abns{s}", [128, 12 * NT], F32, side="right"))
            stwA = R1a.enter_context(nc.sbuf_tensor(f"Astw{s}", [128, 8 * NT], F32, side="right"))
            psA = contextlib.ExitStack()
            ptpB = psA.enter_context(nc.psum_tensor(f"AptpB{s}", [128, 1024], BF16))

            vt = [None, None]
            for k in range(NT):
                bq = k % 2
                tok_l = LP.issue(em, "sync",
                                 lambda E, s=s, k=k, bq=bq: E.dma_start(out=xtl[bq][:], in_=x3[s, k]),
                                 pre_toks=(vt[bq],))

                def f(E, k=k, bq=bq, tok=tok_l):
                    W(E, tok)
                    E.bn_stats(bnsA[:, 12 * k:12 * k + 6], xtl[bq][:, 0:512])
                    E.bn_stats(bnsA[:, 12 * k + 6:12 * k + 12], xtl[bq][:, 512:1024]).then_inc(VV.h, 1)
                em("vector", f)
                vt[bq] = VV.inc()
            tok_stats = vt[(NT - 1) % 2]

            stp = mkvs(12)

            def f(E, tok=tok_stats, stp=stp):
                W(E, tok)
                S = stp(E)
                me = bnsA[:, 1::3]
                m2 = bnsA[:, 2::3]
                R0_, R1_, R2_ = stwA[:, 0:2 * NT], stwA[:, 2 * NT:3 * NT], stwA[:, 3 * NT:4 * NT]
                R3_, R4_, R5_ = stwA[:, 4 * NT:5 * NT], stwA[:, 5 * NT:6 * NT], stwA[:, 6 * NT:7 * NT]
                S(E.tensor_tensor(R0_, me[:, 0::2], me[:, 1::2], OP.add))
                S(E.tensor_tensor(R1_, R0_[:, 0::2], R0_[:, 1::2], OP.add))
                S(E.tensor_tensor(R0_, m2[:, 0::2], m2[:, 1::2], OP.add))
                S(E.tensor_tensor(R2_, R0_[:, 0::2], R0_[:, 1::2], OP.add))
                S(E.tensor_tensor(stwA[:, 4 * NT:8 * NT], me, me, OP.mult))
                S(E.tensor_tensor(R0_, stwA[:, 4 * NT:8 * NT][:, 0::2],
                                  stwA[:, 4 * NT:8 * NT][:, 1::2], OP.add))
                S(E.tensor_tensor(R3_, R0_[:, 0::2], R0_[:, 1::2], OP.add))
                S(E.tensor_scalar(R4_, R1_, 0.25, None, OP.mult))
                S(E.tensor_scalar(R5_, R2_, 1.0 / 1024.0, None, OP.mult))
                S(E.scalar_tensor_tensor(R5_, R3_, 0.25, R5_, OP.mult, OP.add))
                S(E.tensor_tensor(R0_[:, 0:NT], R4_, R4_, OP.mult))
                S(E.scalar_tensor_tensor(R5_, R0_[:, 0:NT], -1.0, R5_, OP.mult, OP.add))
                E.engine_nop().then_inc(VV.h, 1)
            em("vector", f)
            tok_var = VV.inc()
            sstp = mkss(1)

            def f(E, tok=tok_var, sstp=sstp):
                W(E, tok)
                S = sstp(E)
                S(E.activation(stwA[:, 0:NT], stwA[:, 6 * NT:7 * NT], ACT.Ln,
                               bias=seed[:, 8:9], scale=1.0))
                E.activation(rlnA[:], stwA[:, 0:NT], ACT.Exp, bias=0.0,
                             scale=-0.5).then_inc(SS.h, 1)
            em("scalar", f)
            tok_rln = SS.inc()
            stp2 = mkvs(2)

            def f(E, tok=tok_rln, stp=stp2):
                W(E, tok)
                S = stp(E)
                S(E.tensor_tensor(stwA[:, 0:NT], rlnA[:], stwA[:, 5 * NT:6 * NT], OP.mult))
                S(E.tensor_scalar(biaA[:], stwA[:, 0:NT], -1.0, None, OP.mult))
                E.engine_nop().then_inc(VV.h, 1)
            em("vector", f)
            tok_biaA = VV.inc()

            last_tr = None
            for k in range(NT):
                bq = k % 2
                tok_l = LP.issue(em, "sync",
                                 lambda E, s=s, k=k, bq=bq: E.dma_start(out=xtl[bq][:], in_=x3[s, k]),
                                 pre_toks=(vt[bq],))

                def f(E, k=k, bq=bq, tok=tok_l, tok2=tok_biaA):
                    W(E, tok, tok2)
                    E.activation(uB[bq][:], xtl[bq][:], ACT.Identity,
                                 bias=biaA[:, k:k + 1], scale=rlnA[:, k:k + 1]).then_inc(SS.h, 1)
                em("scalar", f)
                tok_u = SS.inc()

                def f(E, bq=bq, tok=tok_u, tokp=last_tr):
                    W(E, tok, tokp)
                    ins = None
                    for c in range(8):
                        ins = E.transpose(ptpB[:, 128 * c:128 * (c + 1)],
                                          uB[bq][:, 128 * c:128 * (c + 1)], identb[:])
                    ins.then_inc(TT.h, 1)
                em("tensor", f)
                tok_t = TT.inc()

                def f(E, k=k, tok=tok_t):
                    W(E, tok)
                    ins = None
                    for c in range(8):
                        ins = E.tensor_scalar(uT[c][:, 128 * k:128 * (k + 1)],
                                              ptpB[:, 128 * c:128 * (c + 1)], 0.0, None, OP.add)
                    ins.then_inc(VV.h, 1)
                em("vector", f)
                last_tr = VV.inc()
                vt[bq] = last_tr
            tok_uT = last_tr
            em.flush()
            psA.close()
            R1a.close()
            fence()

            if upto <= 12:
                bail(tok_uT, s)
                R1.close(); pmid.close(); R0.close(); R00.close()
                fence()
                continue

            psB = contextlib.ExitStack()
            pj = [psB.enter_context(nc.psum_tensor(f"Apj{s}_{i}", [128, 512], F32))
                  for i in range(3)]

            R_qcb = contextlib.ExitStack()
            qcb = [R_qcb.enter_context(nc.sbuf_tensor(f"Aqcb{s}_{c}", [128, TS], BF16, side="right"))
                   for c in range(4)]
            R_qb = contextlib.ExitStack()
            wqb = [R_qb.enter_context(nc.sbuf_tensor(f"Awq{s}_{c}", [128, QR], BF16, side="right"))
                   for c in range(8)]
            wdf1 = R_qb.enter_context(nc.sbuf_tensor(f"Awdf1{s}", [128, 1024], F32, side="right"))
            tok_w = mk_ldcast(wdf1)(wqb, dwq, QR, 1.0)

            tok_qcb = None
            for m in range(4):
                for nch2 in range(NCH):
                    pp = pj[(m * NCH + nch2) % 2]

                    def f(E, m=m, nch2=nch2, pp=pp, tok=tok_uT, tokp=tok_qcb, tokw=tok_w):
                        W(E, tok, tokp, tokw)
                        ins = None
                        for kf in range(8):
                            ins = E.matmul(pp[:, 0:512], wqb[kf][:, 128 * m:128 * (m + 1)],
                                           uT[kf][:, 512 * nch2:512 * (nch2 + 1)],
                                           start=(kf == 0), stop=(kf == 7))
                        ins.then_inc(TT.h, 1)
                    em("tensor", f)
                    tok_mm = TT.inc()

                    def f(E, m=m, nch2=nch2, pp=pp, tok=tok_mm):
                        W(E, tok)
                        E.tensor_scalar(qcb[m][:, 512 * nch2:512 * (nch2 + 1)], pp[:, 0:512],
                                        0.0, None, OP.add).then_inc(VV.h, 1)
                    em("vector", f)
                    tok_qcb = VV.inc()
            tok_qcb_done = tok_qcb
            em.flush()
            R_qb.close()
            fence()

            q_aug = [pmid.enter_context(nc.sbuf_tensor(f"Aqa{s}_{h}", [97, TS], BF16, side="left"))
                     for h in range(16)]
            R_qc = contextlib.ExitStack()
            wqdb = [R_qc.enter_context(nc.sbuf_tensor(f"Awqd{s}_{c}", [128, D], BF16, side="right"))
                    for c in range(4)]
            wdf2 = R_qc.enter_context(nc.sbuf_tensor(f"Awdf2{s}", [128, 1024], F32, side="right"))
            bncA = [R_qc.enter_context(nc.sbuf_tensor(f"Abnc{s}_{i}", [128, 512], BF16, side="right"))
                    for i in range(2)]
            tok_w = mk_ldcast(wdf2)(wqdb, dwqdec, D, SC96)

            tok_qc = None
            tok_bnc = [None, None]
            for m in range(8):
                for nch2 in range(NCH):
                    pp = pj[(m * NCH + nch2) % 2]
                    bi = (m * NCH + nch2) % 2

                    def f(E, m=m, nch2=nch2, pp=pp, tok=tok_qcb_done, tokp=tok_qc, tokw=tok_w):
                        W(E, tok, tokp, tokw)
                        ins = None
                        for kf in range(4):
                            ins = E.matmul(pp[:, 0:512], wqdb[kf][:, 128 * m:128 * (m + 1)],
                                           qcb[kf][:, 512 * nch2:512 * (nch2 + 1)],
                                           start=(kf == 0), stop=(kf == 3))
                        ins.then_inc(TT.h, 1)
                    em("tensor", f)
                    tok_mm = TT.inc()

                    def f(E, m=m, nch2=nch2, pp=pp, bi=bi, tok=tok_mm, tokb=tok_bnc[bi]):
                        W(E, tok, tokb)
                        E.tensor_scalar(q_aug[2 * m][0:64, 512 * nch2:512 * (nch2 + 1)],
                                        pp[0:64, 0:512], 0.0, None, OP.add)
                        E.tensor_scalar(bncA[bi][64:128, 0:512], pp[64:128, 0:512],
                                        0.0, None, OP.add).then_inc(VV.h, 1)
                    em("vector", f)
                    tok_qc = VV.inc()
                    tok_bnc[bi] = OP_.issue(
                        em, "gpsimd",
                        lambda E, m=m, nch2=nch2, bi=bi: E.dma_start(
                            out=q_aug[2 * m + 1][0:64, 512 * nch2:512 * (nch2 + 1)],
                            in_=bncA[bi][64:128, 0:512]),
                        pre_toks=(tok_qc,))
            tok_qcont = tuple(tok_bnc) + (tok_qc,)
            em.flush()
            R_qc.close()
            fence()

            R_qr = contextlib.ExitStack()
            wqrb = [R_qr.enter_context(nc.sbuf_tensor(f"Awqr{s}_{c}", [128, 512], BF16, side="right"))
                    for c in range(4)]
            wdf3 = R_qr.enter_context(nc.sbuf_tensor(f"Awdf3{s}", [128, 512], F32, side="right"))
            cosc = [R_qr.enter_context(nc.sbuf_tensor(f"Acos{s}_{i}", [128, 512], F32, side="right"))
                    for i in range(2)]
            ssinc = [R_qr.enter_context(nc.sbuf_tensor(f"Assin{s}_{i}", [128, 512], F32, side="right"))
                     for i in range(2)]
            krf = R_qr.enter_context(nc.sbuf_tensor(f"Akrf{s}", [128, 512], F32, side="right"))
            krs = R_qr.enter_context(nc.sbuf_tensor(f"Akrs{s}", [128, 512], F32, side="right"))
            kr1 = R_qr.enter_context(nc.sbuf_tensor(f"Akr1{s}", [128, 512], F32, side="right"))
            bnrA = [R_qr.enter_context(nc.sbuf_tensor(f"Abnr{s}_{i}", [128, 512], BF16, side="right"))
                    for i in range(2)]

            def emit_rope(srcs, n_k, wtile, dst_aug, tok_src, init_free, cosc=cosc,
                          ssinc=ssinc, krf=krf, krs=krs, kr1=kr1, bnrA=bnrA):
                tok_last = init_free
                tok_cs = [None, None]
                tok_bnr = [[None] * 4, [None] * 4]
                for nch2 in range(NCH):
                    cb = nch2 % 2
                    tok_c1 = LP.issue(em, "sync",
                                      lambda E, nch2=nch2, cb=cb: E.dma_start(
                                          out=cosc[cb][:], in_=dcos[:, 512 * nch2:512 * (nch2 + 1)]),
                                      pre_toks=(tok_cs[cb],))
                    tok_c2 = LP.issue(em, "sync",
                                      lambda E, nch2=nch2, cb=cb: E.dma_start(
                                          out=ssinc[cb][:], in_=dssin[:, 512 * nch2:512 * (nch2 + 1)]),
                                      pre_toks=(tok_cs[cb],))
                    for m in range(4):
                        pp = pj[m % 2]

                        def f(E, m=m, nch2=nch2, pp=pp, tok=tok_src, tokp=tok_last):
                            W(E, tok, tokp)
                            ins = None
                            for kf in range(n_k):
                                ins = E.matmul(pp[:, 0:512], wtile[kf][:, 128 * m:128 * (m + 1)],
                                               srcs[kf][:, 512 * nch2:512 * (nch2 + 1)],
                                               start=(kf == 0), stop=(kf == n_k - 1))
                            ins.then_inc(TT.h, 1)
                        em("tensor", f)
                        tok_mm = TT.inc()
                        stpr = mkvs(4)

                        def f(E, m=m, pp=pp, cb=cb, tok=tok_mm, stpr=stpr,
                              tokc1=tok_c1, tokc2=tok_c2, tokb=tuple(tok_bnr[m % 2])):
                            W(E, tok, tokc1, tokc2, *tokb)
                            S = stpr(E)
                            S(E.tensor_scalar(krf[:], pp[:, 0:512], 0.0, None, OP.add))
                            S(E.stream_shuffle(krs[:], krf[:], [(i + 16) % 32 for i in range(32)]))
                            S(E.tensor_tensor(kr1[:], krf[:], cosc[cb][:], OP.mult))
                            S(E.tensor_tensor(krs[:], krs[:], ssinc[cb][:], OP.mult))
                            E.tensor_tensor(bnrA[m % 2][:], kr1[:], krs[:], OP.add).then_inc(VV.h, 1)
                        em("vector", f)
                        tok_last = VV.inc()
                        for g in range(4):
                            tok_bnr[m % 2][g] = OP_.issue(
                                em, "gpsimd",
                                lambda E, m=m, g=g, nch2=nch2: E.dma_start(
                                    out=dst_aug[4 * m + g][64:96, 512 * nch2:512 * (nch2 + 1)],
                                    in_=bnrA[m % 2][32 * g:32 * (g + 1), 0:512]),
                                pre_toks=(tok_last,))
                    tok_cs[cb] = tok_last
                toks = [tok_last]
                for a in tok_bnr:
                    toks.extend(a)
                return tuple(toks)

            tok_w = mk_ldcast(wdf3)(wqrb, dwqrope, 512, SC96)
            toks_qrope = emit_rope(qcb, 4, wqrb, q_aug, tok_qcb_done, tok_w)
            em.flush()
            R_qr.close()
            R_qcb.close()
            fence()

            if upto <= 16:
                bail(tok_qcont[-1], s)
                R1.close(); pmid.close(); R0.close(); R00.close(); psB.close()
                fence()
                continue

            R_kv = contextlib.ExitStack()
            wkvb = [R_kv.enter_context(nc.sbuf_tensor(f"Awkv{s}_{c}", [128, KVR], BF16, side="right"))
                    for c in range(8)]
            wdf4 = R_kv.enter_context(nc.sbuf_tensor(f"Awdf4{s}", [128, 1024], F32, side="right"))
            sqs = R_kv.enter_context(nc.sbuf_tensor(f"Asqs{s}", [128, 512], F32, side="right"))
            rtok = R_kv.enter_context(nc.sbuf_tensor(f"Artok{s}", [128, 8], F32, side="right"))
            kvf = [R_kv.enter_context(nc.sbuf_tensor(f"Akvf{s}_{i}", [128, 512], F32, side="right"))
                   for i in range(2)]
            rrow = R_kv.enter_context(nc.sbuf_tensor(f"Arrow{s}", [128, 512], F32, side="right"))
            tok_w = mk_ldcast(wdf4)(wkvb, dwkv, KVR, 1.0)

            tok_kvr = None
            tok_rms_rd = None
            for nch2 in range(NCH):
                col = slice(512 * nch2, 512 * (nch2 + 1))

                def f(E, col=col, tok=tok_uT, tokq=tok_kvr, tokw=tok_w):
                    W(E, tok, tokq, tokw)
                    ins = None
                    for m2 in range(2):
                        for kf in range(8):
                            ins = E.matmul(pj[m2][:, 0:512],
                                           wkvb[kf][:, 128 * m2:128 * (m2 + 1)],
                                           uT[kf][:, col], start=(kf == 0), stop=(kf == 7))
                    ins.then_inc(TT.h, 1)
                em("tensor", f)
                tok_kv = TT.inc()

                stpk = mkvs(4)

                def f(E, tok=tok_kv, stpk=stpk):
                    W(E, tok)
                    S = stpk(E)
                    S(E.tensor_scalar(kvf[0][:], pj[0][:, 0:512], 0.0, None, OP.add))
                    S(E.tensor_scalar(kvf[1][:], pj[1][:, 0:512], 0.0, None, OP.add))
                    S(E.tensor_tensor(sqs[:], kvf[0][:], kvf[0][:], OP.mult))
                    S(E.tensor_tensor(rrow[:], kvf[1][:], kvf[1][:], OP.mult))
                    E.tensor_tensor(sqs[:], sqs[:], rrow[:], OP.add).then_inc(VV.h, 1)
                em("vector", f)
                tok_kve = VV.inc()

                def f(E, tok=tok_kve, tokr=tok_rms_rd):
                    W(E, tok, tokr)
                    ins = None
                    for j in range(4):
                        ins = E.transpose(pj[2][:, 128 * j:128 * (j + 1)],
                                          sqs[:, 128 * j:128 * (j + 1)], identf[:])
                    ins.then_inc(TT.h, 1)
                em("tensor", f)
                tok_sqT = TT.inc()

                stpr = mkvs(1)

                def f(E, tok=tok_sqT, stpr=stpr):
                    W(E, tok)
                    S = stpr(E)
                    ins = None
                    for j in range(4):
                        ins = E.reduce_sum(rtok[:, j:j + 1], pj[2][:, 128 * j:128 * (j + 1)], AX.X)
                    S(ins)
                    E.engine_nop().then_inc(VV.h, 1)
                em("vector", f)
                tok_rms_rd = VV.inc()

                sstp2 = mkss(1)

                def f(E, tok=tok_rms_rd, sstp2=sstp2):
                    W(E, tok)
                    S = sstp2(E)
                    S(E.activation(rtok[:, 4:8], rtok[:, 0:4], ACT.Ln,
                                   bias=seed[:, 9:10], scale=1.0 / 256.0))
                    E.activation(rtok[:, 4:8], rtok[:, 4:8], ACT.Exp,
                                 bias=0.0, scale=-0.5).then_inc(SS.h, 1)
                em("scalar", f)
                tok_rr = SS.inc()

                tok_rk = None
                for j in range(4):
                    def f(E, j=j, tok=tok_rr, tokp=tok_rk):
                        W(E, tok, tokp)
                        E.transpose(pj[2][0:1, 128 * j:128 * (j + 1)], rtok[:, 4 + j:5 + j],
                                    identf[:]).then_inc(TT.h, 1)
                    em("tensor", f)
                    tok_rjT = TT.inc()

                    def f(E, j=j, tok=tok_rjT):
                        W(E, tok)
                        E.tensor_scalar(rrow[0:1, 0:128], pj[2][0:1, 128 * j:128 * (j + 1)],
                                        0.0, None, OP.add).then_inc(VV.h, 1)
                    em("vector", f)
                    tok_rw = VV.inc()

                    def f(E, j=j, tok=tok_rw):
                        W(E, tok)
                        E.matmul(pj[2][:, 128 * j:128 * (j + 1)], onesf[0:1, :],
                                 rrow[0:1, 0:128], start=True, stop=True).then_inc(TT.h, 1)
                    em("tensor", f)
                    tok_rk = TT.inc()
                tok_rb = tok_rk

                def f(E, col=col, tok=tok_rb, tok2=tok_kve):
                    W(E, tok, tok2)
                    E.tensor_tensor(kvr[0][:, col], kvf[0][:], pj[2][:, 0:512], OP.mult)
                    E.tensor_tensor(kvr[1][:, col], kvf[1][:], pj[2][:, 0:512], OP.mult).then_inc(VV.h, 1)
                em("vector", f)
                tok_kvr = VV.inc()
            em.flush()
            R_kv.close()
            R1.close()
            fence()

            if upto <= 13:
                bail(tok_kvr, s)
                pmid.close(); R0.close(); R00.close(); psB.close()
                fence()
                continue

            k_aug = [pmid.enter_context(nc.sbuf_tensor(f"Aka{s}_{h}", [97, TS], BF16, side="left"))
                     for h in range(16)]
            for h in range(16):
                OP_.issue(em, "gpsimd",
                          lambda E, h=h: E.dma_start(out=k_aug[h][96:97, 0:TS], in_=donesrow))

            R_kca = contextlib.ExitStack()
            wkdecb = [R_kca.enter_context(nc.sbuf_tensor(f"Awkd{s}_{c}", [128, D], BF16, side="right"))
                      for c in range(2)]
            wdf5 = R_kca.enter_context(nc.sbuf_tensor(f"Awdf5{s}", [128, 1024], F32, side="right"))
            bnkA = [R_kca.enter_context(nc.sbuf_tensor(f"Abnk{s}_{i}", [128, 512], BF16, side="right"))
                    for i in range(2)]
            tok_wd = None
            for c in range(2):
                tok_l = LP.issue(em, "sync",
                                 lambda E, c=c: E.dma_start(out=wdf5[:], in_=dwkdec[c * 128:(c + 1) * 128, :]),
                                 pre_toks=(tok_wd,))

                def f(E, c=c, tok=tok_l):
                    W(E, tok, tok_g)
                    E.tensor_scalar(wkdecb[c][:], wdf5[:], gkvt[:, c:c + 1], None,
                                    OP.mult).then_inc(VV.h, 1)
                em("vector", f)
                tok_wd = VV.inc()

            tok_kca = None
            tok_bnk = [None, None]
            for hc in range(8):
                for nch2 in range(NCH):
                    pp = pj[(hc * NCH + nch2) % 2]
                    bi = (hc * NCH + nch2) % 2

                    def f(E, hc=hc, nch2=nch2, pp=pp, tok=tok_kvr, tokp=tok_kca, tokw=tok_wd):
                        W(E, tok, tokp, tokw)
                        ins = None
                        for c in range(2):
                            ins = E.matmul(pp[:, 0:512], wkdecb[c][:, 128 * hc:128 * (hc + 1)],
                                           kvr[c][:, 512 * nch2:512 * (nch2 + 1)],
                                           start=(c == 0), stop=(c == 1))
                        ins.then_inc(TT.h, 1)
                    em("tensor", f)
                    tok_mm = TT.inc()

                    def f(E, hc=hc, nch2=nch2, pp=pp, bi=bi, tok=tok_mm, tokb=tok_bnk[bi]):
                        W(E, tok, tokb)
                        E.tensor_scalar(k_aug[2 * hc][0:64, 512 * nch2:512 * (nch2 + 1)],
                                        pp[0:64, 0:512], 0.0, None, OP.add)
                        E.tensor_scalar(bnkA[bi][64:128, 0:512], pp[64:128, 0:512],
                                        0.0, None, OP.add).then_inc(VV.h, 1)
                    em("vector", f)
                    tok_kca = VV.inc()
                    tok_bnk[bi] = OP_.issue(
                        em, "gpsimd",
                        lambda E, hc=hc, nch2=nch2, bi=bi: E.dma_start(
                            out=k_aug[2 * hc + 1][0:64, 512 * nch2:512 * (nch2 + 1)],
                            in_=bnkA[bi][64:128, 0:512]),
                        pre_toks=(tok_kca,))
            em.flush()
            R_kca.close()
            fence()

            vkm = [pmid.enter_context(nc.sbuf_tensor(f"Avk{s}_{t}", [128, 1040], BF16, side="left"))
                   for t in range(NT)]
            R_vkm = contextlib.ExitStack()
            wvdecb = [R_vkm.enter_context(nc.sbuf_tensor(f"Awvd{s}_{c}", [128, D], BF16, side="right"))
                      for c in range(2)]
            wdf6 = R_vkm.enter_context(nc.sbuf_tensor(f"Awdf6{s}", [128, 1024], F32, side="right"))
            tok_wd = None
            for c in range(2):
                tok_l = LP.issue(em, "sync",
                                 lambda E, c=c: E.dma_start(out=wdf6[:], in_=dwvdec[c * 128:(c + 1) * 128, :]),
                                 pre_toks=(tok_wd,))

                def f(E, c=c, tok=tok_l):
                    W(E, tok, tok_g)
                    E.tensor_scalar(wvdecb[c][:], wdf6[:], gkvt[:, c:c + 1], None,
                                    OP.mult).then_inc(VV.h, 1)
                em("vector", f)
                tok_wd = VV.inc()

            tok_vkm_l = None
            for tc in range(NT):
                def f(E, tc=tc, tok=tok_kvr, tokp=tok_vkm_l, tokw=tok_wd):
                    W(E, tok, tokp, tokw)
                    ins = None
                    for half in range(2):
                        for c in range(2):
                            ins = E.matmul(pj[half][:, 0:512],
                                           kvr[c][:, 128 * tc:128 * (tc + 1)],
                                           wvdecb[c][:, 512 * half:512 * (half + 1)],
                                           start=(c == 0), stop=(c == 1))
                    ins.then_inc(TT.h, 1)
                em("tensor", f)
                tok_mmv = TT.inc()

                def f(E, tc=tc, tok=tok_mmv):
                    W(E, tok)
                    va = vkm[tc][:, 0:1040].rearrange("p (h c) -> p h c", c=65)
                    E.tensor_scalar(va[:, 0:8, 0:64],
                                    pj[0][:, 0:512].rearrange("p (h c) -> p h c", c=64),
                                    0.0, None, OP.add)
                    E.tensor_scalar(va[:, 8:16, 0:64],
                                    pj[1][:, 0:512].rearrange("p (h c) -> p h c", c=64),
                                    0.0, None, OP.add)
                    E.tensor_scalar(va[:, 0:16, 64:65],
                                    onesf[:, 0:16].rearrange("p (h c) -> p h c", c=1),
                                    0.0, None, OP.add).then_inc(VV.h, 1)
                em("vector", f)
                tok_vkm_l = VV.inc()
            em.flush()
            R_vkm.close()
            R0.close()
            fence()

            if upto <= 14:
                bail(tok_vkm_l, s)
                pmid.close(); R00.close(); psB.close()
                fence()
                continue

            R_kr = contextlib.ExitStack()
            wkrb = [R_kr.enter_context(nc.sbuf_tensor(f"Awkr{s}_{c}", [128, 512], BF16, side="right"))
                    for c in range(8)]
            wdf7 = R_kr.enter_context(nc.sbuf_tensor(f"Awdf7{s}", [128, 512], F32, side="right"))
            xtl2 = R_kr.enter_context(nc.sbuf_tensor(f"Axt2{s}", [128, 1024], F32, side="right"))
            uB2 = R_kr.enter_context(nc.sbuf_tensor(f"AuB2{s}", [128, 1024], BF16, side="right"))
            uT2 = [R_kr.enter_context(nc.sbuf_tensor(f"AuT2{s}_{c}", [128, 512], BF16, side="right"))
                   for c in range(8)]
            cosc2 = [R_kr.enter_context(nc.sbuf_tensor(f"Acos2{s}_{i}", [128, 512], F32, side="right"))
                     for i in range(2)]
            ssinc2 = [R_kr.enter_context(nc.sbuf_tensor(f"Assin2{s}_{i}", [128, 512], F32, side="right"))
                      for i in range(2)]
            krf2 = R_kr.enter_context(nc.sbuf_tensor(f"Akrf2{s}", [128, 512], F32, side="right"))
            krs2 = R_kr.enter_context(nc.sbuf_tensor(f"Akrs2{s}", [128, 512], F32, side="right"))
            kr12 = R_kr.enter_context(nc.sbuf_tensor(f"Akr12{s}", [128, 512], F32, side="right"))
            bnr2 = [R_kr.enter_context(nc.sbuf_tensor(f"Abnr2{s}_{i}", [128, 512], BF16, side="right"))
                    for i in range(2)]
            psKr = contextlib.ExitStack()
            ptpB2 = psKr.enter_context(nc.psum_tensor(f"AptpB2{s}", [128, 1024], BF16))

            tok_w = mk_ldcast(wdf7)(wkrb, dwkrope, 512, 1.0)
            tok_cs2 = [None, None]
            tok_bnr2 = [[None] * 4, [None] * 4]
            tok_rope = None
            tok_xt = None
            tok_ut2 = None
            for nch2 in range(NCH):
                cb = nch2 % 2
                tok_c1 = LP.issue(em, "sync",
                                  lambda E, nch2=nch2, cb=cb: E.dma_start(
                                      out=cosc2[cb][:], in_=dcos[:, 512 * nch2:512 * (nch2 + 1)]),
                                  pre_toks=(tok_cs2[cb],))
                tok_c2 = LP.issue(em, "sync",
                                  lambda E, nch2=nch2, cb=cb: E.dma_start(
                                      out=ssinc2[cb][:], in_=dssin[:, 512 * nch2:512 * (nch2 + 1)]),
                                  pre_toks=(tok_cs2[cb],))
                tok_ev = None
                for j in range(4):
                    k = 4 * nch2 + j
                    tok_l = LP.issue(em, "sync",
                                     lambda E, s=s, k=k: E.dma_start(out=xtl2[:], in_=x3[s, k]),
                                     pre_toks=(tok_xt,))

                    def f(E, k=k, tok=tok_l, tokp=tok_ut2):
                        W(E, tok, tokp)
                        E.activation(uB2[:], xtl2[:], ACT.Identity,
                                     bias=biaA[:, k:k + 1], scale=rlnA[:, k:k + 1]).then_inc(SS.h, 1)
                    em("scalar", f)
                    tok_xt = SS.inc()

                    def f(E, tok=tok_xt, tokp=tok_ev, tokr=tok_rope):
                        W(E, tok, tokp, tokr)
                        ins = None
                        for c in range(8):
                            ins = E.transpose(ptpB2[:, 128 * c:128 * (c + 1)],
                                              uB2[:, 128 * c:128 * (c + 1)], identb[:])
                        ins.then_inc(TT.h, 1)
                    em("tensor", f)
                    tok_t = TT.inc()

                    def f(E, j=j, tok=tok_t):
                        W(E, tok)
                        ins = None
                        for c in range(8):
                            ins = E.tensor_scalar(uT2[c][:, 128 * j:128 * (j + 1)],
                                                  ptpB2[:, 128 * c:128 * (c + 1)], 0.0, None, OP.add)
                        ins.then_inc(VV.h, 1)
                    em("vector", f)
                    tok_ev = VV.inc()
                    tok_ut2 = tok_ev
                for m in range(4):
                    pp = pj[m % 2]

                    def f(E, m=m, pp=pp, tok=tok_ev, tokw=tok_w, tokr=tok_rope):
                        W(E, tok, tokw, tokr)
                        ins = None
                        for kf in range(8):
                            ins = E.matmul(pp[:, 0:512], wkrb[kf][:, 128 * m:128 * (m + 1)],
                                           uT2[kf][:, 0:512],
                                           start=(kf == 0), stop=(kf == 7))
                        ins.then_inc(TT.h, 1)
                    em("tensor", f)
                    tok_mm = TT.inc()
                    stpr = mkvs(4)

                    def f(E, m=m, pp=pp, cb=cb, tok=tok_mm, stpr=stpr,
                          tokc1=tok_c1, tokc2=tok_c2, tokb=tuple(tok_bnr2[m % 2])):
                        W(E, tok, tokc1, tokc2, *tokb)
                        S = stpr(E)
                        S(E.tensor_scalar(krf2[:], pp[:, 0:512], 0.0, None, OP.add))
                        S(E.stream_shuffle(krs2[:], krf2[:], [(i + 16) % 32 for i in range(32)]))
                        S(E.tensor_tensor(kr12[:], krf2[:], cosc2[cb][:], OP.mult))
                        S(E.tensor_tensor(krs2[:], krs2[:], ssinc2[cb][:], OP.mult))
                        E.tensor_tensor(bnr2[m % 2][:], kr12[:], krs2[:], OP.add).then_inc(VV.h, 1)
                    em("vector", f)
                    tok_rope = VV.inc()
                    for g in range(4):
                        tok_bnr2[m % 2][g] = OP_.issue(
                            em, "gpsimd",
                            lambda E, m=m, g=g, nch2=nch2: E.dma_start(
                                out=k_aug[4 * m + g][64:96, 512 * nch2:512 * (nch2 + 1)],
                                in_=bnr2[m % 2][32 * g:32 * (g + 1), 0:512]),
                            pre_toks=(tok_rope,))
                tok_cs2[cb] = tok_rope
            em.flush()
            psKr.close()
            R_kr.close()
            fence()

            if upto <= 15:
                bail(tok_rope, s)
                pmid.close(); R00.close(); psB.close()
                fence()
                continue

            R_bnd = contextlib.ExitStack()
            sqb = R_bnd.enter_context(nc.sbuf_tensor(f"Asqb{s}", [128, 512], BF16, side="right"))
            qsv = R_bnd.enter_context(nc.sbuf_tensor(f"Aqsv{s}", [128, 512], F32, side="right"))
            qsv2 = R_bnd.enter_context(nc.sbuf_tensor(f"Aqsv2{s}", [128, 512], F32, side="right"))
            msq = R_bnd.enter_context(nc.sbuf_tensor(f"Amsq{s}", [128, 8], F32, side="right"))

            tok_sq = None
            tok_pjf = [None, None]
            for h in range(nheads):
                tok_ms = None
                for c in range(NCH):
                    def f(E, h=h, c=c, tok=tok_sq):
                        W(E, tok)
                        E.tensor_tensor(sqb[0:96, :], k_aug[h][0:96, 512 * c:512 * (c + 1)],
                                        k_aug[h][0:96, 512 * c:512 * (c + 1)],
                                        OP.mult).then_inc(VV.h, 1)
                    em("vector", f)
                    tok_sqv = VV.inc()

                    def f(E, c=c, tok=tok_sqv, tokf=tok_pjf[c % 2]):
                        W(E, tok, tokf)
                        E.matmul(pj[c % 2][:, 0:512], onesb[0:96, :], sqb[0:96, :],
                                 start=True, stop=True).then_inc(TT.h, 1)
                    em("tensor", f)
                    tok_mm = TT.inc()
                    tok_sq = tok_mm

                    def f(E, c=c, tok=tok_mm, tokp=tok_ms):
                        W(E, tok, tokp)
                        E.reduce_max(msq[:, c:c + 1], pj[c % 2][:, 0:512],
                                     AX.X).then_inc(VV.h, 1)
                    em("vector", f)
                    tok_ms = VV.inc()
                    tok_pjf[c % 2] = tok_ms

                def f(E, tok=tok_ms):
                    W(E, tok)
                    E.tensor_reduce(msq[:, 4:5], msq[:, 0:NCH], AX.X, OP.max)
                    E.engine_nop().then_inc(VV.h, 1)
                em("vector", f)
                tok_mk = VV.inc()

                tok_sqrt = None
                for c in range(NCH):
                    def f(E, h=h, c=c, tok=tok_sq, tokm=tok_mk):
                        W(E, tok, tokm)
                        E.tensor_tensor(sqb[0:96, :], q_aug[h][0:96, 512 * c:512 * (c + 1)],
                                        q_aug[h][0:96, 512 * c:512 * (c + 1)],
                                        OP.mult).then_inc(VV.h, 1)
                    em("vector", f)
                    tok_sqv = VV.inc()

                    def f(E, c=c, tok=tok_sqv, tokf=tok_pjf[c % 2]):
                        W(E, tok, tokf)
                        E.matmul(pj[c % 2][:, 0:512], onesb[0:96, :], sqb[0:96, :],
                                 start=True, stop=True).then_inc(TT.h, 1)
                    em("tensor", f)
                    tok_mm = TT.inc()
                    tok_sq = tok_mm

                    def f(E, c=c, tok=tok_mm, tokp=tok_sqrt):
                        W(E, tok, tokp)
                        E.tensor_scalar(qsv[:], pj[c % 2][:, 0:512], msq[:, 4:5],
                                        None, OP.mult).then_inc(VV.h, 1)
                    em("vector", f)
                    tok_qs = VV.inc()
                    tok_pjf[c % 2] = tok_qs

                    sstb = mkss(1)

                    def f(E, tok=tok_qs, sstb=sstb):
                        W(E, tok)
                        S = sstb(E)
                        S(E.activation(qsv2[:], qsv[:], ACT.Ln, bias=0.0, scale=1.0))
                        E.activation(qsv[:], qsv2[:], ACT.Exp, bias=0.0,
                                     scale=0.5).then_inc(SS.h, 1)
                    em("scalar", f)
                    tok_sqrt = SS.inc()

                    def f(E, h=h, c=c, tok=tok_sqrt):
                        W(E, tok)
                        E.tensor_scalar(q_aug[h][96:97, 512 * c:512 * (c + 1)],
                                        qsv[0:1, 0:512], -1.0, None,
                                        OP.mult).then_inc(VV.h, 1)
                    em("vector", f)
                    tok_sq = VV.inc()
            em.flush()
            R_bnd.close()
            psB.close()
            fence()

            psC = contextlib.ExitStack()
            stq = [psC.enter_context(nc.psum_tensor(f"Astq{s}_{i}", [128, 512], F32))
                   for i in range(5)]
            pvp = [psC.enter_context(nc.psum_tensor(f"Apvp{s}_{i}", [128, 512], F32))
                   for i in range(2)]
            bcp = psC.enter_context(nc.psum_tensor(f"Abcp{s}", [128, 512], F32))

            R_ofm = contextlib.ExitStack()
            ofm = [R_ofm.enter_context(nc.sbuf_tensor(f"Aof{s}_{c}", [128, TS], BF16, side="right"))
                   for c in range(8)]
            R_core = contextlib.ExitStack()
            masks = [R_core.enter_context(nc.sbuf_tensor(f"Amsk{s}_{L}", [128, 512], BF16, side="right"))
                     for L in range(4)]
            Ptb = [R_core.enter_context(nc.sbuf_tensor(f"APt{s}_{i}", [128, 512], BF16, side="right"))
                   for i in range(3)]
            rcs = R_core.enter_context(nc.sbuf_tensor(f"Arcs{s}", [128, 512], F32, side="right"))
            rex = R_core.enter_context(nc.sbuf_tensor(f"Arex{s}", [128, 512], F32, side="right"))
            srow = rex[64:65, 0:512]
            masks_tok = [OP_.issue(em, "gpsimd",
                                   lambda E, L=L: E.dma_start(out=masks[L][:], in_=dminm[L]))
                         for L in range(4)]

            iters = [(h, qb) for h in range(nheads) for qb in range(NQB)]
            tok_stq_free = [None] * 5
            tok_Ptb_free = [None] * 3
            tok_pvp_free = [None, None]
            tok_bcp_free = None
            tok_srow_free = None
            tok_core = None

            for i, (h, qb) in enumerate(iters):
                KC = 4 * (qb + 1)
                pv = pvp[i % 2]
                tok_pv = None
                for kc in range(KC):
                    ssl = kc % 5
                    psl = kc % 3

                    def f(E, h=h, qb=qb, kc=kc, ssl=ssl, tokf=tok_stq_free[ssl]):
                        W(E, tokf)
                        E.matmul(stq[ssl][:, 0:512], k_aug[h][0:97, 128 * kc:128 * (kc + 1)],
                                 q_aug[h][0:97, 512 * qb:512 * (qb + 1)],
                                 start=True, stop=True).then_inc(TT.h, 1)
                    em("tensor", f)
                    tok_st = TT.inc()

                    def f(E, ssl=ssl, psl=psl, tok=tok_st, tokf=tok_Ptb_free[psl]):
                        W(E, tok, tokf)
                        E.activation(Ptb[psl][:], stq[ssl][:, 0:512], ACT.Exp,
                                     bias=0.0, scale=1.0).then_inc(SS.h, 1)
                    em("scalar", f)
                    tok_exp = SS.inc()
                    tok_stq_free[ssl] = tok_exp

                    pv_wait = tok_exp
                    if kc >= 4 * qb:
                        L = kc - 4 * qb

                        def f(E, psl=psl, L=L, tok=tok_exp, tokm=masks_tok[L]):
                            W(E, tok, tokm)
                            E.tensor_tensor(Ptb[psl][:], Ptb[psl][:], masks[L][:],
                                            OP.min).then_inc(VV.h, 1)
                        em("vector", f)
                        pv_wait = VV.inc()

                    def f(E, h=h, kc=kc, psl=psl, KC=KC, pv=pv, tok=pv_wait,
                          tokp=tok_pvp_free[i % 2] if kc == 0 else None):
                        W(E, tok, tokp)
                        E.matmul(pv[0:65, 0:512], vkm[kc][:, 65 * h:65 * h + 65],
                                 Ptb[psl][:, 0:512], start=(kc == 0),
                                 stop=(kc == KC - 1)).then_inc(TT.h, 1)
                    em("tensor", f)
                    tok_pv = TT.inc()
                    tok_Ptb_free[psl] = tok_pv

                def f(E, pv=pv, tok=tok_pv, tokf=tok_srow_free):
                    W(E, tok, tokf)
                    E.tensor_scalar(srow, pv[64:65, 0:512], 0.0,
                                    None, OP.add).then_inc(VV.h, 1)
                em("vector", f)
                tok_sr = VV.inc()

                def f(E, tok=tok_sr, tokf=tok_bcp_free):
                    W(E, tok, tokf)
                    E.matmul(bcp[:, 0:512], onesf[64:65, 0:128], srow,
                             start=True, stop=True).then_inc(TT.h, 1)
                em("tensor", f)
                tok_bc = TT.inc()
                tok_srow_free = tok_bc

                sstf = mkss(1)

                def f(E, tok=tok_bc, sstf=sstf, tokr=tok_core):
                    W(E, tok, tokr)
                    S = sstf(E)
                    S(E.activation(rcs[0:64, 0:512], bcp[0:64, 0:512], ACT.Ln,
                                   bias=0.0, scale=1.0))
                    E.activation(rex[0:64, 0:512], rcs[0:64, 0:512], ACT.Exp,
                                 bias=0.0, scale=-1.0).then_inc(SS.h, 1)
                em("scalar", f)
                tok_rc = SS.inc()
                tok_bcp_free = tok_rc

                if h % 2 == 0:
                    def f(E, h=h, qb=qb, pv=pv, tok=tok_rc):
                        W(E, tok)
                        E.tensor_tensor(ofm[h // 2][0:64, 512 * qb:512 * (qb + 1)],
                                        pv[0:64, 0:512], rex[0:64, 0:512],
                                        OP.mult).then_inc(VV.h, 1)
                    em("vector", f)
                    tok_ev = VV.inc()
                else:
                    obl = (KC) % 3

                    def f(E, obl=obl, pv=pv, tok=tok_rc, tokb=tok_Ptb_free[obl]):
                        W(E, tok, tokb)
                        E.tensor_tensor(Ptb[obl][0:64, 0:512],
                                        pv[0:64, 0:512], rex[0:64, 0:512],
                                        OP.mult).then_inc(VV.h, 1)
                    em("vector", f)
                    tok_ev = VV.inc()
                    tok_Ptb_free[obl] = OP_.issue(
                        em, "gpsimd",
                        lambda E, h=h, qb=qb, obl=obl: E.dma_start(
                            out=ofm[h // 2][64:128, 512 * qb:512 * (qb + 1)],
                            in_=Ptb[obl][0:64, 0:512]),
                        pre_toks=(tok_ev,))
                tok_pvp_free[i % 2] = tok_ev
                tok_core = tok_ev

            em.flush()
            R_core.close()
            pmid.close()
            psC.close()
            fence()

            if upto <= 3:
                bail(tok_core, s)
                R_ofm.close(); R00.close()
                fence()
                continue

            psD = contextlib.ExitStack()
            pj2 = [psD.enter_context(nc.psum_tensor(f"Apj2{s}_{i}", [128, 512], F32))
                   for i in range(3)]
            R_B4 = contextlib.ExitStack()
            wob = [R_B4.enter_context(nc.sbuf_tensor(f"Awo{s}_{c}", [128, D], BF16, side="right"))
                   for c in range(8)]
            xa = [R_B4.enter_context(nc.sbuf_tensor(f"Axa{s}_{i}", [128, 1024], F32, side="right"))
                  for i in range(4)]
            woe = [R_B4.enter_context(nc.sbuf_tensor(f"Awoe{s}_{i}", [128, 512], F32, side="right"))
                   for i in range(2)]
            wdf8 = R_B4.enter_context(nc.sbuf_tensor(f"Awdf8{s}", [128, 1024], F32, side="right"))
            tok_wo_ready = mk_ldcast(wdf8)(wob, dwo, D, 1.0)

            tok_xa_store = [None] * 4
            tok_woe_free = [None, None]
            tok_wo_tr = None
            tok_stt_l = None
            for nch2 in range(NCH):
                toks_xl = []
                for j in range(4):
                    toks_xl.append(LP.issue(
                        em, "sync",
                        lambda E, s=s, nch2=nch2, j=j: E.dma_start(out=xa[j][:],
                                                                   in_=x3[s, 4 * nch2 + j]),
                        pre_toks=(tok_xa_store[j],)))
                toks_xl = tuple(toks_xl)
                for d in range(8):
                    db = d % 2

                    def f(E, nch2=nch2, d=d, db=db, tok=tok_core, tokf=tok_woe_free[db],
                          tokw=tok_wo_ready):
                        W(E, tok, tokf, tokw)
                        ins = None
                        for c in range(8):
                            ins = E.matmul(pj2[db][:, 0:512],
                                           wob[c][:, 128 * d:128 * (d + 1)],
                                           ofm[c][:, 512 * nch2:512 * (nch2 + 1)],
                                           start=(c == 0), stop=(c == 7))
                        ins.then_inc(TT.h, 1)
                    em("tensor", f)
                    tok_mm = TT.inc()

                    def f(E, db=db, tok=tok_mm):
                        W(E, tok)
                        E.tensor_scalar(woe[db][:], pj2[db][:, 0:512],
                                        0.0, None, OP.add).then_inc(VV.h, 1)
                    em("vector", f)
                    tok_ev = VV.inc()
                    tok_woe_free[db] = tok_ev

                    def f(E, db=db, tok=tok_ev, tokp=tok_wo_tr):
                        W(E, tok, tokp)
                        ins = None
                        for j in range(4):
                            ins = E.transpose(pj2[2][:, 128 * j:128 * (j + 1)],
                                              woe[db][:, 128 * j:128 * (j + 1)], identf[:])
                        ins.then_inc(TT.h, 1)
                    em("tensor", f)
                    tok_tr = TT.inc()

                    def f(E, d=d, tok=tok_tr, tokx=toks_xl):
                        W(E, tok, *tokx)
                        ins = None
                        for j in range(4):
                            ins = E.tensor_tensor(xa[j][:, 128 * d:128 * (d + 1)],
                                                  xa[j][:, 128 * d:128 * (d + 1)],
                                                  pj2[2][:, 128 * j:128 * (j + 1)], OP.add)
                        ins.then_inc(VV.h, 1)
                    em("vector", f)
                    tok_stt_l = VV.inc()
                    tok_wo_tr = tok_stt_l

                for j in range(4):
                    tok_xa_store[j] = OP_.issue(
                        em, "gpsimd",
                        lambda E, s=s, nch2=nch2, j=j: E.dma_start(out=y3[s, 4 * nch2 + j],
                                                                   in_=xa[j][:]),
                        pre_toks=(tok_stt_l,))
            em.flush()
            R_B4.close()
            R_ofm.close()
            R00.close()
            psD.close()
            fence()

        ps.close()

    if do_attn:
        emit_ffn(dx, dx1, "A")
        fence()
        emit_attn(dx1, dx2 if do_ffn2 else dout, upto=attn_upto, nheads=attn_heads)
        fence()
        if do_ffn2:
            emit_ffn(dx2, dout, "C")
    else:
        emit_ffn(dx, dout, "A")

    toks = OP_.all_tok()

    def f(E, toks=tuple(toks)):
        for t in toks:
            E.wait_ge(t[0], t[1])
    em("gpsimd", f)
    em.flush()
    es.close()
    return nc



def _host_constants(TS):
    import ml_dtypes
    identb = np.eye(128, dtype=ml_dtypes.bfloat16)
    onesb = np.ones((128, 128), dtype=ml_dtypes.bfloat16)
    identf = np.eye(128, dtype=np.float32)
    onesf = np.ones((128, 128), dtype=np.float32)
    seed = np.zeros((128, 16), dtype=np.float32)
    seed[:, 0:8] = 1.5
    seed[:, 8] = EPS_LN
    seed[:, 9] = EPS_RMS
    j = np.arange(16, dtype=np.float32)
    inv_freq = (1.0 / (10000.0 ** (2.0 * j / 32.0))).astype(np.float32)
    t = np.arange(TS, dtype=np.float32)
    ang = np.outer(t, inv_freq)
    emb = np.concatenate([ang, ang], axis=1)
    cos32 = np.cos(emb).astype(np.float32)
    sin32 = np.sin(emb).astype(np.float32)
    p = np.arange(128)
    d32 = p % 32
    cosfm = np.ascontiguousarray(cos32[:, d32].T)
    sgn = np.where(d32 < 16, -1.0, 1.0).astype(np.float32)
    ssinfm = np.ascontiguousarray((sin32[:, d32] * sgn[None, :]).T)
    onesrow = np.ones((1, TS), dtype=ml_dtypes.bfloat16)
    BIG = np.float32(3.0e38)
    minmask = np.zeros((4, 128, 512), dtype=np.float32)
    kr = np.arange(128)[:, None]
    qc = np.arange(128)[None, :]
    for L in range(4):
        for ti in range(4):
            blk = minmask[L][:, 128 * ti:128 * (ti + 1)]
            if ti > L:
                blk[:] = BIG
            elif ti == L:
                blk[:] = np.where(kr <= qc, BIG, np.float32(0.0))
    minmask = minmask.astype(ml_dtypes.bfloat16)
    return dict(identb=identb, identf=identf, onesf=onesf, onesb=onesb, seed=seed,
                cosfm=cosfm, ssinfm=ssinfm, onesrow=onesrow, minmask=minmask)


def _prep_inputs(L0_attn_W_kv, L0_attn_g_kv, L0_attn_W_kdec, L0_attn_W_vdec,
                 L0_attn_W_q, L0_attn_W_qdec, L0_attn_W_krope, L0_attn_W_qrope,
                 L0_attn_W_o, L0_ffn_w1, L0_ffn_conv_w, L0_ffn_conv_b, L0_ffn_w2, TS):
    f32 = lambda a: np.ascontiguousarray(np.asarray(a, np.float32))
    consts = _host_constants(TS)
    return dict(
        wkv=f32(L0_attn_W_kv),
        gkv=np.ascontiguousarray(np.asarray(L0_attn_g_kv, np.float32).reshape(2, 128).T),
        wkdec=f32(L0_attn_W_kdec), wvdec=f32(L0_attn_W_vdec),
        wq=f32(L0_attn_W_q), wqdec=f32(L0_attn_W_qdec),
        wkrope=f32(L0_attn_W_krope), wqrope=f32(L0_attn_W_qrope),
        wo=f32(L0_attn_W_o),
        w1t=np.ascontiguousarray(np.asarray(L0_ffn_w1, np.float32).T),
        w2t=np.ascontiguousarray(np.asarray(L0_ffn_w2, np.float32).T),
        convw=np.ascontiguousarray(np.asarray(L0_ffn_conv_w, np.float32)
                                   .reshape(32, 128, 3).transpose(1, 0, 2).reshape(128, 96)),
        convb=np.ascontiguousarray(np.asarray(L0_ffn_conv_b, np.float32).reshape(32, 128).T),
        **consts)


def kernel(L0_attn_x, L0_attn_W_kv, L0_attn_g_kv, L0_attn_W_kdec, L0_attn_W_vdec,
           L0_attn_W_q, L0_attn_W_qdec, L0_attn_W_krope, L0_attn_W_qrope,
           L0_attn_W_o, L0_ffn_w1, L0_ffn_conv_w, L0_ffn_conv_b, L0_ffn_w2):
    x = np.ascontiguousarray(np.asarray(L0_attn_x, dtype=np.float32))
    B, TS, _ = x.shape
    nseq = B // NCORES
    nc = build_program(TS=TS, NSEQ=nseq)
    shared = _prep_inputs(L0_attn_W_kv, L0_attn_g_kv, L0_attn_W_kdec, L0_attn_W_vdec,
                          L0_attn_W_q, L0_attn_W_qdec, L0_attn_W_krope, L0_attn_W_qrope,
                          L0_attn_W_o, L0_ffn_w1, L0_ffn_conv_w, L0_ffn_conv_b,
                          L0_ffn_w2, TS)
    in_maps = []
    for c in range(NCORES):
        m = dict(shared)
        m["x"] = np.ascontiguousarray(x[c * nseq:(c + 1) * nseq])
        in_maps.append(m)
    res = run_bass_kernel_spmd(nc, in_maps, core_ids=list(range(NCORES)))
    out = np.concatenate([np.asarray(r["out"]) for r in res.results], axis=0)
    return out.astype(np.float32)


# version 07
# speedup vs baseline: 1.4128x; 1.0231x over previous
import contextlib
import numpy as np
import concourse.bass as bass
import concourse.mybir as mybir
from concourse.bass_utils import run_bass_kernel_spmd

F32 = mybir.dt.float32
BF16 = mybir.dt.bfloat16
I32 = mybir.dt.int32
OP = mybir.AluOpType
ACT = mybir.ActivationFunctionType
AX = mybir.AxisListType

D = 1024
HID = 4096
NH = 16
KVR = 256
QR = 512
EPS_LN = 1e-5
EPS_RMS = 1e-6
EPS_CLIP = 1e-5
NEG_BIG = -1.0e30
NCORES = 8
ENGS = ("sync", "gpsimd", "scalar", "vector", "tensor")


class SemC:
    __slots__ = ("h", "v")

    def __init__(self, h):
        self.h = h
        self.v = 0

    def inc(self, n=1):
        self.v += n
        return (self.h, self.v)

    def tok(self):
        return (self.h, self.v)


class DmaPool:

    def __init__(self, nc, es, name, n):
        self.sems = [SemC(es.enter_context(nc.semaphore(f"{name}{i}"))) for i in range(n)]
        self.rr = 0

    def issue(self, em, queue, dma_fn, pre_toks=()):
        s = self.sems[self.rr]
        self.rr = (self.rr + 1) % len(self.sems)
        prev = s.tok()

        def f(E, s=s, prev=prev, pre=tuple(pre_toks), dma_fn=dma_fn):
            if prev[1] > 0:
                E.wait_ge(prev[0], prev[1])
            for t in pre:
                if t is not None:
                    E.wait_ge(t[0], t[1])
            dma_fn(E).then_inc(s.h, 16)
        em(queue, f)
        return s.inc(16)

    def all_tok(self):
        return [s.tok() for s in self.sems if s.v > 0]


class Emit:
    def __init__(self, block):
        self.block = block
        self.q = {e: [] for e in ENGS}

    def __call__(self, eng, fn):
        self.q[eng].append(fn)

    def flush(self):
        for name in ENGS:
            fns = self.q[name]
            if not fns:
                continue

            def body(E, fns=fns):
                for f in fns:
                    f(E)

            getattr(self.block, name)(body)
        self.q = {e: [] for e in ENGS}


def W(E, *toks):
    for t in toks:
        if t is not None:
            E.wait_ge(t[0], t[1])


def build_program(TS=2048, NSEQ=2, do_attn=True, do_ffn2=True, debug=False, attn_upto=99, attn_heads=NH):
    assert TS % 512 == 0
    NT = TS // 128
    NSLAB = TS // 512
    NCH = TS // 512
    nc = bass.Bass("TRN2", target_bir_lowering=False)

    dx = nc.dram_tensor("x", [NSEQ, TS, D], F32, kind="ExternalInput").ap()
    dwkv = nc.dram_tensor("wkv", [D, KVR], F32, kind="ExternalInput").ap()
    dgkv = nc.dram_tensor("gkv", [128, 2], F32, kind="ExternalInput").ap()
    dwkdec = nc.dram_tensor("wkdec", [KVR, D], F32, kind="ExternalInput").ap()
    dwvdec = nc.dram_tensor("wvdec", [KVR, D], F32, kind="ExternalInput").ap()
    dwq = nc.dram_tensor("wq", [D, QR], F32, kind="ExternalInput").ap()
    dwqdec = nc.dram_tensor("wqdec", [QR, D], F32, kind="ExternalInput").ap()
    dwkrope = nc.dram_tensor("wkrope", [D, 512], F32, kind="ExternalInput").ap()
    dwqrope = nc.dram_tensor("wqrope", [QR, 512], F32, kind="ExternalInput").ap()
    dwo = nc.dram_tensor("wo", [D, D], F32, kind="ExternalInput").ap()
    dw1t = nc.dram_tensor("w1t", [D, HID], F32, kind="ExternalInput").ap()
    dw2t = nc.dram_tensor("w2t", [HID, D], F32, kind="ExternalInput").ap()
    dconvw = nc.dram_tensor("convw", [128, 96], F32, kind="ExternalInput").ap()
    dconvb = nc.dram_tensor("convb", [128, 32], F32, kind="ExternalInput").ap()
    didb = nc.dram_tensor("identb", [128, 128], BF16, kind="ExternalInput").ap()
    didf = nc.dram_tensor("identf", [128, 128], F32, kind="ExternalInput").ap()
    donesf = nc.dram_tensor("onesf", [128, 128], F32, kind="ExternalInput").ap()
    donesb = nc.dram_tensor("onesb", [128, 128], BF16, kind="ExternalInput").ap()
    dseed = nc.dram_tensor("seed", [128, 16], F32, kind="ExternalInput").ap()
    dcos = nc.dram_tensor("cosfm", [128, TS], F32, kind="ExternalInput").ap()
    dssin = nc.dram_tensor("ssinfm", [128, TS], F32, kind="ExternalInput").ap()
    donesrow = nc.dram_tensor("onesrow", [1, TS], BF16, kind="ExternalInput").ap()
    dminm = nc.dram_tensor("minmask", [4, 128, 512], BF16, kind="ExternalInput").ap()
    dx1 = nc.dram_tensor("x1s", [NSEQ, TS, D], F32, kind="Internal").ap()
    dx2 = nc.dram_tensor("x2s", [NSEQ, TS, D], F32, kind="Internal").ap()
    dw1q = nc.dram_tensor("w1qs", [4, 128, 8192], BF16, kind="Internal").ap()
    dw2q = nc.dram_tensor("w2qs", [HID, D], BF16, kind="Internal").ap()
    dout = nc.dram_tensor("out", [NSEQ, TS, D], F32, kind="ExternalOutput").ap()
    ddbg = {}
    if debug:
        for nm, shp, dt in [("kons", [128, 1], F32), ("alf", [128, 64], F32),
                            ("bet", [128, 64], F32), ("bia", [128, 64], F32),
                            ("nT0", [128, 514], F32), ("apre", [128, 16384], F32),
                            ("apost", [128, 16384], F32), ("s2t", [128, 4], F32),
                            ("g2t", [128, 4], F32), ("bl2e0", [128, 512], F32),
                            ("amx", [128, 512], F32)]:
            ddbg[nm] = nc.dram_tensor("dbg_" + nm, shp, dt, kind="ExternalOutput").ap()
        ddbg["stw"] = nc.dram_tensor("dbg_stw", [128, 8 * (TS // 128)], F32, kind="ExternalOutput").ap()
        ddbg["bns"] = nc.dram_tensor("dbg_bns", [128, 12 * (TS // 128)], F32, kind="ExternalOutput").ap()
        ddbg["smax"] = nc.dram_tensor("dbg_smax", [128, 64], F32, kind="ExternalOutput").ap()
        ddbg["hp0"] = nc.dram_tensor("dbg_hp0", [128, 514], F32, kind="ExternalOutput").ap()
        ddbg["w1s0"] = nc.dram_tensor("dbg_w1s0", [128, 8192], F32, kind="ExternalOutput").ap()
        ddbg["cv0"] = nc.dram_tensor("dbg_cv0", [128, 512], F32, kind="ExternalOutput").ap()
        ddbg["smin"] = nc.dram_tensor("dbg_smin", [128, 64], F32, kind="ExternalOutput").ap()

    es = contextlib.ExitStack()
    TT = SemC(es.enter_context(nc.semaphore("sTT")))
    VS = SemC(es.enter_context(nc.semaphore("sVS")))
    SZ = SemC(es.enter_context(nc.semaphore("sSZ")))
    VV = SemC(es.enter_context(nc.semaphore("sVV")))
    SS = SemC(es.enter_context(nc.semaphore("sSS")))
    LP = DmaPool(nc, es, "sL", 8)
    OP_ = DmaPool(nc, es, "sO", 4)

    block = es.enter_context(nc.Block())
    em = Emit(block)

    def mkvs(n):
        base = VS.v
        VS.v += n

        def factory(E):
            cnt = [0]

            def S(ins):
                cnt[0] += 1
                assert cnt[0] <= n
                ins.then_inc(VS.h, 1)
                E.wait_ge(VS.h, base + cnt[0])
            return S
        return factory

    def mkss(n):
        base = SZ.v
        SZ.v += n

        def factory(E):
            cnt = [0]

            def S(ins):
                cnt[0] += 1
                assert cnt[0] <= n
                ins.then_inc(SZ.h, 1)
                E.wait_ge(SZ.h, base + cnt[0])
            return S
        return factory

    identb = es.enter_context(nc.sbuf_tensor("sb_identb", [128, 128], BF16))
    onesb = es.enter_context(nc.sbuf_tensor("sb_onesb", [128, 128], BF16))
    identf = es.enter_context(nc.sbuf_tensor("sb_identf", [128, 128], F32))
    onesf = es.enter_context(nc.sbuf_tensor("sb_onesf", [128, 128], F32))
    seed = es.enter_context(nc.sbuf_tensor("sb_seed", [128, 16], F32))
    kons = es.enter_context(nc.sbuf_tensor("kons", [128, 8], F32))
    mw1 = es.enter_context(nc.sbuf_tensor("mw1", [128, 1], F32))
    rw1 = es.enter_context(nc.sbuf_tensor("rw1", [128, 1], F32))
    mw2 = es.enter_context(nc.sbuf_tensor("mw2", [128, 1], F32))
    rw2 = es.enter_context(nc.sbuf_tensor("rw2", [128, 1], F32))
    mw2h = es.enter_context(nc.sbuf_tensor("mw2h", [128, 1], F32))
    convwt = es.enter_context(nc.sbuf_tensor("sb_convwt", [128, 96], F32))
    convbt = es.enter_context(nc.sbuf_tensor("sb_convbt", [128, 32], F32))
    smallp = es.enter_context(nc.sbuf_tensor("smallp", [128, 8], F32))


    def emit_p0():
        tok_c = []
        for dst_t, src_a in ((identb, didb), (identf, didf), (onesf, donesf),
                             (onesb, donesb),
                             (seed, dseed), (convwt, dconvw), (convbt, dconvb)):
            tok_c.append(LP.issue(em, "sync", lambda E, d=dst_t, s=src_a: E.dma_start(out=d[:], in_=s)))

        stp = mkvs(4)

        def f(E, toks=tuple(tok_c), stp=stp):
            W(E, *toks)
            S = stp(E)
            S(E.tensor_scalar(smi[:], onesf[:, 0:128], 1.5, None, OP.mult))
            S(E.tensor_scalar(smib[:], smi[:], 0.0, None, OP.add))
            S(E.tensor_scalar(smallp[:, 0:8], smib[:, 0:8], 0.0, None, OP.add))
            S(E.tensor_scalar(kons[:], smallp[:, 0:8], -0.5, 129.0, OP.mult, OP.add))
            E.engine_nop().then_inc(VV.h, 1)
        em("vector", f)
        tok_kons = VV.inc()

        ps = contextlib.ExitStack()
        smi = ps.enter_context(nc.sbuf_tensor("smi", [128, 128], I32))
        smib = ps.enter_context(nc.sbuf_tensor("smib", [128, 128], BF16))
        wst = [ps.enter_context(nc.sbuf_tensor(f"p0w{i}", [128, 4096], F32)) for i in range(2)]
        wqi = ps.enter_context(nc.sbuf_tensor("p0qi", [128, 4096], I32))
        wqf = ps.enter_context(nc.sbuf_tensor("p0qf", [128, 4096], BF16))
        wqb = ps.enter_context(nc.sbuf_tensor("p0qb", [128, 4096], BF16))
        macc = ps.enter_context(nc.sbuf_tensor("p0macc", [128, 1], F32))
        red = ps.enter_context(nc.sbuf_tensor("p0red", [128, 1], F32))
        pms = ps.enter_context(nc.psum_tensor("p0ps", [128, 512], F32))

        def quant_weight(dsrc, nchunk, chwid, nelem, mean_t, recip_t, spill_fn,
                         multi_spill=False, init_vt=(None, None), init_sp=()):
            src3 = dsrc.rearrange("(c p) w -> c p w", p=128)
            vt = [init_vt[0], init_vt[1]]
            for c in range(nchunk):
                b = wst[c % 2]
                tok_l = LP.issue(em, "sync",
                                 lambda E, b=b, c=c: E.dma_start(out=b[:, 0:chwid], in_=src3[c]),
                                 pre_toks=(vt[c % 2],))

                stp = mkvs(1 if c == 0 else 2)

                def f(E, c=c, b=b, tok=tok_l, stp=stp):
                    W(E, tok)
                    S = stp(E)
                    if c == 0:
                        S(E.reduce_sum(macc[:], b[:, 0:chwid], AX.X,
                                       apply_absolute_value=True))
                    else:
                        S(E.reduce_sum(red[:], b[:, 0:chwid], AX.X, apply_absolute_value=True))
                        S(E.tensor_tensor(macc[:], macc[:], red[:], OP.add))
                    E.engine_nop().then_inc(VV.h, 1)
                em("vector", f)
                vt[c % 2] = VV.inc()
            tok_acc = vt[(nchunk - 1) % 2]

            def f(E, tok=tok_acc):
                W(E, tok)
                E.matmul(pms[0:1, 0:1], onesf[:, 0:1], macc[:], start=True,
                         stop=True).then_inc(TT.h, 1)
            em("tensor", f)
            tok_mm = TT.inc()

            def f(E, tok=tok_mm):
                W(E, tok)
                E.tensor_scalar(smallp[0:1, 0:1], pms[0:1, 0:1], 0.0, None, OP.add).then_inc(VV.h, 1)
            em("vector", f)
            tok_ev = VV.inc()

            def f(E, tok=tok_ev):
                W(E, tok)
                E.matmul(pms[:, 0:1], onesf[0:1, :], smallp[0:1, 0:1], start=True,
                         stop=True).then_inc(TT.h, 1)
            em("tensor", f)
            tok_bc = TT.inc()

            stp = mkvs(2)

            def f(E, tok=tok_bc, mean_t=mean_t, recip_t=recip_t, nelem=nelem, stp=stp):
                W(E, tok)
                S = stp(E)
                S(E.tensor_scalar(mean_t[:], pms[:, 0:1], 1.0 / nelem, EPS_CLIP, OP.mult, OP.max))
                S(E.reciprocal(recip_t[:], mean_t[:]))
                E.engine_nop().then_inc(VV.h, 1)
            em("vector", f)
            tok_scale = VV.inc()

            sp_prev = list(init_sp)
            vt2 = [tok_acc, tok_acc]
            for c in range(nchunk):
                b = wst[c % 2]
                tok_l = LP.issue(em, "sync",
                                 lambda E, b=b, c=c: E.dma_start(out=b[:, 0:chwid], in_=src3[c]),
                                 pre_toks=(vt2[c % 2],))

                stpq = mkvs(3)

                def f(E, c=c, b=b, tok=tok_l, tok3=tuple(sp_prev), recip_t=recip_t, stpq=stpq):
                    W(E, tok, tok_scale, tok_kons, *tok3)
                    S = stpq(E)
                    S(E.tensor_scalar(wqi[:, 0:chwid], b[:, 0:chwid], recip_t[:, 0:1],
                                      kons[:, 0:1], OP.mult, OP.add))
                    S(E.tensor_scalar(wqf[:, 0:chwid], wqi[:, 0:chwid], -128.0, None, OP.add))
                    S(E.tensor_scalar(wqf[:, 0:chwid], wqf[:, 0:chwid], 1.0, None, OP.min))
                    E.tensor_scalar(wqb[:, 0:chwid], wqf[:, 0:chwid], -1.0, None,
                                    OP.max).then_inc(VV.h, 1)
                em("vector", f)
                vt2[c % 2] = VV.inc()
                sp_prev = []
                if multi_spill:
                    for g in range(4):
                        sp_prev.append(OP_.issue(
                            em, "gpsimd",
                            lambda E, c=c, g=g: E.dma_start(
                                out=dw1q[g, :, 1024 * c:1024 * (c + 1)],
                                in_=wqb[:, 1024 * g:1024 * (g + 1)]),
                            pre_toks=(vt2[c % 2],)))
                else:
                    sp_prev.append(OP_.issue(
                        em, "gpsimd",
                        lambda E, c=c: E.dma_start(out=spill_fn(c), in_=wqb[:, 0:chwid]),
                        pre_toks=(vt2[c % 2],)))
            return tok_scale, (tuple(vt2), tuple(sp_prev))

        tok_w1scale, (w1_vt, w1_sp) = quant_weight(
            dw1t, 8, 4096, D * HID, mw1, rw1, None, multi_spill=True)
        tok_w1sp = tuple(OP_.all_tok())
        tok_w2scale, _ = quant_weight(
            dw2t, 32, 1024, D * HID, mw2, rw2,
            lambda c: dw2q[c * 128:(c + 1) * 128, :],
            init_vt=w1_vt, init_sp=w1_sp)
        tok_w2sp = tuple(OP_.all_tok())

        stp = mkvs(2)

        def f(E, stp=stp, tokc=tuple(tok_c)):
            W(E, tok_w2scale, tok_w1scale, *tokc)
            S = stp(E)
            S(E.tensor_scalar(convwt[:], convwt[:], mw1[:, 0:1], None, OP.mult))
            S(E.tensor_scalar(mw2h[:], mw2[:], 0.5 / 127.0, None, OP.mult))
            E.engine_nop().then_inc(VV.h, 1)
        em("vector", f)
        tok_taps = VV.inc()
        em.flush()
        ps.close()
        return tok_taps, tok_w1sp, tok_w2sp, tok_kons

    tok_taps, tok_w1sp, tok_w2sp, tok_kons = emit_p0()

    def fence():
        toks = [TT.tok(), VV.tok(), SS.tok()] + LP.all_tok() + OP_.all_tok()
        for e in ENGS:
            def f(E, toks=tuple(toks)):
                for t in toks:
                    if t[1] > 0:
                        E.wait_ge(t[0], t[1])
            em(e, f)
        em.flush()

    def emit_ffn(src, dst, tag):
        ps = contextlib.ExitStack()
        w2q = [ps.enter_context(nc.sbuf_tensor(f"F{tag}w2q{c}", [128, D], BF16)) for c in range(32)]
        w1s = [ps.enter_context(nc.sbuf_tensor(f"F{tag}w1s{g}", [128, 8192], BF16)) for g in range(2)]
        nT = [[ps.enter_context(nc.sbuf_tensor(f"F{tag}nT{b}_{c}", [128, 514], BF16))
               for c in range(8)] for b in range(2)]
        aAr = ps.enter_context(nc.sbuf_tensor(f"F{tag}aA", [128, 32 * 512], BF16))
        xsl = [ps.enter_context(nc.sbuf_tensor(f"F{tag}xsl{i}", [128, 2048], F32)) for i in range(2)]
        xrr = [ps.enter_context(nc.sbuf_tensor(f"F{tag}xrr{i}", [128, 1024], F32)) for i in range(4)]
        nI = [ps.enter_context(nc.sbuf_tensor(f"F{tag}nI0", [128, 1024], I32))] * 2
        nB = [ps.enter_context(nc.sbuf_tensor(f"F{tag}nB{i}", [128, 1024], BF16)) for i in range(2)]
        csc = [ps.enter_context(nc.sbuf_tensor(f"F{tag}csc{i}", [128, 512], F32)) for i in range(4)]
        nis = ps.enter_context(nc.sbuf_tensor(f"F{tag}nis", [128, 512], I32))
        amx = ps.enter_context(nc.sbuf_tensor(f"F{tag}amx", [128, 512], F32))
        s2r = ps.enter_context(nc.sbuf_tensor(f"F{tag}s2r", [128, 128], F32))
        bl2e = [ps.enter_context(nc.sbuf_tensor(f"F{tag}bl2e{i}", [128, 512], F32)) for i in range(2)]
        bns = ps.enter_context(nc.sbuf_tensor(f"F{tag}bns", [128, 12 * NT], F32))
        smax = ps.enter_context(nc.sbuf_tensor(f"F{tag}smax", [128, NT], F32))
        smin = ps.enter_context(nc.sbuf_tensor(f"F{tag}smin", [128, NT], F32))
        stw = ps.enter_context(nc.sbuf_tensor(f"F{tag}stw", [128, 8 * NT], F32))
        alf = ps.enter_context(nc.sbuf_tensor(f"F{tag}alf", [128, NT], F32))
        bet = ps.enter_context(nc.sbuf_tensor(f"F{tag}bet", [128, NT], F32))
        bia = ps.enter_context(nc.sbuf_tensor(f"F{tag}bia", [128, NT], F32))
        s2t = ps.enter_context(nc.sbuf_tensor(f"F{tag}s2t", [128, 4], F32))
        g2t = ps.enter_context(nc.sbuf_tensor(f"F{tag}g2t", [128, 4], F32))

        hpsA = ps.enter_context(nc.psum_tensor(f"F{tag}hA", [128, 1024], F32))
        hpsB = ps.enter_context(nc.psum_tensor(f"F{tag}hB", [128, 1024], F32))
        ntp = ps.enter_context(nc.psum_tensor(f"F{tag}ntp", [128, 1024], BF16))
        bl2p = ps.enter_context(nc.psum_tensor(f"F{tag}bl2p", [128, 1024], F32))
        pmis = ps.enter_context(nc.psum_tensor(f"F{tag}pmis", [128, 512], F32))
        hps = [hpsA, hpsB]

        x3 = src.rearrange("s (k p) d -> s k p d", p=128)
        y3 = dst.rearrange("s (k p) d -> s k p d", p=128)

        tok_w2l = []
        for c in range(32):
            tok_w2l.append(LP.issue(
                em, "sync",
                lambda E, c=c: E.dma_start(out=w2q[c][:], in_=dw2q[c * 128:(c + 1) * 128, :]),
                pre_toks=tok_w2sp))
        toks_w2ld = tuple(tok_w2l)

        st = {
            "xsl_free": [None, None],
            "nIB_free": [None, None],
            "nT_bl1_done": [None, None],
            "halo_done": {},
            "xside_done": {},
            "conv": [],
            "silu": [],
            "quant_done": None,
            "bl2_done": None,
            "last_stt": None,
            "amx_T": None,
            "w1_used": [None, None],
            "xrr_tok": [None, None, None, None],
            "xout_done": [None, None, None, None],
        }

        def emit_stats(s):
            vt = [None, None]
            for k in range(NT):
                b = k % 2
                tok_l = LP.issue(em, "sync",
                                 lambda E, s=s, k=k, b=b: E.dma_start(out=xsl[b][:, 0:1024], in_=x3[s, k]),
                                 pre_toks=(vt[b], st["xsl_free"][b]))

                def f(E, k=k, b=b, tok=tok_l):
                    W(E, tok)
                    E.bn_stats(bns[:, 12 * k:12 * k + 6], xsl[b][:, 0:512])
                    E.bn_stats(bns[:, 12 * k + 6:12 * k + 12], xsl[b][:, 512:1024])
                    E.reduce_max(smax[:, k:k + 1], xsl[b][:, 0:1024], AX.X)
                    E.tensor_reduce(smin[:, k:k + 1], xsl[b][:, 0:1024], AX.X, OP.min).then_inc(VV.h, 1)
                em("vector", f)
                vt[b] = VV.inc()
            st["xsl_free"] = [vt[0], vt[1]]
            tok_stats = vt[(NT - 1) % 2]

            stp = mkvs(12)

            def f(E, tok=tok_stats, stp=stp):
                W(E, tok, tok_kons)
                S = stp(E)
                me = bns[:, 1::3]
                m2 = bns[:, 2::3]
                R0, R1, R2 = stw[:, 0:2 * NT], stw[:, 2 * NT:3 * NT], stw[:, 3 * NT:4 * NT]
                R3, R4, R5 = stw[:, 4 * NT:5 * NT], stw[:, 5 * NT:6 * NT], stw[:, 6 * NT:7 * NT]
                S(E.tensor_tensor(R0, me[:, 0::2], me[:, 1::2], OP.add))
                S(E.tensor_tensor(R1, R0[:, 0::2], R0[:, 1::2], OP.add))
                S(E.tensor_tensor(R0, m2[:, 0::2], m2[:, 1::2], OP.add))
                S(E.tensor_tensor(R2, R0[:, 0::2], R0[:, 1::2], OP.add))
                S(E.tensor_tensor(stw[:, 4 * NT:8 * NT], me, me, OP.mult))
                S(E.tensor_tensor(R0, stw[:, 4 * NT:8 * NT][:, 0::2],
                                  stw[:, 4 * NT:8 * NT][:, 1::2], OP.add))
                S(E.tensor_tensor(R3, R0[:, 0::2], R0[:, 1::2], OP.add))
                S(E.tensor_scalar(R4, R1, 0.25, None, OP.mult))
                S(E.tensor_scalar(R5, R2, 1.0 / 1024.0, None, OP.mult))
                S(E.scalar_tensor_tensor(R5, R3, 0.25, R5, OP.mult, OP.add))
                S(E.tensor_tensor(R0[:, 0:NT], R4, R4, OP.mult))
                S(E.scalar_tensor_tensor(R5, R0[:, 0:NT], -1.0, R5, OP.mult, OP.add))
                E.engine_nop().then_inc(VV.h, 1)
            em("vector", f)
            tok_var = VV.inc()
            sstp = mkss(1)

            def f(E, tok=tok_var, sstp=sstp):
                W(E, tok)
                S = sstp(E)
                S(E.activation(stw[:, 0:NT], stw[:, 6 * NT:7 * NT], ACT.Ln,
                               bias=seed[:, 8:9], scale=1.0))
                E.activation(stw[:, 7 * NT:8 * NT], stw[:, 0:NT], ACT.Exp,
                             bias=0.0, scale=-0.5).then_inc(SS.h, 1)
            em("scalar", f)
            tok_rln = SS.inc()
            stp2 = mkvs(10)

            def f(E, tok=tok_rln, stp=stp2):
                W(E, tok)
                S = stp(E)
                mean = stw[:, 5 * NT:6 * NT]
                rln = stw[:, 7 * NT:8 * NT]
                t0 = stw[:, 0:NT]
                t1 = stw[:, NT:2 * NT]
                t2 = stw[:, 2 * NT:3 * NT]
                S(E.tensor_tensor(t0, smax[:], mean, OP.subtract))
                S(E.tensor_tensor(t1, mean, smin[:], OP.subtract))
                S(E.tensor_tensor(t0, t0, t1, OP.max))
                S(E.tensor_tensor(t1, t0, rln, OP.mult))
                S(E.tensor_scalar(t1, t1, EPS_CLIP, None, OP.max))
                S(E.reciprocal(t2, t1))
                S(E.scalar_tensor_tensor(alf[:], rln, 127.0, t2, OP.mult, OP.mult))
                S(E.tensor_scalar(bet[:], t1, 1.0 / 127.0, None, OP.mult))
                S(E.tensor_tensor(t0, alf[:], mean, OP.mult))
                S(E.tensor_scalar(bia[:], t0, -1.0, kons[:, 0:1], OP.mult, OP.add))
                E.engine_nop().then_inc(VV.h, 1)
            em("vector", f)
            return VV.inc()

        tok_ab = {}

        def emit_xside(u):
            s, b = divmod(u, NSLAB)
            bb = u % 2
            toks_half = [None, None]
            for h2 in range(2):
                t0 = LP.issue(
                    em, "sync",
                    lambda E, s=s, b=b, h2=h2: E.dma_start(
                        out=xsl[h2][:, 0:1024], in_=x3[s, 4 * b + 2 * h2]),
                    pre_toks=(st["xsl_free"][h2],))
                t1 = LP.issue(
                    em, "sync",
                    lambda E, s=s, b=b, h2=h2: E.dma_start(
                        out=xsl[h2][:, 1024:2048], in_=x3[s, 4 * b + 2 * h2 + 1]))
                toks_half[h2] = (t0, t1)
            last_e = None
            for j in range(4):
                k = 4 * b + j
                jb = j % 2

                def f(E, k=k, j=j, jb=jb, toks=toks_half[j // 2],
                      tok2=tok_ab[s], tokf=st["nIB_free"][jb], tokf2=st["nIB_free"][1 - jb]):
                    W(E, toks[0], toks[1], tok2, tokf, tokf2)
                    E.activation(nI[jb][:], xsl[j // 2][:, 1024 * (j % 2):1024 * (j % 2 + 1)],
                                 ACT.Identity, bias=bia[:, k:k + 1],
                                 scale=alf[:, k:k + 1]).then_inc(SS.h, 1)
                em("scalar", f)
                tok_s = SS.inc()

                stpn = mkvs(1)

                def f(E, k=k, jb=jb, tok=tok_s, stpn=stpn):
                    W(E, tok)
                    S = stpn(E)
                    S(E.tensor_scalar(nB[jb][:], nI[jb][:], -128.0, None, OP.add))
                    E.tensor_scalar(nB[jb][:], nB[jb][:], bet[:, k:k + 1], None,
                                    OP.mult).then_inc(VV.h, 1)
                em("vector", f)
                tok_v = VV.inc()

                def f(E, jb=jb, tok=tok_v, tokp=last_e):
                    W(E, tok, tokp)
                    ins = None
                    for c in range(8):
                        ins = E.transpose(ntp[:, 128 * c:128 * (c + 1)],
                                          nB[jb][:, 128 * c:128 * (c + 1)], identb[:])
                    ins.then_inc(TT.h, 1)
                em("tensor", f)
                tok_t = TT.inc()

                def f(E, b=b, bb=bb, j=j, tok=tok_t, tokbl1=st["nT_bl1_done"][bb]):
                    W(E, tok, tokbl1)
                    ins = None
                    for c in range(8):
                        ins = E.tensor_scalar(nT[bb][c][:, 1 + 128 * j:1 + 128 * (j + 1)],
                                              ntp[:, 128 * c:128 * (c + 1)], 0.0, None, OP.add)
                    if j == 0:
                        if b > 0:
                            for c in range(8):
                                E.tensor_scalar(nT[1 - bb][c][:, 513:514],
                                                ntp[:, 128 * c:128 * c + 1], 0.0, None, OP.add)
                            for c in range(8):
                                ins = E.tensor_scalar(nT[bb][c][:, 0:1],
                                                      nT[1 - bb][c][:, 512:513], 0.0, None, OP.add)
                        else:
                            for c in range(8):
                                ins = E.memset(nT[bb][c][:, 0:1], 0.0)
                    if j == 3 and b == NSLAB - 1:
                        for c in range(8):
                            ins = E.memset(nT[bb][c][:, 513:514], 0.0)
                    ins.then_inc(VV.h, 1)
                em("vector", f)
                last_e = VV.inc()
                st["nIB_free"][jb] = last_e
                if j == 0 and b > 0:
                    st["halo_done"][u - 1] = last_e
                if j == 1:
                    st["xsl_free"][0] = last_e
                if j == 3:
                    st["xsl_free"][1] = last_e
            st["xside_done"][u] = last_e
            if b == NSLAB - 1:
                st["halo_done"][u] = last_e

        def emit_slab(u):
            s, b = divmod(u, NSLAB)
            bb = u % 2
            tok_w1g = [None] * 4
            for g in range(2):
                tok_w1g[g] = LP.issue(
                    em, "sync",
                    lambda E, g=g: E.dma_start(out=w1s[g % 2][:], in_=dw1q[g]),
                    pre_toks=(st["w1_used"][g % 2],) + tok_w1sp)

            if debug and u == 0 and src is dx:
                OP_.issue(em, "gpsimd", lambda E: E.dma_start(out=ddbg["w1s0"], in_=w1s[0][:]),
                          pre_toks=(tok_w1g[0],))
            nm0 = len(st["conv"])
            tok_am = None
            for m in range(32):
                g = m // 8
                gb = g % 2
                mc = m % 8
                hp = hps[m % 2]
                gi = nm0 + m
                tok_hp_free = st["conv"][gi - 2] if gi >= 2 else None
                if mc == 0 and g >= 2:
                    tok_w1g[g] = LP.issue(
                        em, "sync",
                        lambda E, g=g: E.dma_start(out=w1s[g % 2][:], in_=dw1q[g]),
                        pre_toks=(st["w1_used"][g % 2],) + tok_w1sp)

                def f(E, bb=bb, m=m, gb=gb, mc=mc, hp=hp, tokn=st["xside_done"][u],
                      tokh=st["halo_done"].get(u), tokw=tok_w1g[g], tokf=tok_hp_free):
                    W(E, tokn, tokh, tokw, tokf)
                    ins = None
                    for kf in range(8):
                        lhs = w1s[gb][:, 1024 * kf + 128 * mc:1024 * kf + 128 * (mc + 1) - 0]
                        E.matmul(hp[:, 0:512], lhs, nT[bb][kf][:, 0:512],
                                 start=(kf == 0), stop=(kf == 7))
                        ins = E.matmul(hp[:, 512:514], lhs, nT[bb][kf][:, 512:514],
                                       start=(kf == 0), stop=(kf == 7))
                    ins.then_inc(TT.h, 1)
                em("tensor", f)
                tok_bl1m = TT.inc()
                if mc == 7:
                    st["w1_used"][gb] = tok_bl1m
                if m == 31:
                    st["nT_bl1_done"][bb] = tok_bl1m

                c0 = csc[2 * (m % 2)]
                c1 = csc[2 * (m % 2) + 1]

                stpc = mkvs(2)

                def f(E, m=m, hp=hp, c0=c0, c1=c1, tok=tok_bl1m, stpc=stpc,
                      tokf=st["silu"][gi - 2] if gi >= 2 else None):
                    W(E, tok, tok_taps, tokf)
                    S = stpc(E)
                    S(E.tensor_scalar(c0[:], hp[:, 1:513], convwt[:, 3 * m + 1:3 * m + 2],
                                      convbt[:, m:m + 1], OP.mult, OP.add))
                    S(E.scalar_tensor_tensor(c1[:], hp[:, 0:512], convwt[:, 3 * m:3 * m + 1],
                                             c0[:], OP.mult, OP.add))
                    E.scalar_tensor_tensor(c0[:], hp[:, 2:514], convwt[:, 3 * m + 2:3 * m + 3],
                                           c1[:], OP.mult, OP.add).then_inc(VV.h, 1)
                em("vector", f)
                tok_conv = VV.inc()
                st["conv"].append(tok_conv)
                if debug and u == 0 and m == 0 and src is dx:
                    def fdbg(E, hp=hp, c0=c0, tok=tok_bl1m, tok2=tok_conv):
                        W(E, tok, tok2)
                        E.tensor_scalar(bl2e[0][:, 0:512], hp[:, 0:512], 0.0, None, OP.add)
                        E.tensor_scalar(bl2e[1][:, 0:2], hp[:, 512:514], 0.0, None, OP.add)
                        E.tensor_scalar(bl2e[1][:, 2:512], c0[:, 0:510], 0.0, None, OP.add).then_inc(VV.h, 1)
                    em("vector", fdbg)
                    tok_d = VV.inc()
                    OP_.issue(em, "gpsimd", lambda E: E.dma_start(out=ddbg["hp0"][:, 0:512], in_=bl2e[0][:, 0:512]),
                              pre_toks=(tok_d,))
                    OP_.issue(em, "gpsimd", lambda E: E.dma_start(out=ddbg["hp0"][:, 512:514], in_=bl2e[1][:, 0:2]),
                              pre_toks=(tok_d,))
                    tok_dd = OP_.issue(em, "gpsimd", lambda E: E.dma_start(out=ddbg["cv0"][:, 0:510], in_=bl2e[1][:, 2:512]),
                              pre_toks=(tok_d,))
                    st["bl2_done"] = tok_dd
                    st["conv"][-1] = tok_d

                def f(E, m=m, c0=c0, tok=tok_conv, tokq=st["quant_done"]):
                    W(E, tok, tokq)
                    E.activation(aAr[:, 512 * m:512 * (m + 1)], c0[:], ACT.Silu).then_inc(SS.h, 1)
                em("scalar", f)
                tok_sl = SS.inc()
                st["silu"].append(tok_sl)

            tok_amax = tok_sl

            def f(E, tok=tok_amax, toka=st["amx_T"]):
                W(E, tok, toka)
                E.tensor_reduce(amx[:], aAr[:].rearrange("p (c w) -> p w c", c=32),
                                AX.X, OP.max, apply_absolute_value=True).then_inc(VV.h, 1)
            em("vector", f)
            tok_amax = VV.inc()

            if debug and u == 0 and src is dx:
                OP_.issue(em, "gpsimd", lambda E: E.dma_start(out=ddbg["nT0"], in_=nT[0][0][:]),
                          pre_toks=(st["xside_done"][0], st["halo_done"].get(0)))
                OP_.issue(em, "gpsimd", lambda E: E.dma_start(out=ddbg["apre"], in_=aAr[:]),
                          pre_toks=(tok_amax,))

            def f(E, tok=tok_amax, tokp=st["last_stt"]):
                W(E, tok, tokp)
                ins = None
                for j in range(4):
                    ins = E.transpose(pmis[:, 128 * j:128 * (j + 1)],
                                      amx[:, 128 * j:128 * (j + 1)], identf[:])
                ins.then_inc(TT.h, 1)
            em("tensor", f)
            st["amx_T"] = TT.inc()
            tok_amT = st["amx_T"]

            stp_s2 = mkvs(5)

            def f(E, tok=tok_amT, stp=stp_s2):
                W(E, tok)
                S = stp(E)
                for j in range(4):
                    ins = E.reduce_max(s2t[:, j:j + 1], pmis[:, 128 * j:128 * (j + 1)], AX.X)
                S(ins)
                S(E.tensor_scalar(s2t[:], s2t[:], EPS_CLIP, None, OP.max))
                S(E.tensor_scalar(g2t[:], s2t[:], mw2h[:, 0:1], None, OP.mult))
                S(E.reciprocal(s2t[:], s2t[:]))
                S(E.tensor_scalar(s2t[:], s2t[:], 127.0, None, OP.mult))
                E.engine_nop().then_inc(VV.h, 1)
            em("vector", f)
            tok_s2 = VV.inc()

            tok_rank1 = None
            for j in range(4):
                def f(E, j=j, tok=tok_s2, tokp=tok_rank1):
                    W(E, tok, tokp)
                    E.transpose(pmis[0:1, 128 * j:128 * (j + 1)], s2t[:, j:j + 1],
                                identf[:]).then_inc(TT.h, 1)
                em("tensor", f)
                tok_jT = TT.inc()

                def f(E, j=j, tok=tok_jT):
                    W(E, tok)
                    E.tensor_scalar(s2r[0:1, 0:128], pmis[0:1, 128 * j:128 * (j + 1)],
                                    0.0, None, OP.add).then_inc(VV.h, 1)
                em("vector", f)
                tok_row = VV.inc()

                def f(E, j=j, tok=tok_row):
                    W(E, tok)
                    E.matmul(pmis[:, 128 * j:128 * (j + 1)], onesf[0:1, :],
                             s2r[0:1, 0:128], start=True, stop=True).then_inc(TT.h, 1)
                em("tensor", f)
                tok_rank1 = TT.inc()
            tok_s2b = tok_rank1

            tok_q = None
            for m in range(32):
                stpz = mkvs(2)

                def f(E, m=m, tok=tok_s2b, stpz=stpz):
                    W(E, tok)
                    S = stpz(E)
                    acol = aAr[:, 512 * m:512 * (m + 1)]
                    S(E.tensor_tensor(csc[0][:], acol, pmis[:, 0:512], OP.mult))
                    S(E.tensor_scalar(nis[:], csc[0][:], kons[:, 0:1], None, OP.add))
                    E.tensor_scalar(acol, nis[:], -128.0, None, OP.add).then_inc(VV.h, 1)
                em("vector", f)
                tok_q = VV.inc()
            st["quant_done"] = tok_q
            if debug and u == 0 and src is dx:
                OP_.issue(em, "gpsimd", lambda E: E.dma_start(out=ddbg["amx"], in_=amx[:]),
                          pre_toks=(tok_amax,))
                OP_.issue(em, "gpsimd", lambda E: E.dma_start(out=ddbg["s2t"], in_=s2t[:]),
                          pre_toks=(tok_q,))
                OP_.issue(em, "gpsimd", lambda E: E.dma_start(out=ddbg["g2t"], in_=g2t[:]),
                          pre_toks=(tok_q,))
                OP_.issue(em, "gpsimd", lambda E: E.dma_start(out=ddbg["apost"], in_=aAr[:]),
                          pre_toks=(tok_q,))

            for j in range(4):
                st["xrr_tok"][j] = LP.issue(
                    em, "sync",
                    lambda E, s=s, b=b, j=j: E.dma_start(out=xrr[j][:], in_=x3[s, 4 * b + j]),
                    pre_toks=(st["xout_done"][j],))
            toks_xr = tuple(st["xrr_tok"])

            tok_prev_pair_ev = None
            tok_stt_last = st["last_stt"]
            for p in range(4):
                def f(E, p=p, tok=tok_q, toke=tok_prev_pair_ev,
                      tokb=st["bl2_done"] if p == 0 else None):
                    W(E, tok, toke, tokb, *toks_w2ld)
                    ins = None
                    for half in range(2):
                        d = 2 * p + half
                        for c in range(32):
                            ins = E.matmul(bl2p[:, 512 * half:512 * (half + 1)],
                                           w2q[c][:, 128 * d:128 * (d + 1)],
                                           aAr[:, 512 * c:512 * (c + 1)],
                                           start=(c == 0), stop=(c == 31))
                    ins.then_inc(TT.h, 1)
                em("tensor", f)
                tok_mm = TT.inc()

                def f(E, tok=tok_mm):
                    W(E, tok)
                    E.tensor_scalar(bl2e[0][:], bl2p[:, 0:512], 0.0, None, OP.add)
                    E.tensor_scalar(bl2e[1][:], bl2p[:, 512:1024], 0.0, None, OP.add).then_inc(VV.h, 1)
                em("vector", f)
                tok_ev = VV.inc()
                tok_prev_pair_ev = tok_ev
                if debug and u == 0 and p == 0 and src is dx:
                    OP_.issue(em, "gpsimd", lambda E: E.dma_start(out=ddbg["bl2e0"], in_=bl2e[0][:]),
                              pre_toks=(tok_ev,))

                for half in range(2):
                    d = 2 * p + half

                    def f(E, half=half, tok=tok_ev, tokq2=tok_q, tokst=tok_stt_last):
                        W(E, tok, tokq2, tokst)
                        ins = None
                        for j in range(4):
                            ins = E.transpose(pmis[:, 128 * j:128 * (j + 1)],
                                              bl2e[half][:, 128 * j:128 * (j + 1)], identf[:])
                        ins.then_inc(TT.h, 1)
                    em("tensor", f)
                    tok_tr = TT.inc()

                    def f(E, d=d, tok=tok_tr, tokx=toks_xr):
                        W(E, tok, *tokx)
                        ins = None
                        for j in range(4):
                            ins = E.scalar_tensor_tensor(xrr[j][:, 128 * d:128 * (d + 1)],
                                                         pmis[:, 128 * j:128 * (j + 1)],
                                                         g2t[:, j:j + 1],
                                                         xrr[j][:, 128 * d:128 * (d + 1)],
                                                         OP.mult, OP.add)
                        ins.then_inc(VV.h, 1)
                    em("vector", f)
                    tok_stt_last = VV.inc()
            st["bl2_done"] = tok_prev_pair_ev
            st["last_stt"] = tok_stt_last

            for j in range(4):
                st["xout_done"][j] = OP_.issue(
                    em, "gpsimd",
                    lambda E, s=s, b=b, j=j: E.dma_start(out=y3[s, 4 * b + j], in_=xrr[j][:]),
                    pre_toks=(tok_stt_last,))

        for s in range(NSEQ):
            tok_ab[s] = emit_stats(s)
            if debug and s == 0 and src is dx:
                for nm, t in [("kons", kons), ("alf", alf), ("bet", bet), ("bia", bia)]:
                    def fd(E, nm=nm, t=t, tok=tok_ab[s]):
                        W(E, tok)
                        E.dma_start(out=ddbg[nm][:, 0:t.shape[1] if False else None] if False else ddbg[nm][:, 0:NT] if nm != "kons" else ddbg[nm],
                                    in_=t[:]).then_inc(SS.h, 0) if False else None
                    pass
                OP_.issue(em, "gpsimd", lambda E: E.dma_start(out=ddbg["kons"], in_=kons[:, 0:1]), pre_toks=(tok_ab[s],))
                OP_.issue(em, "gpsimd", lambda E: E.dma_start(out=ddbg["alf"][:, 0:NT], in_=alf[:]), pre_toks=(tok_ab[s],))
                OP_.issue(em, "gpsimd", lambda E: E.dma_start(out=ddbg["bet"][:, 0:NT], in_=bet[:]), pre_toks=(tok_ab[s],))
                OP_.issue(em, "gpsimd", lambda E: E.dma_start(out=ddbg["bia"][:, 0:NT], in_=bia[:]), pre_toks=(tok_ab[s],))
                OP_.issue(em, "gpsimd", lambda E: E.dma_start(out=ddbg["stw"], in_=stw[:]), pre_toks=(tok_ab[s],))
                OP_.issue(em, "gpsimd", lambda E: E.dma_start(out=ddbg["bns"], in_=bns[:]), pre_toks=(tok_ab[s],))
                OP_.issue(em, "gpsimd", lambda E: E.dma_start(out=ddbg["smax"][:, 0:NT], in_=smax[:]), pre_toks=(tok_ab[s],))
                OP_.issue(em, "gpsimd", lambda E: E.dma_start(out=ddbg["smin"][:, 0:NT], in_=smin[:]), pre_toks=(tok_ab[s],))
            base = s * NSLAB
            emit_xside(base)
            if NSLAB > 1:
                emit_xside(base + 1)
            for b in range(NSLAB):
                emit_slab(base + b)
                if b + 2 < NSLAB:
                    emit_xside(base + b + 2)
        em.flush()
        ps.close()

    def emit_attn(src, dst, upto=99, nheads=NH):
        NQB = TS // 512
        ps = contextlib.ExitStack()
        gkvt = ps.enter_context(nc.sbuf_tensor("Agkv", [128, 2], F32))
        tok_g = LP.issue(em, "sync", lambda E: E.dma_start(out=gkvt[:], in_=dgkv))
        em.flush()

        x3 = src.rearrange("s (k p) d -> s k p d", p=128)
        y3 = dst.rearrange("s (k p) d -> s k p d", p=128)
        SC96 = float(1.0 / np.sqrt(96.0))

        def bail(tok, s):
            for k in range(NT):
                OP_.issue(em, "gpsimd",
                          lambda E, s=s, k=k: E.dma_start(out=y3[s, k], in_=x3[s, k]),
                          pre_toks=(tok,))
            em.flush()

        def mk_ldcast(wdft):
            state = {"t": None}

            def ldcast(dsts, dsrc, width, scale):
                for c in range(len(dsts)):
                    tok_l = LP.issue(em, "sync",
                                     lambda E, c=c, dsrc=dsrc, width=width: E.dma_start(
                                         out=wdft[:, 0:width], in_=dsrc[c * 128:(c + 1) * 128, :]),
                                     pre_toks=(state["t"],))

                    def f(E, c=c, dsts=dsts, width=width, scale=scale, tok=tok_l):
                        W(E, tok)
                        E.tensor_scalar(dsts[c][:, 0:width], wdft[:, 0:width], scale,
                                        None, OP.mult).then_inc(VV.h, 1)
                    em("vector", f)
                    state["t"] = VV.inc()
                return state["t"]
            return ldcast

        for s in range(NSEQ):
            R00 = contextlib.ExitStack()
            rlnA = R00.enter_context(nc.sbuf_tensor(f"ArlnA{s}", [128, NT], F32, side="right"))
            biaA = R00.enter_context(nc.sbuf_tensor(f"AbiaA{s}", [128, NT], F32, side="right"))
            R0 = contextlib.ExitStack()
            kvr = [R0.enter_context(nc.sbuf_tensor(f"Akvr{s}_{c}", [128, TS], BF16, side="right"))
                   for c in range(2)]
            pmid = contextlib.ExitStack()

            R1 = contextlib.ExitStack()
            uT = [R1.enter_context(nc.sbuf_tensor(f"AuT{s}_{c}", [128, TS], BF16, side="right"))
                  for c in range(8)]
            R1a = contextlib.ExitStack()
            xtl = [R1a.enter_context(nc.sbuf_tensor(f"Axt{s}_{i}", [128, 1024], F32, side="right"))
                   for i in range(2)]
            uB = [R1a.enter_context(nc.sbuf_tensor(f"AuB{s}_{i}", [128, 1024], BF16, side="right"))
                  for i in range(2)]
            bnsA = R1a.enter_context(nc.sbuf_tensor(f"Abns{s}", [128, 12 * NT], F32, side="right"))
            stwA = R1a.enter_context(nc.sbuf_tensor(f"Astw{s}", [128, 8 * NT], F32, side="right"))
            psA = contextlib.ExitStack()
            ptpB = psA.enter_context(nc.psum_tensor(f"AptpB{s}", [128, 1024], BF16))

            vt = [None, None]
            for k in range(NT):
                bq = k % 2
                tok_l = LP.issue(em, "sync",
                                 lambda E, s=s, k=k, bq=bq: E.dma_start(out=xtl[bq][:], in_=x3[s, k]),
                                 pre_toks=(vt[bq],))

                def f(E, k=k, bq=bq, tok=tok_l):
                    W(E, tok)
                    E.bn_stats(bnsA[:, 12 * k:12 * k + 6], xtl[bq][:, 0:512])
                    E.bn_stats(bnsA[:, 12 * k + 6:12 * k + 12], xtl[bq][:, 512:1024]).then_inc(VV.h, 1)
                em("vector", f)
                vt[bq] = VV.inc()
            tok_stats = vt[(NT - 1) % 2]

            stp = mkvs(12)

            def f(E, tok=tok_stats, stp=stp):
                W(E, tok)
                S = stp(E)
                me = bnsA[:, 1::3]
                m2 = bnsA[:, 2::3]
                R0_, R1_, R2_ = stwA[:, 0:2 * NT], stwA[:, 2 * NT:3 * NT], stwA[:, 3 * NT:4 * NT]
                R3_, R4_, R5_ = stwA[:, 4 * NT:5 * NT], stwA[:, 5 * NT:6 * NT], stwA[:, 6 * NT:7 * NT]
                S(E.tensor_tensor(R0_, me[:, 0::2], me[:, 1::2], OP.add))
                S(E.tensor_tensor(R1_, R0_[:, 0::2], R0_[:, 1::2], OP.add))
                S(E.tensor_tensor(R0_, m2[:, 0::2], m2[:, 1::2], OP.add))
                S(E.tensor_tensor(R2_, R0_[:, 0::2], R0_[:, 1::2], OP.add))
                S(E.tensor_tensor(stwA[:, 4 * NT:8 * NT], me, me, OP.mult))
                S(E.tensor_tensor(R0_, stwA[:, 4 * NT:8 * NT][:, 0::2],
                                  stwA[:, 4 * NT:8 * NT][:, 1::2], OP.add))
                S(E.tensor_tensor(R3_, R0_[:, 0::2], R0_[:, 1::2], OP.add))
                S(E.tensor_scalar(R4_, R1_, 0.25, None, OP.mult))
                S(E.tensor_scalar(R5_, R2_, 1.0 / 1024.0, None, OP.mult))
                S(E.scalar_tensor_tensor(R5_, R3_, 0.25, R5_, OP.mult, OP.add))
                S(E.tensor_tensor(R0_[:, 0:NT], R4_, R4_, OP.mult))
                S(E.scalar_tensor_tensor(R5_, R0_[:, 0:NT], -1.0, R5_, OP.mult, OP.add))
                E.engine_nop().then_inc(VV.h, 1)
            em("vector", f)
            tok_var = VV.inc()
            sstp = mkss(1)

            def f(E, tok=tok_var, sstp=sstp):
                W(E, tok)
                S = sstp(E)
                S(E.activation(stwA[:, 0:NT], stwA[:, 6 * NT:7 * NT], ACT.Ln,
                               bias=seed[:, 8:9], scale=1.0))
                E.activation(rlnA[:], stwA[:, 0:NT], ACT.Exp, bias=0.0,
                             scale=-0.5).then_inc(SS.h, 1)
            em("scalar", f)
            tok_rln = SS.inc()
            stp2 = mkvs(2)

            def f(E, tok=tok_rln, stp=stp2):
                W(E, tok)
                S = stp(E)
                S(E.tensor_tensor(stwA[:, 0:NT], rlnA[:], stwA[:, 5 * NT:6 * NT], OP.mult))
                S(E.tensor_scalar(biaA[:], stwA[:, 0:NT], -1.0, None, OP.mult))
                E.engine_nop().then_inc(VV.h, 1)
            em("vector", f)
            tok_biaA = VV.inc()

            last_tr = None
            for k in range(NT):
                bq = k % 2
                tok_l = LP.issue(em, "sync",
                                 lambda E, s=s, k=k, bq=bq: E.dma_start(out=xtl[bq][:], in_=x3[s, k]),
                                 pre_toks=(vt[bq],))

                def f(E, k=k, bq=bq, tok=tok_l, tok2=tok_biaA):
                    W(E, tok, tok2)
                    E.activation(uB[bq][:], xtl[bq][:], ACT.Identity,
                                 bias=biaA[:, k:k + 1], scale=rlnA[:, k:k + 1]).then_inc(SS.h, 1)
                em("scalar", f)
                tok_u = SS.inc()

                def f(E, bq=bq, tok=tok_u, tokp=last_tr):
                    W(E, tok, tokp)
                    ins = None
                    for c in range(8):
                        ins = E.transpose(ptpB[:, 128 * c:128 * (c + 1)],
                                          uB[bq][:, 128 * c:128 * (c + 1)], identb[:])
                    ins.then_inc(TT.h, 1)
                em("tensor", f)
                tok_t = TT.inc()

                def f(E, k=k, tok=tok_t):
                    W(E, tok)
                    ins = None
                    for c in range(8):
                        ins = E.tensor_scalar(uT[c][:, 128 * k:128 * (k + 1)],
                                              ptpB[:, 128 * c:128 * (c + 1)], 0.0, None, OP.add)
                    ins.then_inc(VV.h, 1)
                em("vector", f)
                last_tr = VV.inc()
                vt[bq] = last_tr
            tok_uT = last_tr
            em.flush()
            psA.close()
            R1a.close()
            fence()

            if upto <= 12:
                bail(tok_uT, s)
                R1.close(); pmid.close(); R0.close(); R00.close()
                fence()
                continue

            psB = contextlib.ExitStack()
            pj = [psB.enter_context(nc.psum_tensor(f"Apj{s}_{i}", [128, 512], F32))
                  for i in range(3)]

            R_qcb = contextlib.ExitStack()
            qcb = [R_qcb.enter_context(nc.sbuf_tensor(f"Aqcb{s}_{c}", [128, TS], BF16, side="right"))
                   for c in range(4)]
            R_qb = contextlib.ExitStack()
            wqb = [R_qb.enter_context(nc.sbuf_tensor(f"Awq{s}_{c}", [128, QR], BF16, side="right"))
                   for c in range(8)]
            wdf1 = R_qb.enter_context(nc.sbuf_tensor(f"Awdf1{s}", [128, 1024], F32, side="right"))
            tok_w = mk_ldcast(wdf1)(wqb, dwq, QR, 1.0)

            tok_qcb = None
            tok_pjb = [None, None, None]
            for m in range(4):
                for nch2 in range(NCH):
                    bk = (m * NCH + nch2) % 3
                    pp = pj[bk]

                    def f(E, m=m, nch2=nch2, pp=pp, tok=tok_uT, tokp=tok_pjb[bk], tokw=tok_w):
                        W(E, tok, tokp, tokw)
                        ins = None
                        for kf in range(8):
                            ins = E.matmul(pp[:, 0:512], wqb[kf][:, 128 * m:128 * (m + 1)],
                                           uT[kf][:, 512 * nch2:512 * (nch2 + 1)],
                                           start=(kf == 0), stop=(kf == 7))
                        ins.then_inc(TT.h, 1)
                    em("tensor", f)
                    tok_mm = TT.inc()

                    def f(E, m=m, nch2=nch2, pp=pp, tok=tok_mm):
                        W(E, tok)
                        E.tensor_scalar(qcb[m][:, 512 * nch2:512 * (nch2 + 1)], pp[:, 0:512],
                                        0.0, None, OP.add).then_inc(VV.h, 1)
                    em("vector", f)
                    tok_qcb = VV.inc()
                    tok_pjb[bk] = tok_qcb
            tok_qcb_done = tok_qcb
            em.flush()
            R_qb.close()
            fence()

            q_aug = [pmid.enter_context(nc.sbuf_tensor(f"Aqa{s}_{h}", [97, TS], BF16, side="left"))
                     for h in range(16)]
            R_qc = contextlib.ExitStack()
            wqdb = [R_qc.enter_context(nc.sbuf_tensor(f"Awqd{s}_{c}", [128, D], BF16, side="right"))
                    for c in range(4)]
            wdf2 = R_qc.enter_context(nc.sbuf_tensor(f"Awdf2{s}", [128, 1024], F32, side="right"))
            bncA = [R_qc.enter_context(nc.sbuf_tensor(f"Abnc{s}_{i}", [128, 512], BF16, side="right"))
                    for i in range(2)]
            tok_w = mk_ldcast(wdf2)(wqdb, dwqdec, D, SC96)

            tok_qc = None
            tok_bnc = [None, None]
            tok_pjc = [None, None, None]
            for m in range(8):
                for nch2 in range(NCH):
                    bk = (m * NCH + nch2) % 3
                    pp = pj[bk]
                    bi = (m * NCH + nch2) % 2

                    def f(E, m=m, nch2=nch2, pp=pp, tok=tok_qcb_done, tokp=tok_pjc[bk], tokw=tok_w):
                        W(E, tok, tokp, tokw)
                        ins = None
                        for kf in range(4):
                            ins = E.matmul(pp[:, 0:512], wqdb[kf][:, 128 * m:128 * (m + 1)],
                                           qcb[kf][:, 512 * nch2:512 * (nch2 + 1)],
                                           start=(kf == 0), stop=(kf == 3))
                        ins.then_inc(TT.h, 1)
                    em("tensor", f)
                    tok_mm = TT.inc()

                    def f(E, m=m, nch2=nch2, pp=pp, bi=bi, tok=tok_mm, tokb=tok_bnc[bi]):
                        W(E, tok, tokb)
                        E.tensor_scalar(q_aug[2 * m][0:64, 512 * nch2:512 * (nch2 + 1)],
                                        pp[0:64, 0:512], 0.0, None, OP.add)
                        E.tensor_scalar(bncA[bi][64:128, 0:512], pp[64:128, 0:512],
                                        0.0, None, OP.add).then_inc(VV.h, 1)
                    em("vector", f)
                    tok_qc = VV.inc()
                    tok_pjc[bk] = tok_qc
                    tok_bnc[bi] = OP_.issue(
                        em, "gpsimd",
                        lambda E, m=m, nch2=nch2, bi=bi: E.dma_start(
                            out=q_aug[2 * m + 1][0:64, 512 * nch2:512 * (nch2 + 1)],
                            in_=bncA[bi][64:128, 0:512]),
                        pre_toks=(tok_qc,))
            tok_qcont = tuple(tok_bnc) + (tok_qc,)
            em.flush()
            R_qc.close()
            fence()

            R_qr = contextlib.ExitStack()
            wqrb = [R_qr.enter_context(nc.sbuf_tensor(f"Awqr{s}_{c}", [128, 512], BF16, side="right"))
                    for c in range(4)]
            wdf3 = R_qr.enter_context(nc.sbuf_tensor(f"Awdf3{s}", [128, 512], F32, side="right"))
            cosc = [R_qr.enter_context(nc.sbuf_tensor(f"Acos{s}_{i}", [128, 512], F32, side="right"))
                    for i in range(2)]
            ssinc = [R_qr.enter_context(nc.sbuf_tensor(f"Assin{s}_{i}", [128, 512], F32, side="right"))
                     for i in range(2)]
            krf = R_qr.enter_context(nc.sbuf_tensor(f"Akrf{s}", [128, 512], F32, side="right"))
            krs = R_qr.enter_context(nc.sbuf_tensor(f"Akrs{s}", [128, 512], F32, side="right"))
            kr1 = R_qr.enter_context(nc.sbuf_tensor(f"Akr1{s}", [128, 512], F32, side="right"))
            bnrA = [R_qr.enter_context(nc.sbuf_tensor(f"Abnr{s}_{i}", [128, 512], BF16, side="right"))
                    for i in range(2)]

            def emit_rope(srcs, n_k, wtile, dst_aug, tok_src, init_free, cosc=cosc,
                          ssinc=ssinc, krf=krf, krs=krs, kr1=kr1, bnrA=bnrA):
                tok_last = init_free
                tok_cs = [None, None]
                tok_bnr = [[None] * 4, [None] * 4]
                for nch2 in range(NCH):
                    cb = nch2 % 2
                    tok_c1 = LP.issue(em, "sync",
                                      lambda E, nch2=nch2, cb=cb: E.dma_start(
                                          out=cosc[cb][:], in_=dcos[:, 512 * nch2:512 * (nch2 + 1)]),
                                      pre_toks=(tok_cs[cb],))
                    tok_c2 = LP.issue(em, "sync",
                                      lambda E, nch2=nch2, cb=cb: E.dma_start(
                                          out=ssinc[cb][:], in_=dssin[:, 512 * nch2:512 * (nch2 + 1)]),
                                      pre_toks=(tok_cs[cb],))
                    for m in range(4):
                        pp = pj[m % 2]

                        def f(E, m=m, nch2=nch2, pp=pp, tok=tok_src, tokp=tok_last):
                            W(E, tok, tokp)
                            ins = None
                            for kf in range(n_k):
                                ins = E.matmul(pp[:, 0:512], wtile[kf][:, 128 * m:128 * (m + 1)],
                                               srcs[kf][:, 512 * nch2:512 * (nch2 + 1)],
                                               start=(kf == 0), stop=(kf == n_k - 1))
                            ins.then_inc(TT.h, 1)
                        em("tensor", f)
                        tok_mm = TT.inc()
                        stpr = mkvs(4)

                        def f(E, m=m, pp=pp, cb=cb, tok=tok_mm, stpr=stpr,
                              tokc1=tok_c1, tokc2=tok_c2, tokb=tuple(tok_bnr[m % 2])):
                            W(E, tok, tokc1, tokc2, *tokb)
                            S = stpr(E)
                            S(E.tensor_scalar(krf[:], pp[:, 0:512], 0.0, None, OP.add))
                            S(E.stream_shuffle(krs[:], krf[:], [(i + 16) % 32 for i in range(32)]))
                            S(E.tensor_tensor(kr1[:], krf[:], cosc[cb][:], OP.mult))
                            S(E.tensor_tensor(krs[:], krs[:], ssinc[cb][:], OP.mult))
                            E.tensor_tensor(bnrA[m % 2][:], kr1[:], krs[:], OP.add).then_inc(VV.h, 1)
                        em("vector", f)
                        tok_last = VV.inc()
                        for g in range(4):
                            tok_bnr[m % 2][g] = OP_.issue(
                                em, "gpsimd",
                                lambda E, m=m, g=g, nch2=nch2: E.dma_start(
                                    out=dst_aug[4 * m + g][64:96, 512 * nch2:512 * (nch2 + 1)],
                                    in_=bnrA[m % 2][32 * g:32 * (g + 1), 0:512]),
                                pre_toks=(tok_last,))
                    tok_cs[cb] = tok_last
                toks = [tok_last]
                for a in tok_bnr:
                    toks.extend(a)
                return tuple(toks)

            tok_w = mk_ldcast(wdf3)(wqrb, dwqrope, 512, SC96)
            toks_qrope = emit_rope(qcb, 4, wqrb, q_aug, tok_qcb_done, tok_w)
            em.flush()
            R_qr.close()
            R_qcb.close()
            fence()

            if upto <= 16:
                bail(tok_qcont[-1], s)
                R1.close(); pmid.close(); R0.close(); R00.close(); psB.close()
                fence()
                continue

            R_kv = contextlib.ExitStack()
            wkvb = [R_kv.enter_context(nc.sbuf_tensor(f"Awkv{s}_{c}", [128, KVR], BF16, side="right"))
                    for c in range(8)]
            wdf4 = R_kv.enter_context(nc.sbuf_tensor(f"Awdf4{s}", [128, 1024], F32, side="right"))
            sqs = R_kv.enter_context(nc.sbuf_tensor(f"Asqs{s}", [128, 512], F32, side="right"))
            rtok = R_kv.enter_context(nc.sbuf_tensor(f"Artok{s}", [128, 8], F32, side="right"))
            kvf = [R_kv.enter_context(nc.sbuf_tensor(f"Akvf{s}_{i}", [128, 512], F32, side="right"))
                   for i in range(2)]
            rrow = R_kv.enter_context(nc.sbuf_tensor(f"Arrow{s}", [128, 512], F32, side="right"))
            tok_w = mk_ldcast(wdf4)(wkvb, dwkv, KVR, 1.0)

            tok_kvr = None
            tok_rms_rd = None
            tok_kve_p = None
            for nch2 in range(NCH):
                col = slice(512 * nch2, 512 * (nch2 + 1))

                def f(E, col=col, tok=tok_uT, tokq=tok_kve_p, tokw=tok_w):
                    W(E, tok, tokq, tokw)
                    ins = None
                    for m2 in range(2):
                        for kf in range(8):
                            ins = E.matmul(pj[m2][:, 0:512],
                                           wkvb[kf][:, 128 * m2:128 * (m2 + 1)],
                                           uT[kf][:, col], start=(kf == 0), stop=(kf == 7))
                    ins.then_inc(TT.h, 1)
                em("tensor", f)
                tok_kv = TT.inc()

                stpk = mkvs(4)

                def f(E, tok=tok_kv, stpk=stpk):
                    W(E, tok)
                    S = stpk(E)
                    S(E.tensor_scalar(kvf[0][:], pj[0][:, 0:512], 0.0, None, OP.add))
                    S(E.tensor_scalar(kvf[1][:], pj[1][:, 0:512], 0.0, None, OP.add))
                    S(E.tensor_tensor(sqs[:], kvf[0][:], kvf[0][:], OP.mult))
                    S(E.tensor_tensor(rrow[:], kvf[1][:], kvf[1][:], OP.mult))
                    E.tensor_tensor(sqs[:], sqs[:], rrow[:], OP.add).then_inc(VV.h, 1)
                em("vector", f)
                tok_kve = VV.inc()
                tok_kve_p = tok_kve

                def f(E, tok=tok_kve, tokr=tok_rms_rd):
                    W(E, tok, tokr)
                    ins = None
                    for j in range(4):
                        ins = E.transpose(pj[2][:, 128 * j:128 * (j + 1)],
                                          sqs[:, 128 * j:128 * (j + 1)], identf[:])
                    ins.then_inc(TT.h, 1)
                em("tensor", f)
                tok_sqT = TT.inc()

                stpr = mkvs(1)

                def f(E, tok=tok_sqT, stpr=stpr):
                    W(E, tok)
                    S = stpr(E)
                    ins = None
                    for j in range(4):
                        ins = E.reduce_sum(rtok[:, j:j + 1], pj[2][:, 128 * j:128 * (j + 1)], AX.X)
                    S(ins)
                    E.engine_nop().then_inc(VV.h, 1)
                em("vector", f)
                tok_rms_rd = VV.inc()

                sstp2 = mkss(1)

                def f(E, tok=tok_rms_rd, sstp2=sstp2):
                    W(E, tok)
                    S = sstp2(E)
                    S(E.activation(rtok[:, 4:8], rtok[:, 0:4], ACT.Ln,
                                   bias=seed[:, 9:10], scale=1.0 / 256.0))
                    E.activation(rtok[:, 4:8], rtok[:, 4:8], ACT.Exp,
                                 bias=0.0, scale=-0.5).then_inc(SS.h, 1)
                em("scalar", f)
                tok_rr = SS.inc()

                tok_rk = None
                for j in range(4):
                    def f(E, j=j, tok=tok_rr, tokp=tok_rk):
                        W(E, tok, tokp)
                        E.transpose(pj[2][0:1, 128 * j:128 * (j + 1)], rtok[:, 4 + j:5 + j],
                                    identf[:]).then_inc(TT.h, 1)
                    em("tensor", f)
                    tok_rjT = TT.inc()

                    def f(E, j=j, tok=tok_rjT):
                        W(E, tok)
                        E.tensor_scalar(rrow[0:1, 0:128], pj[2][0:1, 128 * j:128 * (j + 1)],
                                        0.0, None, OP.add).then_inc(VV.h, 1)
                    em("vector", f)
                    tok_rw = VV.inc()

                    def f(E, j=j, tok=tok_rw):
                        W(E, tok)
                        E.matmul(pj[2][:, 128 * j:128 * (j + 1)], onesf[0:1, :],
                                 rrow[0:1, 0:128], start=True, stop=True).then_inc(TT.h, 1)
                    em("tensor", f)
                    tok_rk = TT.inc()
                tok_rb = tok_rk

                def f(E, col=col, tok=tok_rb, tok2=tok_kve):
                    W(E, tok, tok2)
                    E.tensor_tensor(kvr[0][:, col], kvf[0][:], pj[2][:, 0:512], OP.mult)
                    E.tensor_tensor(kvr[1][:, col], kvf[1][:], pj[2][:, 0:512], OP.mult).then_inc(VV.h, 1)
                em("vector", f)
                tok_kvr = VV.inc()
            em.flush()
            R_kv.close()
            R1.close()
            fence()

            if upto <= 13:
                bail(tok_kvr, s)
                pmid.close(); R0.close(); R00.close(); psB.close()
                fence()
                continue

            k_aug = [pmid.enter_context(nc.sbuf_tensor(f"Aka{s}_{h}", [97, TS], BF16, side="left"))
                     for h in range(16)]
            for h in range(16):
                OP_.issue(em, "gpsimd",
                          lambda E, h=h: E.dma_start(out=k_aug[h][96:97, 0:TS], in_=donesrow))

            R_kca = contextlib.ExitStack()
            wkdecb = [R_kca.enter_context(nc.sbuf_tensor(f"Awkd{s}_{c}", [128, D], BF16, side="right"))
                      for c in range(2)]
            wdf5 = R_kca.enter_context(nc.sbuf_tensor(f"Awdf5{s}", [128, 1024], F32, side="right"))
            bnkA = [R_kca.enter_context(nc.sbuf_tensor(f"Abnk{s}_{i}", [128, 512], BF16, side="right"))
                    for i in range(2)]
            tok_wd = None
            for c in range(2):
                tok_l = LP.issue(em, "sync",
                                 lambda E, c=c: E.dma_start(out=wdf5[:], in_=dwkdec[c * 128:(c + 1) * 128, :]),
                                 pre_toks=(tok_wd,))

                def f(E, c=c, tok=tok_l):
                    W(E, tok, tok_g)
                    E.tensor_scalar(wkdecb[c][:], wdf5[:], gkvt[:, c:c + 1], None,
                                    OP.mult).then_inc(VV.h, 1)
                em("vector", f)
                tok_wd = VV.inc()

            tok_kca = None
            tok_bnk = [None, None]
            tok_pjk = [None, None, None]
            for hc in range(8):
                for nch2 in range(NCH):
                    bk = (hc * NCH + nch2) % 3
                    pp = pj[bk]
                    bi = (hc * NCH + nch2) % 2

                    def f(E, hc=hc, nch2=nch2, pp=pp, tok=tok_kvr, tokp=tok_pjk[bk], tokw=tok_wd):
                        W(E, tok, tokp, tokw)
                        ins = None
                        for c in range(2):
                            ins = E.matmul(pp[:, 0:512], wkdecb[c][:, 128 * hc:128 * (hc + 1)],
                                           kvr[c][:, 512 * nch2:512 * (nch2 + 1)],
                                           start=(c == 0), stop=(c == 1))
                        ins.then_inc(TT.h, 1)
                    em("tensor", f)
                    tok_mm = TT.inc()

                    def f(E, hc=hc, nch2=nch2, pp=pp, bi=bi, tok=tok_mm, tokb=tok_bnk[bi]):
                        W(E, tok, tokb)
                        E.tensor_scalar(k_aug[2 * hc][0:64, 512 * nch2:512 * (nch2 + 1)],
                                        pp[0:64, 0:512], 0.0, None, OP.add)
                        E.tensor_scalar(bnkA[bi][64:128, 0:512], pp[64:128, 0:512],
                                        0.0, None, OP.add).then_inc(VV.h, 1)
                    em("vector", f)
                    tok_kca = VV.inc()
                    tok_pjk[bk] = tok_kca
                    tok_bnk[bi] = OP_.issue(
                        em, "gpsimd",
                        lambda E, hc=hc, nch2=nch2, bi=bi: E.dma_start(
                            out=k_aug[2 * hc + 1][0:64, 512 * nch2:512 * (nch2 + 1)],
                            in_=bnkA[bi][64:128, 0:512]),
                        pre_toks=(tok_kca,))
            em.flush()
            R_kca.close()
            fence()

            vkm = [pmid.enter_context(nc.sbuf_tensor(f"Avk{s}_{t}", [128, 1040], BF16, side="left"))
                   for t in range(NT)]
            R_vkm = contextlib.ExitStack()
            wvdecb = [R_vkm.enter_context(nc.sbuf_tensor(f"Awvd{s}_{c}", [128, D], BF16, side="right"))
                      for c in range(2)]
            wdf6 = R_vkm.enter_context(nc.sbuf_tensor(f"Awdf6{s}", [128, 1024], F32, side="right"))
            tok_wd = None
            for c in range(2):
                tok_l = LP.issue(em, "sync",
                                 lambda E, c=c: E.dma_start(out=wdf6[:], in_=dwvdec[c * 128:(c + 1) * 128, :]),
                                 pre_toks=(tok_wd,))

                def f(E, c=c, tok=tok_l):
                    W(E, tok, tok_g)
                    E.tensor_scalar(wvdecb[c][:], wdf6[:], gkvt[:, c:c + 1], None,
                                    OP.mult).then_inc(VV.h, 1)
                em("vector", f)
                tok_wd = VV.inc()

            tok_vkm_l = None
            for tc in range(NT):
                def f(E, tc=tc, tok=tok_kvr, tokp=tok_vkm_l, tokw=tok_wd):
                    W(E, tok, tokp, tokw)
                    ins = None
                    for half in range(2):
                        for c in range(2):
                            ins = E.matmul(pj[half][:, 0:512],
                                           kvr[c][:, 128 * tc:128 * (tc + 1)],
                                           wvdecb[c][:, 512 * half:512 * (half + 1)],
                                           start=(c == 0), stop=(c == 1))
                    ins.then_inc(TT.h, 1)
                em("tensor", f)
                tok_mmv = TT.inc()

                def f(E, tc=tc, tok=tok_mmv):
                    W(E, tok)
                    va = vkm[tc][:, 0:1040].rearrange("p (h c) -> p h c", c=65)
                    E.tensor_scalar(va[:, 0:8, 0:64],
                                    pj[0][:, 0:512].rearrange("p (h c) -> p h c", c=64),
                                    0.0, None, OP.add)
                    E.tensor_scalar(va[:, 8:16, 0:64],
                                    pj[1][:, 0:512].rearrange("p (h c) -> p h c", c=64),
                                    0.0, None, OP.add)
                    E.tensor_scalar(va[:, 0:16, 64:65],
                                    onesf[:, 0:16].rearrange("p (h c) -> p h c", c=1),
                                    0.0, None, OP.add).then_inc(VV.h, 1)
                em("vector", f)
                tok_vkm_l = VV.inc()
            em.flush()
            R_vkm.close()
            R0.close()
            fence()

            if upto <= 14:
                bail(tok_vkm_l, s)
                pmid.close(); R00.close(); psB.close()
                fence()
                continue

            R_kr = contextlib.ExitStack()
            wkrb = [R_kr.enter_context(nc.sbuf_tensor(f"Awkr{s}_{c}", [128, 512], BF16, side="right"))
                    for c in range(8)]
            wdf7 = R_kr.enter_context(nc.sbuf_tensor(f"Awdf7{s}", [128, 512], F32, side="right"))
            xtl2 = R_kr.enter_context(nc.sbuf_tensor(f"Axt2{s}", [128, 1024], F32, side="right"))
            uB2 = R_kr.enter_context(nc.sbuf_tensor(f"AuB2{s}", [128, 1024], BF16, side="right"))
            uT2 = [R_kr.enter_context(nc.sbuf_tensor(f"AuT2{s}_{c}", [128, 512], BF16, side="right"))
                   for c in range(8)]
            cosc2 = [R_kr.enter_context(nc.sbuf_tensor(f"Acos2{s}_{i}", [128, 512], F32, side="right"))
                     for i in range(2)]
            ssinc2 = [R_kr.enter_context(nc.sbuf_tensor(f"Assin2{s}_{i}", [128, 512], F32, side="right"))
                      for i in range(2)]
            krf2 = R_kr.enter_context(nc.sbuf_tensor(f"Akrf2{s}", [128, 512], F32, side="right"))
            krs2 = R_kr.enter_context(nc.sbuf_tensor(f"Akrs2{s}", [128, 512], F32, side="right"))
            kr12 = R_kr.enter_context(nc.sbuf_tensor(f"Akr12{s}", [128, 512], F32, side="right"))
            bnr2 = [R_kr.enter_context(nc.sbuf_tensor(f"Abnr2{s}_{i}", [128, 512], BF16, side="right"))
                    for i in range(2)]
            psKr = contextlib.ExitStack()
            ptpB2 = psKr.enter_context(nc.psum_tensor(f"AptpB2{s}", [128, 1024], BF16))

            tok_w = mk_ldcast(wdf7)(wkrb, dwkrope, 512, 1.0)
            tok_cs2 = [None, None]
            tok_bnr2 = [[None] * 4, [None] * 4]
            tok_rope = None
            tok_xt = None
            tok_ut2 = None
            for nch2 in range(NCH):
                cb = nch2 % 2
                tok_c1 = LP.issue(em, "sync",
                                  lambda E, nch2=nch2, cb=cb: E.dma_start(
                                      out=cosc2[cb][:], in_=dcos[:, 512 * nch2:512 * (nch2 + 1)]),
                                  pre_toks=(tok_cs2[cb],))
                tok_c2 = LP.issue(em, "sync",
                                  lambda E, nch2=nch2, cb=cb: E.dma_start(
                                      out=ssinc2[cb][:], in_=dssin[:, 512 * nch2:512 * (nch2 + 1)]),
                                  pre_toks=(tok_cs2[cb],))
                tok_ev = None
                for j in range(4):
                    k = 4 * nch2 + j
                    tok_l = LP.issue(em, "sync",
                                     lambda E, s=s, k=k: E.dma_start(out=xtl2[:], in_=x3[s, k]),
                                     pre_toks=(tok_xt,))

                    def f(E, k=k, tok=tok_l, tokp=tok_ut2):
                        W(E, tok, tokp)
                        E.activation(uB2[:], xtl2[:], ACT.Identity,
                                     bias=biaA[:, k:k + 1], scale=rlnA[:, k:k + 1]).then_inc(SS.h, 1)
                    em("scalar", f)
                    tok_xt = SS.inc()

                    def f(E, tok=tok_xt, tokp=tok_ev, tokr=tok_rope):
                        W(E, tok, tokp, tokr)
                        ins = None
                        for c in range(8):
                            ins = E.transpose(ptpB2[:, 128 * c:128 * (c + 1)],
                                              uB2[:, 128 * c:128 * (c + 1)], identb[:])
                        ins.then_inc(TT.h, 1)
                    em("tensor", f)
                    tok_t = TT.inc()

                    def f(E, j=j, tok=tok_t):
                        W(E, tok)
                        ins = None
                        for c in range(8):
                            ins = E.tensor_scalar(uT2[c][:, 128 * j:128 * (j + 1)],
                                                  ptpB2[:, 128 * c:128 * (c + 1)], 0.0, None, OP.add)
                        ins.then_inc(VV.h, 1)
                    em("vector", f)
                    tok_ev = VV.inc()
                    tok_ut2 = tok_ev
                for m in range(4):
                    pp = pj[m % 2]

                    def f(E, m=m, pp=pp, tok=tok_ev, tokw=tok_w, tokr=tok_rope):
                        W(E, tok, tokw, tokr)
                        ins = None
                        for kf in range(8):
                            ins = E.matmul(pp[:, 0:512], wkrb[kf][:, 128 * m:128 * (m + 1)],
                                           uT2[kf][:, 0:512],
                                           start=(kf == 0), stop=(kf == 7))
                        ins.then_inc(TT.h, 1)
                    em("tensor", f)
                    tok_mm = TT.inc()
                    stpr = mkvs(4)

                    def f(E, m=m, pp=pp, cb=cb, tok=tok_mm, stpr=stpr,
                          tokc1=tok_c1, tokc2=tok_c2, tokb=tuple(tok_bnr2[m % 2])):
                        W(E, tok, tokc1, tokc2, *tokb)
                        S = stpr(E)
                        S(E.tensor_scalar(krf2[:], pp[:, 0:512], 0.0, None, OP.add))
                        S(E.stream_shuffle(krs2[:], krf2[:], [(i + 16) % 32 for i in range(32)]))
                        S(E.tensor_tensor(kr12[:], krf2[:], cosc2[cb][:], OP.mult))
                        S(E.tensor_tensor(krs2[:], krs2[:], ssinc2[cb][:], OP.mult))
                        E.tensor_tensor(bnr2[m % 2][:], kr12[:], krs2[:], OP.add).then_inc(VV.h, 1)
                    em("vector", f)
                    tok_rope = VV.inc()
                    for g in range(4):
                        tok_bnr2[m % 2][g] = OP_.issue(
                            em, "gpsimd",
                            lambda E, m=m, g=g, nch2=nch2: E.dma_start(
                                out=k_aug[4 * m + g][64:96, 512 * nch2:512 * (nch2 + 1)],
                                in_=bnr2[m % 2][32 * g:32 * (g + 1), 0:512]),
                            pre_toks=(tok_rope,))
                tok_cs2[cb] = tok_rope
            em.flush()
            psKr.close()
            R_kr.close()
            fence()

            if upto <= 15:
                bail(tok_rope, s)
                pmid.close(); R00.close(); psB.close()
                fence()
                continue

            R_bnd = contextlib.ExitStack()
            sqb = R_bnd.enter_context(nc.sbuf_tensor(f"Asqb{s}", [128, 512], BF16, side="right"))
            qsv = R_bnd.enter_context(nc.sbuf_tensor(f"Aqsv{s}", [128, 512], F32, side="right"))
            qsv2 = R_bnd.enter_context(nc.sbuf_tensor(f"Aqsv2{s}", [128, 512], F32, side="right"))
            msq = R_bnd.enter_context(nc.sbuf_tensor(f"Amsq{s}", [128, 8], F32, side="right"))

            tok_sq = None
            tok_pjf = [None, None]
            for h in range(nheads):
                tok_ms = None
                for c in range(NCH):
                    def f(E, h=h, c=c, tok=tok_sq):
                        W(E, tok)
                        E.tensor_tensor(sqb[0:96, :], k_aug[h][0:96, 512 * c:512 * (c + 1)],
                                        k_aug[h][0:96, 512 * c:512 * (c + 1)],
                                        OP.mult).then_inc(VV.h, 1)
                    em("vector", f)
                    tok_sqv = VV.inc()

                    def f(E, c=c, tok=tok_sqv, tokf=tok_pjf[c % 2]):
                        W(E, tok, tokf)
                        E.matmul(pj[c % 2][:, 0:512], onesb[0:96, :], sqb[0:96, :],
                                 start=True, stop=True).then_inc(TT.h, 1)
                    em("tensor", f)
                    tok_mm = TT.inc()
                    tok_sq = tok_mm

                    def f(E, c=c, tok=tok_mm, tokp=tok_ms):
                        W(E, tok, tokp)
                        E.reduce_max(msq[:, c:c + 1], pj[c % 2][:, 0:512],
                                     AX.X).then_inc(VV.h, 1)
                    em("vector", f)
                    tok_ms = VV.inc()
                    tok_pjf[c % 2] = tok_ms

                def f(E, tok=tok_ms):
                    W(E, tok)
                    E.tensor_reduce(msq[:, 4:5], msq[:, 0:NCH], AX.X, OP.max)
                    E.engine_nop().then_inc(VV.h, 1)
                em("vector", f)
                tok_mk = VV.inc()

                tok_sqrt = None
                for c in range(NCH):
                    def f(E, h=h, c=c, tok=tok_sq, tokm=tok_mk):
                        W(E, tok, tokm)
                        E.tensor_tensor(sqb[0:96, :], q_aug[h][0:96, 512 * c:512 * (c + 1)],
                                        q_aug[h][0:96, 512 * c:512 * (c + 1)],
                                        OP.mult).then_inc(VV.h, 1)
                    em("vector", f)
                    tok_sqv = VV.inc()

                    def f(E, c=c, tok=tok_sqv, tokf=tok_pjf[c % 2]):
                        W(E, tok, tokf)
                        E.matmul(pj[c % 2][:, 0:512], onesb[0:96, :], sqb[0:96, :],
                                 start=True, stop=True).then_inc(TT.h, 1)
                    em("tensor", f)
                    tok_mm = TT.inc()
                    tok_sq = tok_mm

                    def f(E, c=c, tok=tok_mm, tokp=tok_sqrt):
                        W(E, tok, tokp)
                        E.tensor_scalar(qsv[:], pj[c % 2][:, 0:512], msq[:, 4:5],
                                        None, OP.mult).then_inc(VV.h, 1)
                    em("vector", f)
                    tok_qs = VV.inc()
                    tok_pjf[c % 2] = tok_qs

                    sstb = mkss(1)

                    def f(E, tok=tok_qs, sstb=sstb):
                        W(E, tok)
                        S = sstb(E)
                        S(E.activation(qsv2[:], qsv[:], ACT.Ln, bias=0.0, scale=1.0))
                        E.activation(qsv[:], qsv2[:], ACT.Exp, bias=0.0,
                                     scale=0.5).then_inc(SS.h, 1)
                    em("scalar", f)
                    tok_sqrt = SS.inc()

                    def f(E, h=h, c=c, tok=tok_sqrt):
                        W(E, tok)
                        E.tensor_scalar(q_aug[h][96:97, 512 * c:512 * (c + 1)],
                                        qsv[0:1, 0:512], -1.0, None,
                                        OP.mult).then_inc(VV.h, 1)
                    em("vector", f)
                    tok_sq = VV.inc()
            em.flush()
            R_bnd.close()
            psB.close()
            fence()

            psC = contextlib.ExitStack()
            stq = [psC.enter_context(nc.psum_tensor(f"Astq{s}_{i}", [128, 512], F32))
                   for i in range(5)]
            pvp = [psC.enter_context(nc.psum_tensor(f"Apvp{s}_{i}", [128, 512], F32))
                   for i in range(2)]
            bcp = psC.enter_context(nc.psum_tensor(f"Abcp{s}", [128, 512], F32))

            R_ofm = contextlib.ExitStack()
            ofm = [R_ofm.enter_context(nc.sbuf_tensor(f"Aof{s}_{c}", [128, TS], BF16, side="right"))
                   for c in range(8)]
            R_core = contextlib.ExitStack()
            masks = [R_core.enter_context(nc.sbuf_tensor(f"Amsk{s}_{L}", [128, 512], BF16, side="right"))
                     for L in range(4)]
            Ptb = [R_core.enter_context(nc.sbuf_tensor(f"APt{s}_{i}", [128, 512], BF16, side="right"))
                   for i in range(3)]
            rcs = R_core.enter_context(nc.sbuf_tensor(f"Arcs{s}", [128, 512], F32, side="right"))
            rex = R_core.enter_context(nc.sbuf_tensor(f"Arex{s}", [128, 512], F32, side="right"))
            srow = rex[64:65, 0:512]
            masks_tok = [OP_.issue(em, "gpsimd",
                                   lambda E, L=L: E.dma_start(out=masks[L][:], in_=dminm[L]))
                         for L in range(4)]

            iters = [(h, qb) for h in range(nheads) for qb in range(NQB)]
            tok_stq_free = [None] * 5
            tok_Ptb_free = [None] * 3
            tok_pvp_free = [None, None]
            tok_bcp_free = None
            tok_srow_free = None
            tok_core = None

            def emit_F(i, h, qb, pv, tok_pv):
                nonlocal tok_srow_free, tok_bcp_free, tok_core

                def f(E, pv=pv, tok=tok_pv, tokf=tok_srow_free):
                    W(E, tok, tokf)
                    E.tensor_scalar(srow, pv[64:65, 0:512], 0.0,
                                    None, OP.add).then_inc(VV.h, 1)
                em("vector", f)
                tok_sr = VV.inc()

                def f(E, tok=tok_sr, tokf=tok_bcp_free):
                    W(E, tok, tokf)
                    E.matmul(bcp[:, 0:512], onesf[64:65, 0:128], srow,
                             start=True, stop=True).then_inc(TT.h, 1)
                em("tensor", f)
                tok_bc = TT.inc()
                tok_srow_free = tok_bc

                sstf = mkss(1)

                def f(E, tok=tok_bc, sstf=sstf, tokr=tok_core):
                    W(E, tok, tokr)
                    S = sstf(E)
                    S(E.activation(rcs[0:64, 0:512], bcp[0:64, 0:512], ACT.Ln,
                                   bias=0.0, scale=1.0))
                    E.activation(rex[0:64, 0:512], rcs[0:64, 0:512], ACT.Exp,
                                 bias=0.0, scale=-1.0).then_inc(SS.h, 1)
                em("scalar", f)
                tok_rc = SS.inc()
                tok_bcp_free = tok_rc

                if h % 2 == 0:
                    def f(E, h=h, qb=qb, pv=pv, tok=tok_rc):
                        W(E, tok)
                        E.tensor_tensor(ofm[h // 2][0:64, 512 * qb:512 * (qb + 1)],
                                        pv[0:64, 0:512], rex[0:64, 0:512],
                                        OP.mult).then_inc(VV.h, 1)
                    em("vector", f)
                    tok_ev = VV.inc()
                else:
                    KC = 4 * (qb + 1)
                    obl = KC % 3

                    def f(E, obl=obl, pv=pv, tok=tok_rc, tokb=tok_Ptb_free[obl]):
                        W(E, tok, tokb)
                        E.tensor_tensor(Ptb[obl][0:64, 0:512],
                                        pv[0:64, 0:512], rex[0:64, 0:512],
                                        OP.mult).then_inc(VV.h, 1)
                    em("vector", f)
                    tok_ev = VV.inc()
                    tok_Ptb_free[obl] = OP_.issue(
                        em, "gpsimd",
                        lambda E, h=h, qb=qb, obl=obl: E.dma_start(
                            out=ofm[h // 2][64:128, 512 * qb:512 * (qb + 1)],
                            in_=Ptb[obl][0:64, 0:512]),
                        pre_toks=(tok_ev,))
                tok_pvp_free[i % 2] = tok_ev
                tok_core = tok_ev

            pend = None
            for i, (h, qb) in enumerate(iters):
                KC = 4 * (qb + 1)
                pv = pvp[i % 2]
                tok_pv = None
                for kc in range(KC):
                    ssl = kc % 5
                    psl = kc % 3

                    def f(E, h=h, qb=qb, kc=kc, ssl=ssl, tokf=tok_stq_free[ssl]):
                        W(E, tokf)
                        E.matmul(stq[ssl][:, 0:512], k_aug[h][0:97, 128 * kc:128 * (kc + 1)],
                                 q_aug[h][0:97, 512 * qb:512 * (qb + 1)],
                                 start=True, stop=True).then_inc(TT.h, 1)
                    em("tensor", f)
                    tok_st = TT.inc()

                    def f(E, ssl=ssl, psl=psl, tok=tok_st, tokf=tok_Ptb_free[psl]):
                        W(E, tok, tokf)
                        E.activation(Ptb[psl][:], stq[ssl][:, 0:512], ACT.Exp,
                                     bias=0.0, scale=1.0).then_inc(SS.h, 1)
                    em("scalar", f)
                    tok_exp = SS.inc()
                    tok_stq_free[ssl] = tok_exp

                    pv_wait = tok_exp
                    if kc >= 4 * qb:
                        L = kc - 4 * qb

                        def f(E, psl=psl, L=L, tok=tok_exp, tokm=masks_tok[L]):
                            W(E, tok, tokm)
                            E.tensor_tensor(Ptb[psl][:], Ptb[psl][:], masks[L][:],
                                            OP.min).then_inc(VV.h, 1)
                        em("vector", f)
                        pv_wait = VV.inc()

                    def f(E, h=h, kc=kc, psl=psl, KC=KC, pv=pv, tok=pv_wait,
                          tokp=tok_pvp_free[i % 2] if kc == 0 else None):
                        W(E, tok, tokp)
                        E.matmul(pv[0:65, 0:512], vkm[kc][:, 65 * h:65 * h + 65],
                                 Ptb[psl][:, 0:512], start=(kc == 0),
                                 stop=(kc == KC - 1)).then_inc(TT.h, 1)
                    em("tensor", f)
                    tok_pv = TT.inc()
                    tok_Ptb_free[psl] = tok_pv

                if pend is not None:
                    emit_F(*pend)
                pend = (i, h, qb, pv, tok_pv)
            emit_F(*pend)

            em.flush()
            R_core.close()
            pmid.close()
            psC.close()
            fence()

            if upto <= 3:
                bail(tok_core, s)
                R_ofm.close(); R00.close()
                fence()
                continue

            psD = contextlib.ExitStack()
            pj2 = [psD.enter_context(nc.psum_tensor(f"Apj2{s}_{i}", [128, 512], F32))
                   for i in range(3)]
            R_B4 = contextlib.ExitStack()
            wob = [R_B4.enter_context(nc.sbuf_tensor(f"Awo{s}_{c}", [128, D], BF16, side="right"))
                   for c in range(8)]
            xa = [R_B4.enter_context(nc.sbuf_tensor(f"Axa{s}_{i}", [128, 1024], F32, side="right"))
                  for i in range(4)]
            woe = [R_B4.enter_context(nc.sbuf_tensor(f"Awoe{s}_{i}", [128, 512], F32, side="right"))
                   for i in range(2)]
            wdf8 = R_B4.enter_context(nc.sbuf_tensor(f"Awdf8{s}", [128, 1024], F32, side="right"))
            tok_wo_ready = mk_ldcast(wdf8)(wob, dwo, D, 1.0)

            tok_xa_store = [None] * 4
            tok_woe_free = [None, None]
            tok_wo_tr = None
            tok_stt_l = None
            for nch2 in range(NCH):
                toks_xl = []
                for j in range(4):
                    toks_xl.append(LP.issue(
                        em, "sync",
                        lambda E, s=s, nch2=nch2, j=j: E.dma_start(out=xa[j][:],
                                                                   in_=x3[s, 4 * nch2 + j]),
                        pre_toks=(tok_xa_store[j],)))
                toks_xl = tuple(toks_xl)
                for d in range(8):
                    db = d % 2

                    def f(E, nch2=nch2, d=d, db=db, tok=tok_core, tokf=tok_woe_free[db],
                          tokw=tok_wo_ready):
                        W(E, tok, tokf, tokw)
                        ins = None
                        for c in range(8):
                            ins = E.matmul(pj2[db][:, 0:512],
                                           wob[c][:, 128 * d:128 * (d + 1)],
                                           ofm[c][:, 512 * nch2:512 * (nch2 + 1)],
                                           start=(c == 0), stop=(c == 7))
                        ins.then_inc(TT.h, 1)
                    em("tensor", f)
                    tok_mm = TT.inc()

                    def f(E, db=db, tok=tok_mm):
                        W(E, tok)
                        E.tensor_scalar(woe[db][:], pj2[db][:, 0:512],
                                        0.0, None, OP.add).then_inc(VV.h, 1)
                    em("vector", f)
                    tok_ev = VV.inc()
                    tok_woe_free[db] = tok_ev

                    def f(E, db=db, tok=tok_ev, tokp=tok_wo_tr):
                        W(E, tok, tokp)
                        ins = None
                        for j in range(4):
                            ins = E.transpose(pj2[2][:, 128 * j:128 * (j + 1)],
                                              woe[db][:, 128 * j:128 * (j + 1)], identf[:])
                        ins.then_inc(TT.h, 1)
                    em("tensor", f)
                    tok_tr = TT.inc()

                    def f(E, d=d, tok=tok_tr, tokx=toks_xl):
                        W(E, tok, *tokx)
                        ins = None
                        for j in range(4):
                            ins = E.tensor_tensor(xa[j][:, 128 * d:128 * (d + 1)],
                                                  xa[j][:, 128 * d:128 * (d + 1)],
                                                  pj2[2][:, 128 * j:128 * (j + 1)], OP.add)
                        ins.then_inc(VV.h, 1)
                    em("vector", f)
                    tok_stt_l = VV.inc()
                    tok_wo_tr = tok_stt_l

                for j in range(4):
                    tok_xa_store[j] = OP_.issue(
                        em, "gpsimd",
                        lambda E, s=s, nch2=nch2, j=j: E.dma_start(out=y3[s, 4 * nch2 + j],
                                                                   in_=xa[j][:]),
                        pre_toks=(tok_stt_l,))
            em.flush()
            R_B4.close()
            R_ofm.close()
            R00.close()
            psD.close()
            fence()

        ps.close()

    if do_attn:
        emit_ffn(dx, dx1, "A")
        fence()
        emit_attn(dx1, dx2 if do_ffn2 else dout, upto=attn_upto, nheads=attn_heads)
        fence()
        if do_ffn2:
            emit_ffn(dx2, dout, "C")
    else:
        emit_ffn(dx, dout, "A")

    toks = OP_.all_tok()

    def f(E, toks=tuple(toks)):
        for t in toks:
            E.wait_ge(t[0], t[1])
    em("gpsimd", f)
    em.flush()
    es.close()
    return nc



def _host_constants(TS):
    import ml_dtypes
    identb = np.eye(128, dtype=ml_dtypes.bfloat16)
    onesb = np.ones((128, 128), dtype=ml_dtypes.bfloat16)
    identf = np.eye(128, dtype=np.float32)
    onesf = np.ones((128, 128), dtype=np.float32)
    seed = np.zeros((128, 16), dtype=np.float32)
    seed[:, 0:8] = 1.5
    seed[:, 8] = EPS_LN
    seed[:, 9] = EPS_RMS
    j = np.arange(16, dtype=np.float32)
    inv_freq = (1.0 / (10000.0 ** (2.0 * j / 32.0))).astype(np.float32)
    t = np.arange(TS, dtype=np.float32)
    ang = np.outer(t, inv_freq)
    emb = np.concatenate([ang, ang], axis=1)
    cos32 = np.cos(emb).astype(np.float32)
    sin32 = np.sin(emb).astype(np.float32)
    p = np.arange(128)
    d32 = p % 32
    cosfm = np.ascontiguousarray(cos32[:, d32].T)
    sgn = np.where(d32 < 16, -1.0, 1.0).astype(np.float32)
    ssinfm = np.ascontiguousarray((sin32[:, d32] * sgn[None, :]).T)
    onesrow = np.ones((1, TS), dtype=ml_dtypes.bfloat16)
    BIG = np.float32(3.0e38)
    minmask = np.zeros((4, 128, 512), dtype=np.float32)
    kr = np.arange(128)[:, None]
    qc = np.arange(128)[None, :]
    for L in range(4):
        for ti in range(4):
            blk = minmask[L][:, 128 * ti:128 * (ti + 1)]
            if ti > L:
                blk[:] = BIG
            elif ti == L:
                blk[:] = np.where(kr <= qc, BIG, np.float32(0.0))
    minmask = minmask.astype(ml_dtypes.bfloat16)
    return dict(identb=identb, identf=identf, onesf=onesf, onesb=onesb, seed=seed,
                cosfm=cosfm, ssinfm=ssinfm, onesrow=onesrow, minmask=minmask)


def _prep_inputs(L0_attn_W_kv, L0_attn_g_kv, L0_attn_W_kdec, L0_attn_W_vdec,
                 L0_attn_W_q, L0_attn_W_qdec, L0_attn_W_krope, L0_attn_W_qrope,
                 L0_attn_W_o, L0_ffn_w1, L0_ffn_conv_w, L0_ffn_conv_b, L0_ffn_w2, TS):
    f32 = lambda a: np.ascontiguousarray(np.asarray(a, np.float32))
    consts = _host_constants(TS)
    return dict(
        wkv=f32(L0_attn_W_kv),
        gkv=np.ascontiguousarray(np.asarray(L0_attn_g_kv, np.float32).reshape(2, 128).T),
        wkdec=f32(L0_attn_W_kdec), wvdec=f32(L0_attn_W_vdec),
        wq=f32(L0_attn_W_q), wqdec=f32(L0_attn_W_qdec),
        wkrope=f32(L0_attn_W_krope), wqrope=f32(L0_attn_W_qrope),
        wo=f32(L0_attn_W_o),
        w1t=np.ascontiguousarray(np.asarray(L0_ffn_w1, np.float32).T),
        w2t=np.ascontiguousarray(np.asarray(L0_ffn_w2, np.float32).T),
        convw=np.ascontiguousarray(np.asarray(L0_ffn_conv_w, np.float32)
                                   .reshape(32, 128, 3).transpose(1, 0, 2).reshape(128, 96)),
        convb=np.ascontiguousarray(np.asarray(L0_ffn_conv_b, np.float32).reshape(32, 128).T),
        **consts)


def kernel(L0_attn_x, L0_attn_W_kv, L0_attn_g_kv, L0_attn_W_kdec, L0_attn_W_vdec,
           L0_attn_W_q, L0_attn_W_qdec, L0_attn_W_krope, L0_attn_W_qrope,
           L0_attn_W_o, L0_ffn_w1, L0_ffn_conv_w, L0_ffn_conv_b, L0_ffn_w2):
    x = np.ascontiguousarray(np.asarray(L0_attn_x, dtype=np.float32))
    B, TS, _ = x.shape
    nseq = B // NCORES
    nc = build_program(TS=TS, NSEQ=nseq)
    shared = _prep_inputs(L0_attn_W_kv, L0_attn_g_kv, L0_attn_W_kdec, L0_attn_W_vdec,
                          L0_attn_W_q, L0_attn_W_qdec, L0_attn_W_krope, L0_attn_W_qrope,
                          L0_attn_W_o, L0_ffn_w1, L0_ffn_conv_w, L0_ffn_conv_b,
                          L0_ffn_w2, TS)
    in_maps = []
    for c in range(NCORES):
        m = dict(shared)
        m["x"] = np.ascontiguousarray(x[c * nseq:(c + 1) * nseq])
        in_maps.append(m)
    res = run_bass_kernel_spmd(nc, in_maps, core_ids=list(range(NCORES)))
    out = np.concatenate([np.asarray(r["out"]) for r in res.results], axis=0)
    return out.astype(np.float32)
